# version 07
# baseline (speedup 1.0000x reference)
import numpy as np
import ml_dtypes
from contextlib import ExitStack

import concourse.bass as bass
import concourse.mybir as mybir
from concourse.bass_utils import run_bass_kernel_spmd

F32 = mybir.dt.float32
BF16 = mybir.dt.bfloat16
I32 = mybir.dt.int32
ALU = mybir.AluOpType
AF = mybir.ActivationFunctionType
AX = mybir.AxisListType

D = 1024
T = 4096
INNER = 4096
NPAIR = 8
N_OWN = 2048
NT_OWN = 16
SS = 4
LN_EPS = 1e-5
Q_EPS = 1e-5
RNDC = 1.5 * 2.0 ** 23
RNDK = 1.5 * 2.0 ** 52 - 1.5 * 2.0 ** 23

ENGINES = ("sync", "tensor", "vector", "scalar", "gpsimd")
DEBUG = False


def own_chunks(half):
    return [0, 1, 6, 7] if half == 0 else [2, 3, 4, 5]


def kv_tiles(half):
    return 32 if half == 0 else 24


class Prog:
    def __init__(self):
        self.items = {e: [] for e in ENGINES}
        self.cnt = {}

    def ev(self, sem):
        return (sem, self.cnt.get(sem, 0))

    def _add(self, eng, fn, waits, sem, amt, drain):
        self.cnt[sem] = self.cnt.get(sem, 0) + amt
        evt = (sem, self.cnt[sem])
        self.items[eng].append((fn, tuple(w for w in waits if w is not None), sem, amt, drain))
        return evt

    def dma(self, fn, w=(), sem="ds"):
        return self._add("sync", fn, w, sem, 16, False)

    def dmag(self, fn, w=(), sem="dg"):
        return self._add("gpsimd", fn, w, sem, 16, False)

    def t(self, fn, w=(), drain=False):
        return self._add("tensor", fn, w, "t", 1, drain)

    def v(self, fn, w=(), drain=False):
        return self._add("vector", fn, w, "v", 1, drain)

    def a(self, fn, w=(), drain=False):
        return self._add("scalar", fn, w, "a", 1, drain)

    def g(self, fn, w=(), drain=False):
        return self._add("gpsimd", fn, w, "g", 1, drain)

    def barrier(self, eng):
        if eng == "vector":
            return self.v(lambda e: e.engine_nop())
        if eng == "scalar":
            return self.a(lambda e: e.engine_nop())
        raise ValueError(eng)


class Ctx:
    pass


def alloc_tensors(nc, stack):
    c = Ctx()
    sb = lambda n, s, d: stack.enter_context(nc.sbuf_tensor(n, s, d))
    c.bfA = sb("bfA", [128, 60416], BF16)
    c.f32A = sb("f32A", [128, 19968], F32)
    c.ps = stack.enter_context(nc.psum_tensor("ps", [128, 4096], F32))

    c.o_hnT = 0
    c.o_KmA = 32768
    c.o_KmB = 36864
    c.o_Vtok = 40960
    c.o_QmA = 45056
    c.o_QmB = 47104
    c.o_WTqA = 49152
    c.o_WTqB = 50176
    c.o_WTkA = 51200
    c.o_WTkB = 52224
    c.o_WTv = 53248
    c.o_Xb = 54272
    c.o_Pbuf = 55296
    c.o_zbuf = 56320
    c.o_cnat = 58368
    c.o_zfull = 0
    c.o_WprojT = 16384
    c.o_c1T = 24576
    c.o_icc = 0
    c.o_c2T = 8192
    c.o_cdblk = 49152

    c.o_cosT = 0
    c.o_sinS = 4096
    c.o_xstg = 8192
    c.o_wstg = 10240
    c.o_d2 = 14336
    c.o_Xf = 15360
    c.o_tA = 16384
    c.o_s2sb = 17408
    c.o_gu = 0
    c.o_gates = 16384
    c.o_ups = 18432
    c.o_x1rb = 18944

    c.cols = sb("cols", [128, 64], F32)
    c.r1g = sb("r1g", [128, 16], F32)
    c.r1u = sb("r1u", [128, 16], F32)
    c.rdc = sb("rdc", [128, 16], F32)
    c.Ccol = sb("Ccol", [128, 1], F32)
    c.nCcol = sb("nCcol", [128, 1], F32)
    c.swg = sb("swg", [128, 1], F32)
    c.swu = sb("swu", [128, 1], F32)
    c.swd = sb("swd", [128, 1], F32)
    c.acc = sb("acc", [128, 80], F32)
    c.consts = sb("constsb", [128, 4], F32)
    c.onesrow = sb("onesrowb", [1, 128], F32)
    c.trimask = sb("trimaskb", [128, 128], BF16)
    c.pswap = sb("pswapb", [128, 128], BF16)
    c.ident = sb("identb", [128, 128], F32)
    c.flag = sb("flagsb", [1, 1], I32)
    c.x1stg = sb("x1stg", [128, 2048], F32)
    c.ssc = sb("ssc", [1, 1], F32)
    return c


def bank(c, i):
    return c.ps[:, i * 512:(i + 1) * 512]


def build_half(nc, c, P, half, io):
    OC = own_chunks(half)
    KT = kv_tiles(half)
    bfA, f32A, ps = c.bfA, c.f32A, c.ps
    x, out = io["x"], io["out"]
    wT = io["wT"]
    scr = io["scr"]

    ones = c.consts[:, 0:1]
    epsv = c.consts[:, 1:2]
    jcol = c.consts[:, 2:3]
    sgn = c.consts[:, 3:4]
    cosT = f32A[:, c.o_cosT:c.o_cosT + 4096]
    sinS = f32A[:, c.o_sinS:c.o_sinS + 4096]

    P.dma(lambda e: e.dma_start(out=c.consts[:, 0:4], in_=io["consts"]), sem="dc")
    P.dma(lambda e: e.dma_start(out=c.onesrow[:], in_=io["onesrow"]), sem="dc")
    P.dma(lambda e: e.dma_start(out=c.trimask[:], in_=io["trimask"]), sem="dc")
    P.dma(lambda e: e.dma_start(out=c.pswap[:], in_=io["pswap"]), sem="dc")
    e_consts = e_or = e_tri = e_psw = e_id = P.dma(
        lambda e: e.dma_start(out=c.ident[:], in_=io["ident"]), sem="dc")

    P.v(lambda e: e.tensor_add(c.cols[:, 0:1], ones, epsv), w=[e_consts], drain=True)
    P.v(lambda e: e.tensor_sub(c.cols[:, 0:1], c.cols[:, 0:1], ones), drain=True)
    P.v(lambda e: e.tensor_scalar(c.cols[:, 0:1], c.cols[:, 0:1], 2.0 ** 30, None, ALU.mult), drain=True)
    P.v(lambda e: e.tensor_scalar(c.Ccol[:], c.cols[:, 0:1], RNDK, RNDC, ALU.mult, ALU.add), drain=True)
    e_C = P.v(lambda e: e.tensor_scalar(c.nCcol[:], c.Ccol[:], -1.0, None, ALU.mult), drain=True)

    LNB = float(np.log(10000.0) / 32.0)
    e_w = P.a(lambda e: e.activation(c.cols[:, 1:2], jcol, AF.Exp, bias=0.0, scale=-LNB),
              w=[e_consts], drain=True)
    e_hp2 = P.v(lambda e: e.tensor_scalar(c.cols[:, 2:3], ones, float(np.pi / 2), None, ALU.mult),
                w=[e_consts], drain=True)
    e_s1 = P.a(lambda e: e.activation(c.cols[:, 3:4], c.cols[:, 1:2], AF.Sin, bias=0.0, scale=1.0),
               w=[e_w], drain=True)
    e_c1 = P.a(lambda e: e.activation(c.cols[:, 4:5], c.cols[:, 1:2], AF.Sin, bias=c.cols[:, 2:3],
                                      scale=-1.0), w=[e_w, e_hp2], drain=True)
    P.v(lambda e: e.tensor_scalar(cosT[:, 0:1], ones, 1.0, None, ALU.mult), w=[e_consts], drain=True)
    P.v(lambda e: e.tensor_scalar(sinS[:, 0:1], ones, 0.0, None, ALU.mult), drain=True)
    P.v(lambda e: e.tensor_copy(c.cols[:, 5:6], c.cols[:, 4:5]), w=[e_c1], drain=True)
    P.v(lambda e: e.tensor_copy(c.cols[:, 6:7], c.cols[:, 3:4]), w=[e_s1], drain=True)
    rc, rs, nrs = c.cols[:, 5:6], c.cols[:, 6:7], c.cols[:, 7:8]
    n = 1
    while n < 4096:
        P.v(lambda e: e.tensor_scalar(nrs, rs, -1.0, None, ALU.mult), drain=True)
        def wide(e, n=n):
            tmp = f32A[:, c.o_d2:c.o_d2 + n] if n <= 1024 else f32A[:, c.o_wstg:c.o_wstg + n]
            e.tensor_scalar(tmp, cosT[:, 0:n], rc, None, ALU.mult)
            e.drain()
            e.scalar_tensor_tensor(cosT[:, n:2 * n], sinS[:, 0:n], nrs, tmp, ALU.mult, ALU.add)
            e.drain()
            e.tensor_scalar(tmp, sinS[:, 0:n], rc, None, ALU.mult)
            e.drain()
            return e.scalar_tensor_tensor(sinS[:, n:2 * n], cosT[:, 0:n], rs, tmp, ALU.mult, ALU.add)
        P.v(wide, drain=True)
        if n < 2048:
            P.v(lambda e: e.tensor_mul(c.cols[:, 8:9], rc, rc), drain=True)
            P.v(lambda e: e.tensor_mul(c.cols[:, 9:10], rs, rs), drain=True)
            P.v(lambda e: e.tensor_mul(c.cols[:, 10:11], rc, rs), drain=True)
            P.v(lambda e: e.tensor_sub(rc, c.cols[:, 8:9], c.cols[:, 9:10]), drain=True)
            P.v(lambda e: e.tensor_scalar(rs, c.cols[:, 10:11], 2.0, None, ALU.mult), drain=True)
            P.v(lambda e: e.tensor_mul(c.cols[:, 8:9], rc, rc), drain=True)
            P.v(lambda e: e.tensor_mul(c.cols[:, 9:10], rs, rs), drain=True)
            P.v(lambda e: e.tensor_add(c.cols[:, 8:9], c.cols[:, 8:9], c.cols[:, 9:10]), drain=True)
            P.v(lambda e: e.tensor_scalar(c.cols[:, 8:9], c.cols[:, 8:9], -0.5, 1.5,
                                          ALU.mult, ALU.add), drain=True)
            P.v(lambda e: e.tensor_mul(rc, rc, c.cols[:, 8:9]), drain=True)
            P.v(lambda e: e.tensor_mul(rs, rs, c.cols[:, 8:9]), drain=True)
        n *= 2
    e_tab = P.v(lambda e: e.tensor_scalar(sinS[:], sinS[:], sgn, None, ALU.mult), drain=True)

    ln_done = []
    for gt in range(KT):
        par = gt % 2
        xst = f32A[:, c.o_xstg + par * 1024: c.o_xstg + (par + 1) * 1024]
        d2 = f32A[:, c.o_d2:c.o_d2 + 1024]
        w_x = [ln_done[gt - 2]] if gt >= 2 else []
        e_x = P.dma(lambda e, gt=gt, xst=xst: e.dma_start(out=xst, in_=x[gt * 128:(gt + 1) * 128, :]),
                    w=w_x, sem="dx%d" % (gt % 2))
        w_cols = [ln_done[gt - 1]] if gt >= 1 else [e_C]
        P.v(lambda e, xst=xst: e.reduce_sum(c.cols[:, 16:17], xst, AX.X), w=[e_x] + w_cols, drain=True)
        e_nm = P.v(lambda e: e.tensor_scalar(c.cols[:, 17:18], c.cols[:, 16:17], -1.0 / D, None, ALU.mult),
                   drain=True)
        e_d2 = P.a(lambda e, xst=xst, d2=d2: e.activation(d2, xst, AF.Square, bias=c.cols[:, 17:18],
                                                          scale=1.0), w=[e_nm], drain=True)
        P.v(lambda e, d2=d2: e.reduce_sum(c.cols[:, 18:19], d2, AX.X), w=[e_d2], drain=True)
        e_vp = P.v(lambda e: e.tensor_scalar(c.cols[:, 18:19], c.cols[:, 18:19], 1.0 / D, LN_EPS,
                                             ALU.mult, ALU.add), drain=True)
        e_sq = P.a(lambda e: e.activation(c.cols[:, 19:20], c.cols[:, 18:19], AF.Sqrt, bias=0.0,
                                          scale=1.0), w=[e_vp], drain=True)
        P.v(lambda e: e.reciprocal(c.cols[:, 20:21], c.cols[:, 19:20]), w=[e_sq], drain=True)
        for _ in range(2):
            P.v(lambda e: e.tensor_mul(c.cols[:, 21:22], c.cols[:, 20:21], c.cols[:, 20:21]), drain=True)
            P.v(lambda e: e.tensor_mul(c.cols[:, 21:22], c.cols[:, 21:22], c.cols[:, 18:19]), drain=True)
            P.v(lambda e: e.tensor_scalar(c.cols[:, 21:22], c.cols[:, 21:22], -0.5, 1.5, ALU.mult,
                                          ALU.add), drain=True)
            P.v(lambda e: e.tensor_mul(c.cols[:, 20:21], c.cols[:, 20:21], c.cols[:, 21:22]), drain=True)
        e_nmr = P.v(lambda e: e.tensor_mul(c.cols[:, 22:23], c.cols[:, 17:18], c.cols[:, 20:21]),
                    drain=True)
        e_hn = P.a(lambda e, xst=xst, d2=d2: e.activation(d2, xst, AF.Identity, bias=c.cols[:, 22:23],
                                                          scale=c.cols[:, 20:21]), w=[e_nmr], drain=True)
        evs = []
        for fc in range(8):
            slot = fc % 2
            w_tr = [e_hn, e_id]
            if fc >= 2:
                w_tr.append(evs[fc - 2])
            elif gt >= 1:
                w_tr.append(ln_done[gt - 1])
            e_tr = P.t(lambda e, fc=fc, slot=slot, d2=d2: e.transpose(
                bank(c, slot)[:, 0:128], d2[:, fc * 128:(fc + 1) * 128], c.ident[:]), w=w_tr)
            e_ev = P.a(lambda e, fc=fc, slot=slot, gt=gt: e.copy(
                bfA[:, c.o_hnT + fc * 4096 + gt * 128: c.o_hnT + fc * 4096 + gt * 128 + 128],
                bank(c, slot)[:, 0:128]), w=[e_tr])
            evs.append(e_ev)
        ln_done.append(evs[-1])
    e_hnT = ln_done[-1]

    if DEBUG:
        P.dma(lambda e: e.dma_start(out=io["dbg_tab"], in_=f32A[:, 0:8192]), w=[e_tab], sem="dbg")
        P.dma(lambda e: e.dma_start(out=io["dbg_hnT"], in_=bfA[:, 0:32768]), w=[e_hnT], sem="dbg")

    slot0 = {"wgT": 0, "wuT": 16, "wdT": 48}
    nchunks = {"wgT": 8, "wuT": 8, "wdT": 32}
    ccols = {"wgT": INNER, "wuT": INNER, "wdT": D}
    prev_red = e_tab
    for wname in ("wgT", "wuT", "wdT"):
        for ck in range(nchunks[wname]):
            wst = f32A[:, c.o_wstg: c.o_wstg + ccols[wname]]
            e_l = P.dma(lambda e, wname=wname, ck=ck, wst=wst: e.dma_start(
                out=wst, in_=wT[wname][ck * 128:(ck + 1) * 128, :]), w=[prev_red], sem="dw")
            prev_red = P.v(lambda e, wst=wst, s=slot0[wname] + ck: e.reduce_sum(
                c.acc[:, s:s + 1], wst, AX.X, apply_absolute_value=True), w=[e_l])
    e_p1 = P.barrier("vector")

    sw_events = {}

    def emit_sw_totals():
        for wname, swcol, nelem in (("wgT", c.swg, INNER * D), ("wuT", c.swu, INNER * D),
                                    ("wdT", c.swd, D * INNER)):
            s0, nck = slot0[wname], nchunks[wname]
            e_r = P.v(lambda e, s0=s0, nck=nck: e.reduce_sum(c.cols[:, 24:25],
                                                             c.acc[:, s0:s0 + nck], AX.X),
                      w=[e_p1], drain=True)
            e_m1 = P.t(lambda e: e.matmul(ps[0:1, 1024:1025], c.cols[:, 24:25], ones,
                                          start=True, stop=True), w=[e_r, e_consts, P.ev("v")])
            e_e1 = P.a(lambda e: e.copy(c.ssc[:], ps[0:1, 1024:1025]), w=[e_m1], drain=True)
            e_m2 = P.t(lambda e: e.matmul(ps[:, 1025:1026], c.onesrow[:], c.ssc[:],
                                          start=True, stop=True), w=[e_e1, e_or])
            e_e2 = P.a(lambda e: e.copy(c.cols[:, 25:26], ps[:, 1025:1026]), w=[e_m2], drain=True)
            P.v(lambda e, nelem=nelem: e.tensor_scalar(c.cols[:, 26:27], c.cols[:, 25:26],
                                                       1.0 / nelem, None, ALU.mult), w=[e_e2], drain=True)
            P.v(lambda e: e.tensor_scalar_max(c.cols[:, 26:27], c.cols[:, 26:27], Q_EPS), drain=True)
            sw_events[wname] = P.v(lambda e, swcol=swcol: e.reciprocal(swcol[:], c.cols[:, 26:27]),
                                   drain=True)

    p2_ev = {"cg": [None] * 8, "cu": [None] * 8, "cd": [None] * 32}
    _p2_state = {"prevq": e_p1, "prevst": []}

    def emit_p2_chunk(wname, ck):
        swcol = {"wgT": c.swg, "wuT": c.swu, "wdT": c.swd}[wname]
        scrkey = {"wgT": "cg", "wuT": "cu", "wdT": "cd"}[wname]
        ncol = ccols[wname]
        wst = f32A[:, c.o_wstg: c.o_wstg + ncol]
        w_l = [_p2_state["prevq"]]
        e_l = P.dma(lambda e, wname=wname, ck=ck, wst=wst: e.dma_start(
            out=wst, in_=wT[wname][ck * 128:(ck + 1) * 128, :]), w=w_l, sem="dw")
        eq1 = P.v(lambda e, wst=wst, swcol=swcol: e.tensor_scalar(wst, wst, swcol[:], None, ALU.mult),
                  w=[e_l, sw_events[wname], e_C])
        eq2 = P.v(lambda e, wst=wst: e.tensor_scalar(wst, wst, c.Ccol[:], None, ALU.add), w=[eq1])
        eq3 = P.v(lambda e, wst=wst: e.tensor_scalar(wst, wst, c.nCcol[:], None, ALU.add), w=[eq2])
        e_last = None
        e_q = None
        for hh in range(0, ncol, 2048):
            hw = min(2048, ncol - hh)
            cn = bfA[:, c.o_cnat:c.o_cnat + hw]
            w_cn = [_p2_state["prevst"][-1]] if _p2_state["prevst"] else []
            e_q = P.v(lambda e, wst=wst, cn=cn, hh=hh, hw=hw: e.tensor_scalar(
                cn, wst[:, hh:hh + hw], -1.0, 1.0, ALU.max, ALU.min), w=w_cn + [eq3])
            e_st = P.dma(lambda e, cn=cn, scrkey=scrkey, ck=ck, hh=hh, hw=hw: e.dma_start(
                out=scr[scrkey][ck][:, hh:hh + hw], in_=cn), w=[e_q], sem="dq")
            _p2_state["prevst"].append(e_st)
            e_last = e_st
        _p2_state["prevq"] = e_q
        p2_ev[scrkey][ck] = e_last

    p2_list = [("wgT", i) for i in range(8)] + [("wuT", i) for i in range(8)] + \
              [("wdT", i) for i in range(32)]

    rst = {}
    w_dbg_guard = []
    zspill = []
    e_prev_pair = None
    e_wt_free = None
    for hp in range(NPAIR):
        wl = []
        w_guard = [e_wt_free] if e_wt_free is not None else []
        for (wa, wb, offA, offB) in (("wqT", "wq2T", c.o_WTqA, c.o_WTqB),
                                     ("wkT", "wk2T", c.o_WTkA, c.o_WTkB)):
            for sub, off in ((0, offA), (1, offB)):
                for wn, halfcol in ((wa, 0), (wb, 64)):
                    def f(e, wn=wn, sub=sub, off=off, halfcol=halfcol, hp=hp):
                        src = wT[wn].rearrange("(fc p) o -> p fc o", p=128)[
                            :, :, hp * 128 + sub * 64: hp * 128 + sub * 64 + 64]
                        dst = bfA[:, off:off + 1024].rearrange("p (fc z) -> p fc z", z=128)[
                            :, :, halfcol:halfcol + 64]
                        return e.dma_start(out=dst, in_=src)
                    wl.append(P.dmag(f, w=w_guard, sem="dwt"))
        def fv(e, hp=hp):
            src = wT["wvT"].rearrange("(fc p) o -> p fc o", p=128)[:, :, hp * 128:(hp + 1) * 128]
            dst = bfA[:, c.o_WTv:c.o_WTv + 1024].rearrange("p (fc z) -> p fc z", z=128)
            return e.dma_start(out=dst, in_=src)
        wl.append(P.dmag(fv, w=w_guard, sem="dwt"))
        P.v(lambda e: e.tensor_scalar(bfA[:, c.o_WTqA:c.o_WTqA + 1024],
                                      bfA[:, c.o_WTqA:c.o_WTqA + 1024], 1.0 / 64, None, ALU.mult),
            w=wl)
        e_wready = P.v(lambda e: e.tensor_scalar(bfA[:, c.o_WTqB:c.o_WTqB + 1024],
                                                 bfA[:, c.o_WTqB:c.o_WTqB + 1024], 1.0 / 64, None,
                                                 ALU.mult), w=wl)

        w_bufs = [e_prev_pair] if e_prev_pair is not None else []
        w_bufs += w_dbg_guard
        if e_prev_pair is None and hp > 0:
            w_bufs.append(zspill[-1])

        vp_last = []
        for ch in range(KT // 4):
            par = ch % 2
            pslot = 5 + par
            e_mm = None
            for fc in range(8):
                w_mm = []
                if fc == 0:
                    w_mm = [e_hnT] + wl + [rst.get("ps%d" % par)]
                e_mm = P.t(lambda e, fc=fc, ch=ch, pslot=pslot: e.matmul(
                    bank(c, pslot), bfA[:, c.o_WTv + fc * 128:c.o_WTv + fc * 128 + 128],
                    bfA[:, c.o_hnT + fc * 4096 + ch * 512: c.o_hnT + fc * 4096 + ch * 512 + 512],
                    start=(fc == 0), stop=(fc == 7)), w=w_mm)
            e_ev = P.a(lambda e, pslot=pslot, ch=ch, par=par: e.copy(
                f32A[:, c.o_Xf + par * 512: c.o_Xf + par * 512 + 512], bank(c, pslot)),
                w=[e_mm, rst.get("xf%d" % par)])
            rst["ps%d" % par] = e_ev
            tr_ev = []
            e_tr = None
            for j in range(4):
                tslot = j % 2
                w_tr = [e_ev, e_id] + w_bufs
                if j >= 2:
                    w_tr.append(tr_ev[j - 2])
                elif ch >= 1:
                    w_tr.append(vp_last[ch - 1])
                e_tr = P.t(lambda e, j=j, ch=ch, tslot=tslot, par=par: e.transpose(
                    bank(c, tslot)[:, 0:128],
                    f32A[:, c.o_Xf + par * 512 + j * 128: c.o_Xf + par * 512 + j * 128 + 128],
                    c.ident[:]), w=w_tr)
                tr_ev.append(P.a(lambda e, j=j, ch=ch, tslot=tslot: e.copy(
                    bfA[:, c.o_Vtok + (ch * 4 + j) * 128: c.o_Vtok + (ch * 4 + j) * 128 + 128],
                    bank(c, tslot)[:, 0:128]), w=[e_tr]))
            rst["xf%d" % par] = e_tr
            vp_last.append(tr_ev[-1])
        e_vtok = vp_last[-1]

        def emit_rope(setoff, dstoff, colbases):
            evs = []
            for i, (dcb, hcb, tcb) in enumerate(colbases):
                par = i % 2
                pslot = 5 + par
                sslot = 7 if par == 0 else 4
                e_mm = None
                for fc in range(8):
                    w_mm = []
                    if fc == 0:
                        w_mm = [e_wready, e_hnT, rst.get("ps%d" % par)]
                    e_mm = P.t(lambda e, fc=fc, pslot=pslot, setoff=setoff, hcb=hcb: e.matmul(
                        bank(c, pslot), bfA[:, setoff + fc * 128: setoff + fc * 128 + 128],
                        bfA[:, c.o_hnT + fc * 4096 + hcb: c.o_hnT + fc * 4096 + hcb + 512],
                        start=(fc == 0), stop=(fc == 7)), w=w_mm)
                e_f = P.a(lambda e, par=par, pslot=pslot: e.copy(
                    f32A[:, c.o_Xf + par * 512: c.o_Xf + par * 512 + 512], bank(c, pslot)),
                    w=[e_mm, rst.get("xf%d" % par)])
                e_b = P.a(lambda e, par=par, pslot=pslot: e.copy(
                    bfA[:, c.o_Xb + par * 512: c.o_Xb + par * 512 + 512], bank(c, pslot)),
                    w=[rst.get("xb%d" % par)])
                rst["ps%d" % par] = e_b
                e_sw = P.t(lambda e, par=par, sslot=sslot: e.matmul(
                    bank(c, sslot), c.pswap[:],
                    bfA[:, c.o_Xb + par * 512: c.o_Xb + par * 512 + 512],
                    start=True, stop=True), w=[e_b, e_psw, rst.get("sw%d" % par)])
                rst["xb%d" % par] = e_sw
                e_tA = P.v(lambda e, par=par, sslot=sslot, tcb=tcb: e.tensor_tensor(
                    f32A[:, c.o_tA + par * 512: c.o_tA + par * 512 + 512], bank(c, sslot),
                    sinS[:, tcb:tcb + 512], ALU.mult),
                    w=[e_sw, e_tab, rst.get("ta%d" % par)])
                rst["sw%d" % par] = e_tA
                e_tB = P.g(lambda e, par=par, tcb=tcb: e.tensor_tensor(
                    f32A[:, c.o_Xf + par * 512: c.o_Xf + par * 512 + 512],
                    f32A[:, c.o_Xf + par * 512: c.o_Xf + par * 512 + 512],
                    cosT[:, tcb:tcb + 512], ALU.mult), w=[e_f, e_tab])
                e_d = P.v(lambda e, par=par, dcb=dcb, dstoff=dstoff: e.tensor_add(
                    bfA[:, dstoff + dcb: dstoff + dcb + 512],
                    f32A[:, c.o_tA + par * 512: c.o_tA + par * 512 + 512],
                    f32A[:, c.o_Xf + par * 512: c.o_Xf + par * 512 + 512]),
                    w=[e_tA, e_tB] + w_bufs)
                rst["xf%d" % par] = e_d
                rst["ta%d" % par] = e_d
                evs.append((e_mm, e_f, e_b, e_sw, e_tA, e_d))
            return evs

        k_cbs = [(ch * 512, ch * 512, ch * 512) for ch in range(KT // 4)]
        q_cbs = [(oc * 512, OC[oc] * 512, OC[oc] * 512) for oc in range(4)]
        evKA = emit_rope(c.o_WTkA, c.o_KmA, k_cbs)
        evKB = emit_rope(c.o_WTkB, c.o_KmB, k_cbs)
        evQA = emit_rope(c.o_WTqA, c.o_QmA, q_cbs)
        evQB = emit_rope(c.o_WTqB, c.o_QmB, q_cbs)
        e_rope = evQB[-1][5]
        e_wt_free = evQB[-1][0]

        zevs = []
        e_zguard = zspill[-1] if zspill else None
        for hb, (koff, qoff, hrow) in enumerate(((c.o_KmA, c.o_QmA, 0), (c.o_KmB, c.o_QmB, 64))):
            for oc in range(4):
                gstart = OC[oc] * 512
                nkt = OC[oc] * 4 + 4
                eP = {}
                ez = None
                for kt in range(nkt):
                    par = kt % 2
                    sb_slot = 0 if par == 0 else 2
                    s2_slot = 1 if par == 0 else 3
                    c0 = max(0, kt * 128 - gstart)
                    w_mm = [e_rope, e_vtok]
                    if kt >= 2:
                        w_mm.append(eP[kt - 2])
                    e_s = P.t(lambda e, koff=koff, qoff=qoff, kt=kt, oc=oc, c0=c0, sb_slot=sb_slot:
                              e.matmul(bank(c, sb_slot)[:, c0:512],
                                       bfA[0:64, koff + kt * 128: koff + kt * 128 + 128],
                                       bfA[0:64, qoff + oc * 512 + c0: qoff + oc * 512 + 512],
                                       start=True, stop=True, tile_position=(0, 0)), w=w_mm)
                    e_s2 = P.t(lambda e, koff=koff, qoff=qoff, kt=kt, oc=oc, c0=c0, s2_slot=s2_slot:
                               e.matmul(bank(c, s2_slot)[:, c0:512],
                                        bfA[64:128, koff + kt * 128: koff + kt * 128 + 128],
                                        bfA[64:128, qoff + oc * 512 + c0: qoff + oc * 512 + 512],
                                        start=True, stop=True, tile_position=(64, 0)))
                    w_e = [e_s2]
                    if kt >= 2:
                        w_e.append(eP[kt - 2])
                    e_ev2 = P.a(lambda e, par=par, c0=c0, s2_slot=s2_slot: e.copy(
                        f32A[:, c.o_s2sb + par * 512 + c0: c.o_s2sb + par * 512 + 512],
                        bank(c, s2_slot)[:, c0:512]), w=w_e)
                    w_p = [e_ev2, e_s]
                    if kt >= 2:
                        w_p.append(eP[kt - 2])
                    if ez is not None:
                        w_p.append(ez)
                    e_p = P.v(lambda e, par=par, c0=c0, sb_slot=sb_slot: e.tensor_tensor(
                        bfA[:, c.o_Pbuf + par * 512 + c0: c.o_Pbuf + par * 512 + 512],
                        bank(c, sb_slot)[:, c0:512],
                        f32A[:, c.o_s2sb + par * 512 + c0: c.o_s2sb + par * 512 + 512],
                        ALU.mult), w=w_p)
                    if kt * 128 >= gstart:
                        e_p = P.v(lambda e, par=par, c0=c0: e.tensor_tensor(
                            bfA[:, c.o_Pbuf + par * 512 + c0: c.o_Pbuf + par * 512 + c0 + 128],
                            bfA[:, c.o_Pbuf + par * 512 + c0: c.o_Pbuf + par * 512 + c0 + 128],
                            c.trimask[:], ALU.mult), w=[e_tri, e_p])
                    eP[kt] = e_p
                    w_z = [e_p]
                    if kt == 0 and zevs:
                        w_z.append(zevs[-1])
                    ez = P.t(lambda e, kt=kt, hrow=hrow, c0=c0, nkt=nkt: e.matmul(
                        bank(c, 4)[hrow:hrow + 64, c0:512],
                        bfA[:, c.o_Vtok + kt * 128 + hrow: c.o_Vtok + kt * 128 + hrow + 64],
                        bfA[:, c.o_Pbuf + (kt % 2) * 512 + c0: c.o_Pbuf + (kt % 2) * 512 + 512],
                        start=(kt == 0), stop=(kt == nkt - 1),
                        tile_position=(0, hrow)), w=w_z)
                w_zv = [ez]
                if e_zguard is not None:
                    w_zv.append(e_zguard)
                zevs.append(P.a(lambda e, hrow=hrow, oc=oc: e.copy(
                    bfA[hrow:hrow + 64, c.o_zbuf + oc * 512: c.o_zbuf + oc * 512 + 512],
                    bank(c, 4)[hrow:hrow + 64, 0:512]), w=w_zv))
        e_prev_pair = zevs[-1]
        zspill.append(P.dma(lambda e, hp=hp: e.dma_start(out=scr["z"][hp],
                                                         in_=bfA[:, c.o_zbuf:c.o_zbuf + 2048]),
                            w=zevs, sem="dz"))
        if DEBUG and hp == 0:
            e_dbg = P.dma(lambda e: e.dma_start(out=io["dbg_p0"],
                                                in_=bfA[:, c.o_KmA:c.o_KmA + 16384]),
                          w=[e_rope, e_vtok], sem="dbg")
            e_prev_pair = None
            w_dbg_guard.append(e_dbg)

        if hp == 1:
            emit_sw_totals()
        if hp >= 2:
            for _ in range(16):
                if p2_list:
                    emit_p2_chunk(*p2_list.pop(0))
    while p2_list:
        emit_p2_chunk(*p2_list.pop(0))

    e_zf = []
    for hp in range(NPAIR):
        e_zf.append(P.dma(lambda e, hp=hp: e.dma_start(
            out=bfA[:, c.o_zfull + hp * 2048: c.o_zfull + hp * 2048 + 2048], in_=scr["z"][hp]),
            w=[zspill[hp], e_wt_free], sem="dzf"))
    e_wp = []
    for hp in range(NPAIR):
        e_wp.append(P.dmag(lambda e, hp=hp: e.dma_start(
            out=bfA[:, c.o_WprojT + hp * 1024: c.o_WprojT + hp * 1024 + 1024],
            in_=wT["wprojT"][hp * 128:(hp + 1) * 128, :]), w=[e_prev_pair], sem="dwp"))

    c1_evs = []
    x1_spills = []
    prev_x1rd = {}
    prev_yadd = {}
    for tt in range(NT_OWN):
        par = tt % 2
        x1t = c.x1stg[:, par * 1024:(par + 1) * 1024]
        grow = OC[tt // 4] * 512 + (tt % 4) * 128
        w_x = list(prev_x1rd.get(par, ()))
        e_x = P.dma(lambda e, grow=grow, par=par: e.dma_start(
            out=f32A[:, c.o_xstg + par * 1024: c.o_xstg + par * 1024 + 1024],
            in_=x[grow:grow + 128, :]), w=w_x, sem="dy%d" % par)
        ev_oc = []
        for oc2 in range(2):
            yslot = 2 + oc2
            e_mm = None
            for hp in range(NPAIR):
                w_mm = []
                if hp == 0:
                    w_mm = list(e_zf) + list(e_wp)
                    if oc2 in prev_yadd:
                        w_mm.append(prev_yadd[oc2])
                e_mm = P.t(lambda e, hp=hp, tt=tt, oc2=oc2, yslot=yslot: e.matmul(
                    bank(c, yslot),
                    bfA[:, c.o_zfull + hp * 2048 + tt * 128: c.o_zfull + hp * 2048 + tt * 128 + 128],
                    bfA[:, c.o_WprojT + hp * 1024 + oc2 * 512: c.o_WprojT + hp * 1024 + oc2 * 512 + 512],
                    start=(hp == 0), stop=(hp == 7)), w=w_mm)
            e_add = P.v(lambda e, oc2=oc2, yslot=yslot, x1t=x1t, par=par: e.tensor_add(
                x1t[:, oc2 * 512:(oc2 + 1) * 512], bank(c, yslot),
                f32A[:, c.o_xstg + par * 1024 + oc2 * 512: c.o_xstg + par * 1024 + oc2 * 512 + 512]),
                w=[e_mm, e_x] + w_x)
            prev_yadd[oc2] = e_add
            ev_oc.append(e_add)
        e_sp = P.dma(lambda e, tt=tt, x1t=x1t: e.dma_start(out=scr["x1"][tt], in_=x1t), w=ev_oc,
                     sem="dsp%d" % (tt % 2))
        x1_spills.append(e_sp)

        d2 = f32A[:, c.o_d2:c.o_d2 + 1024]
        w_cols = [c1_evs[tt - 1]] if tt >= 1 else [e_C]
        P.v(lambda e, x1t=x1t: e.reduce_sum(c.cols[:, 30:31], x1t, AX.X), w=ev_oc + w_cols, drain=True)
        e_nm = P.v(lambda e: e.tensor_scalar(c.cols[:, 31:32], c.cols[:, 30:31], -1.0 / D, None,
                                             ALU.mult), drain=True)
        e_d2 = P.a(lambda e, x1t=x1t, d2=d2: e.activation(d2, x1t, AF.Square, bias=c.cols[:, 31:32],
                                                          scale=1.0), w=[e_nm], drain=True)
        P.v(lambda e, d2=d2: e.reduce_sum(c.cols[:, 32:33], d2, AX.X), w=[e_d2], drain=True)
        e_vp = P.v(lambda e: e.tensor_scalar(c.cols[:, 32:33], c.cols[:, 32:33], 1.0 / D, LN_EPS,
                                             ALU.mult, ALU.add), drain=True)
        e_sq = P.a(lambda e: e.activation(c.cols[:, 33:34], c.cols[:, 32:33], AF.Sqrt, bias=0.0,
                                          scale=1.0), w=[e_vp], drain=True)
        P.v(lambda e: e.reciprocal(c.cols[:, 34:35], c.cols[:, 33:34]), w=[e_sq], drain=True)
        for _ in range(2):
            P.v(lambda e: e.tensor_mul(c.cols[:, 35:36], c.cols[:, 34:35], c.cols[:, 34:35]), drain=True)
            P.v(lambda e: e.tensor_mul(c.cols[:, 35:36], c.cols[:, 35:36], c.cols[:, 32:33]), drain=True)
            P.v(lambda e: e.tensor_scalar(c.cols[:, 35:36], c.cols[:, 35:36], -0.5, 1.5, ALU.mult,
                                          ALU.add), drain=True)
            P.v(lambda e: e.tensor_mul(c.cols[:, 34:35], c.cols[:, 34:35], c.cols[:, 35:36]), drain=True)
        e_rstd = P.barrier("vector")
        e_ab = P.a(lambda e, x1t=x1t, d2=d2: e.activation(d2, x1t, AF.Abs, bias=c.cols[:, 31:32],
                                                          scale=1.0), w=[e_rstd], drain=True)
        P.v(lambda e, d2=d2: e.reduce_max(c.cols[:, 36:37], d2, AX.X), w=[e_ab], drain=True)
        P.v(lambda e: e.tensor_mul(c.cols[:, 36:37], c.cols[:, 36:37], c.cols[:, 34:35]), drain=True)
        P.v(lambda e: e.tensor_scalar_max(c.cols[:, 36:37], c.cols[:, 36:37], Q_EPS), drain=True)
        P.v(lambda e: e.reciprocal(c.cols[:, 37:38], c.cols[:, 36:37]), drain=True)
        P.v(lambda e: e.tensor_scalar(c.cols[:, 37:38], c.cols[:, 37:38], 127.0, None, ALU.mult),
            drain=True)
        P.v(lambda e: e.tensor_mul(c.cols[:, 38:39], c.cols[:, 37:38], c.cols[:, 34:35]), drain=True)
        e_nq = P.v(lambda e: e.tensor_mul(c.cols[:, 39:40], c.cols[:, 31:32], c.cols[:, 38:39]),
                   drain=True)
        P.v(lambda e: e.tensor_mul(c.cols[:, 40:41], c.cols[:, 37:38], c.swg[:]),
            w=[sw_events["wgT"]], drain=True)
        P.v(lambda e, tt=tt: e.reciprocal(c.r1g[:, tt:tt + 1], c.cols[:, 40:41]), drain=True)
        P.v(lambda e: e.tensor_mul(c.cols[:, 40:41], c.cols[:, 37:38], c.swu[:]),
            w=[sw_events["wuT"]], drain=True)
        P.v(lambda e, tt=tt: e.reciprocal(c.r1u[:, tt:tt + 1], c.cols[:, 40:41]), drain=True)
        e_dq = P.a(lambda e, x1t=x1t, d2=d2: e.activation(d2, x1t, AF.Identity, bias=c.cols[:, 39:40],
                                                          scale=c.cols[:, 38:39]), w=[e_nq], drain=True)
        prev_x1rd[par] = (e_dq, e_sp)
        e_r1a = P.v(lambda e, d2=d2: e.tensor_scalar(d2, d2, c.Ccol[:], None, ALU.add), w=[e_dq])
        e_rnd = P.v(lambda e, d2=d2: e.tensor_scalar(d2, d2, c.nCcol[:], None, ALU.add), w=[e_r1a])
        tr = []
        e_tr = None
        for fc in range(8):
            tslot = fc % 2
            w_tr = [e_rnd, e_id]
            if fc >= 2:
                w_tr.append(tr[fc - 2])
            elif tt >= 1:
                w_tr.append(c1_evs[tt - 1])
            e_tr = P.t(lambda e, fc=fc, tslot=tslot, d2=d2: e.transpose(
                bank(c, tslot)[:, 0:128], d2[:, fc * 128:(fc + 1) * 128], c.ident[:]), w=w_tr)
            tr.append(P.a(lambda e, fc=fc, tslot=tslot, tt=tt: e.copy(
                bfA[:, c.o_c1T + fc * 2048 + tt * 128: c.o_c1T + fc * 2048 + tt * 128 + 128],
                bank(c, tslot)[:, 0:128]), w=[e_tr]))
        c1_evs.append(tr[-1])
    e_c1T = c1_evs[-1]
    if DEBUG:
        P.dma(lambda e: e.dma_start(out=io["dbg_c1T"], in_=bfA[:, c.o_c1T:c.o_c1T + 16384]),
              w=[e_c1T], sem="dbg")


    out_dmas = []
    prev_icc = {}
    prev_dstream = {}
    prev_eo = {}
    e_prev_downevac = None
    for ss in range(NT_OWN // SS):
        tts = list(range(ss * SS, (ss + 1) * SS))
        gu_evs = {tt: [] for tt in tts}
        prev_gu_gate = {}
        prev_up = None
        for icc in range(8):
            par = icc % 2
            iccb = bfA[:, c.o_icc + par * 4096: c.o_icc + par * 4096 + 4096]
            w_l0 = [prev_icc.get(par, e_c1T)]
            lg = []
            for fc in range(8):
                lg.append(P.dma(lambda e, fc=fc, icc=icc, iccb=iccb: e.dma_start(
                    out=iccb[:, fc * 512:(fc + 1) * 512],
                    in_=scr["cg"][fc][:, icc * 512:(icc + 1) * 512]),
                    w=(w_l0 if fc == 0 else []) + [p2_ev["cg"][fc]], sem="di%d" % (icc % 2)))
            ev_gate = {}
            for i, tt in enumerate(tts):
                gslot = i % 2
                e_mm = None
                for fc in range(8):
                    w_mm = []
                    if fc == 0:
                        w_mm = lg + [e_c1T]
                        if e_prev_downevac is not None:
                            w_mm.append(e_prev_downevac)
                        if i >= 2:
                            w_mm.append(ev_gate[tts[i - 2]])
                    e_mm = P.t(lambda e, fc=fc, tt=tt, gslot=gslot, iccb=iccb: e.matmul(
                        bank(c, gslot),
                        bfA[:, c.o_c1T + fc * 2048 + tt * 128: c.o_c1T + fc * 2048 + tt * 128 + 128],
                        iccb[:, fc * 512:(fc + 1) * 512], start=(fc == 0), stop=(fc == 7)), w=w_mm)
                w_g = [e_mm]
                if tt in prev_gu_gate:
                    w_g.append(prev_gu_gate[tt])
                ev_gate[tt] = P.a(lambda e, tt=tt, i=i, gslot=gslot: e.activation(
                    f32A[:, c.o_gates + i * 512: c.o_gates + i * 512 + 512], bank(c, gslot),
                    AF.Silu, bias=0.0, scale=c.r1g[:, tt:tt + 1]), w=w_g)
            lu = []
            for fc in range(8):
                lu.append(P.dma(lambda e, fc=fc, icc=icc, iccb=iccb: e.dma_start(
                    out=iccb[:, fc * 512:(fc + 1) * 512],
                    in_=scr["cu"][fc][:, icc * 512:(icc + 1) * 512]),
                    w=(list(ev_gate.values()) if fc == 0 else []) + [p2_ev["cu"][fc]], sem="di%d" % (icc % 2)))
            for i, tt in enumerate(tts):
                uslot = 2 + i % 2
                e_mm = None
                for fc in range(8):
                    w_mm = []
                    if fc == 0:
                        w_mm = lu
                        if i >= 2:
                            w_mm.append(gu_evs[tts[i - 2]][-1])
                    e_mm = P.t(lambda e, fc=fc, tt=tt, uslot=uslot, iccb=iccb: e.matmul(
                        bank(c, uslot),
                        bfA[:, c.o_c1T + fc * 2048 + tt * 128: c.o_c1T + fc * 2048 + tt * 128 + 128],
                        iccb[:, fc * 512:(fc + 1) * 512], start=(fc == 0), stop=(fc == 7)), w=w_mm)
                w_u = [e_mm]
                if prev_up is not None:
                    w_u.append(prev_up)
                e_u = P.a(lambda e, tt=tt, uslot=uslot: e.activation(
                    f32A[:, c.o_ups: c.o_ups + 512], bank(c, uslot),
                    AF.Copy, bias=0.0, scale=c.r1u[:, tt:tt + 1]), w=w_u)
                e_gu = P.v(lambda e, tt=tt, icc=icc, i=i: e.tensor_tensor(
                    f32A[:, c.o_gu + i * 4096 + icc * 512: c.o_gu + i * 4096 + icc * 512 + 512],
                    f32A[:, c.o_gates + i * 512: c.o_gates + i * 512 + 512],
                    f32A[:, c.o_ups: c.o_ups + 512], ALU.mult), w=[e_u, ev_gate[tt]])
                prev_up = e_gu
                prev_gu_gate[tt] = e_gu
                gu_evs[tt].append(e_gu)
            prev_icc[par] = gu_evs[tts[-1]][-1]
        c2_evs = {}
        for i, tt in enumerate(tts):
            gut = f32A[:, c.o_gu + i * 4096: c.o_gu + i * 4096 + 4096]
            w_dbgu = []
            if DEBUG and tt == 0:
                w_dbgu.append(P.dma(lambda e, gut=gut: e.dma_start(out=io["dbg_gu"], in_=gut),
                                    w=gu_evs[tt], sem="dbg"))
            P.v(lambda e, gut=gut: e.reduce_max(c.cols[:, 44:45], gut, AX.X,
                                                apply_absolute_value=True),
                w=gu_evs[tt] + w_dbgu, drain=True)
            P.v(lambda e: e.tensor_scalar_max(c.cols[:, 44:45], c.cols[:, 44:45], Q_EPS), drain=True)
            P.v(lambda e: e.reciprocal(c.cols[:, 45:46], c.cols[:, 44:45]), drain=True)
            e_s2 = P.v(lambda e: e.tensor_scalar(c.cols[:, 45:46], c.cols[:, 45:46], 127.0, None,
                                                 ALU.mult), drain=True)
            P.v(lambda e: e.tensor_mul(c.cols[:, 46:47], c.cols[:, 45:46], c.swd[:]),
                w=[sw_events["wdT"]], drain=True)
            e_rd = P.v(lambda e, tt=tt: e.reciprocal(c.rdc[:, tt:tt + 1], c.cols[:, 46:47]),
                       drain=True)
            e_c2a = P.v(lambda e, gut=gut: e.tensor_scalar(gut, gut, c.cols[:, 45:46], None, ALU.mult),
                        w=[e_s2])
            e_c2b = P.v(lambda e, gut=gut: e.tensor_scalar(gut, gut, c.Ccol[:], None, ALU.add),
                        w=[e_c2a])
            e_rnd = P.v(lambda e, gut=gut: e.tensor_scalar(gut, gut, c.nCcol[:], None, ALU.add),
                        w=[e_c2b])
            tr = []
            for ic in range(32):
                tslot = 4 + ic % 2
                w_tr = [e_rnd, e_id]
                if ic >= 2:
                    w_tr.append(tr[ic - 2])
                elif i >= 1:
                    w_tr.append(c2_evs[tts[i - 1]])
                elif e_prev_downevac is not None:
                    w_tr.append(e_prev_downevac)
                e_tr = P.t(lambda e, ic=ic, tslot=tslot, gut=gut: e.transpose(
                    bank(c, tslot)[:, 0:128], gut[:, ic * 128:(ic + 1) * 128], c.ident[:]), w=w_tr)
                tr.append(P.a(lambda e, ic=ic, i=i, tslot=tslot: e.copy(
                    bfA[:, c.o_c2T + i * 4096 + ic * 128: c.o_c2T + i * 4096 + ic * 128 + 128],
                    bank(c, tslot)[:, 0:128]), w=[e_tr]))
            c2_evs[tt] = tr[-1]
        down_mm = {}
        for ic in range(32):
            par = ic % 2
            cdb = bfA[:, c.o_cdblk + par * 1024: c.o_cdblk + par * 1024 + 1024]
            w_l = [prev_dstream.get(par), p2_ev["cd"][ic]]
            e_cd = P.dma(lambda e, ic=ic, cdb=cdb: e.dma_start(out=cdb, in_=scr["cd"][ic]), w=w_l,
                         sem="dd%d" % (ic % 2))
            for foc in range(2):
                for i, tt in enumerate(tts):
                    dslot = i * 2 + foc
                    w_mm = [e_cd]
                    if ic == 0:
                        w_mm += [c2_evs[tts[-1]]]
                    down_mm[(ic, foc, i)] = P.t(lambda e, ic=ic, foc=foc, i=i, dslot=dslot,
                                                cdb=cdb: e.matmul(
                        bank(c, dslot),
                        bfA[:, c.o_c2T + i * 4096 + ic * 128: c.o_c2T + i * 4096 + ic * 128 + 128],
                        cdb[:, foc * 512:(foc + 1) * 512], start=(ic == 0), stop=(ic == 31)),
                        w=w_mm)
            prev_dstream[par] = down_mm[(ic, 1, SS - 1)]
        e_last_eo = None
        for i, tt in enumerate(tts):
            x1r = f32A[:, c.o_x1rb: c.o_x1rb + 1024]
            w_xr = [x1_spills[-1], x1_spills[-2]]
            if out_dmas:
                w_xr.append(out_dmas[-1])
            e_xr = P.dma(lambda e, tt=tt, x1r=x1r: e.dma_start(out=x1r, in_=scr["x1"][tt]), w=w_xr,
                         sem="dr")
            for foc in range(2):
                dslot = i * 2 + foc
                w_xo = [down_mm[(31, foc, i)], x1_spills[-1], x1_spills[-2]]
                if (foc,) in prev_eo:
                    w_xo.append(prev_eo[(foc,)])
                e_xo = P.a(lambda e, dslot=dslot, tt=tt, foc=foc: e.activation(
                    c.x1stg[:, foc * 512: foc * 512 + 512], bank(c, dslot),
                    AF.Copy, bias=0.0, scale=c.rdc[:, tt:tt + 1]), w=w_xo)
                w_o = [e_xo, e_xr]
                if len(out_dmas) >= 2:
                    w_o.append(out_dmas[-2])
                e_o = P.v(lambda e, foc=foc: e.tensor_add(
                    c.x1stg[:, 1024 + foc * 512: 1024 + foc * 512 + 512],
                    c.x1stg[:, foc * 512: foc * 512 + 512],
                    f32A[:, c.o_x1rb + foc * 512: c.o_x1rb + foc * 512 + 512]), w=w_o)
                prev_eo[(foc,)] = e_o
                out_dmas.append(P.dma(lambda e, tt=tt, foc=foc: e.dma_start(
                    out=out[tt * 128:(tt + 1) * 128, foc * 512:(foc + 1) * 512],
                    in_=c.x1stg[:, 1024 + foc * 512: 1024 + foc * 512 + 512]), w=[e_o],
                    sem="do%d" % foc))
                e_last_eo = e_o
        e_prev_downevac = e_last_eo

    P.items["sync"].append((None, tuple((sname, tot) for sname, tot in P.cnt.items()), None, 0, False))
    return P


def _fix_dma_waits(P):
    for ring_sem, ring_eng in (("ds", "sync"), ("dg", "gpsimd")):
        bounds = []
        tot = 0
        for fn, waits, sem, amt, drain in P.items[ring_eng]:
            if waits:
                bounds.append(tot)
            if sem == ring_sem:
                tot += amt
        bounds.append(tot)
        import bisect
        def remap(v, bounds=bounds):
            i = bisect.bisect_left(bounds, v)
            return bounds[i] if i < len(bounds) else v
        for eng in ENGINES:
            new = []
            for fn, waits, sem, amt, drain in P.items[eng]:
                nw = tuple((s2, remap(v) if s2 == ring_sem else v) for (s2, v) in waits)
                new.append((fn, nw, sem, amt, drain))
            P.items[eng] = new


def build():
    nc = bass.Bass("TRN2", target_bir_lowering=False)
    io = {}
    io["x"] = nc.dram_tensor("x", [T, D], F32, kind="ExternalInput").ap()
    wT = {}
    for wn, shp in (("wqT", [D, D]), ("wkT", [D, D]), ("wq2T", [D, D]), ("wk2T", [D, D]),
                    ("wvT", [D, D]), ("wprojT", [D, D]), ("wgT", [D, INNER]), ("wuT", [D, INNER]),
                    ("wdT", [INNER, D])):
        wT[wn] = nc.dram_tensor(wn, shp, F32, kind="ExternalInput").ap()
    io["wT"] = wT
    io["consts"] = nc.dram_tensor("consts", [128, 4], F32, kind="ExternalInput").ap()
    io["onesrow"] = nc.dram_tensor("onesrow", [1, 128], F32, kind="ExternalInput").ap()
    io["trimask"] = nc.dram_tensor("trimask", [128, 128], BF16, kind="ExternalInput").ap()
    io["pswap"] = nc.dram_tensor("pswap", [128, 128], BF16, kind="ExternalInput").ap()
    io["ident"] = nc.dram_tensor("ident", [128, 128], F32, kind="ExternalInput").ap()
    io["flagin"] = nc.dram_tensor("flag", [1, 1], I32, kind="ExternalInput").ap()
    io["out"] = nc.dram_tensor("out", [N_OWN, D], F32, kind="ExternalOutput").ap()
    kind = "ExternalOutput"
    scr = {}
    scr["z"] = [nc.dram_tensor(f"scz{i}", [128, 2048], BF16, kind=kind).ap() for i in range(NPAIR)]
    scr["cg"] = [nc.dram_tensor(f"scg{i}", [128, INNER], BF16, kind=kind).ap() for i in range(8)]
    scr["cu"] = [nc.dram_tensor(f"scu{i}", [128, INNER], BF16, kind=kind).ap() for i in range(8)]
    scr["cd"] = [nc.dram_tensor(f"scd{i}", [128, D], BF16, kind=kind).ap() for i in range(32)]
    scr["x1"] = [nc.dram_tensor(f"sx1{i}", [128, D], F32, kind=kind).ap() for i in range(NT_OWN)]
    io["scr"] = scr
    if DEBUG:
        io["dbg_tab"] = nc.dram_tensor("dbg_tab", [128, 8192], F32, kind="ExternalOutput").ap()
        io["dbg_hnT"] = nc.dram_tensor("dbg_hnT", [128, 32768], BF16, kind="ExternalOutput").ap()
        io["dbg_p0"] = nc.dram_tensor("dbg_p0", [128, 16384], BF16, kind="ExternalOutput").ap()
        io["dbg_c1T"] = nc.dram_tensor("dbg_c1T", [128, 16384], BF16, kind="ExternalOutput").ap()
        io["dbg_gu"] = nc.dram_tensor("dbg_gu", [128, 4096], F32, kind="ExternalOutput").ap()
        io["dbg_cols"] = nc.dram_tensor("dbg_cols", [128, 41], F32, kind="ExternalOutput").ap()

    with ExitStack() as stack:
        c = alloc_tensors(nc, stack)
        progs = [build_half(nc, c, Prog(), h, io) for h in (0, 1)]
        sem_names = sorted(set(list(progs[0].cnt) + list(progs[1].cnt))) + ["fl"]
        sems = {s: stack.enter_context(nc.semaphore(s)) for s in sem_names}
        block = stack.enter_context(nc.Block())

        def replay(eng, items):
            waited = {}
            self_cnt = {}
            for fn, waits, sem, amt, drain in items:
                need = {}
                for (s, v) in waits:
                    if v > 0 and v > waited.get(s, 0):
                        need[s] = max(need.get(s, 0), v)
                for s, v in need.items():
                    eng.wait_ge(sems[s], v)
                    waited[s] = v
                if fn is None:
                    continue
                inst = fn(eng)
                if sem is not None:
                    inst.then_inc(sems[sem], amt)
                if drain and sem is not None:
                    eng.wait_ge(sems[sem], self_cnt.get(sem, 0) + amt)
                if sem is not None:
                    self_cnt[sem] = self_cnt.get(sem, 0) + amt
                    waited.setdefault(sem, 0)
                    if drain:
                        waited[sem] = max(waited[sem], self_cnt[sem])

        def section(eng_name):
            def body(eng):
                if eng_name == "sync":
                    eng.dma_start(out=c.flag[:], in_=io["flagin"]).then_inc(sems["fl"], 16)
                eng.wait_ge(sems["fl"], 16)
                with eng.register("rf_" + eng_name) as rf:
                    eng.load(rf, c.flag[0:1, 0:1])
                    with eng.If_eq(rf, 0):
                        replay(eng, progs[0].items[eng_name])
                    with eng.If_eq(rf, 1):
                        replay(eng, progs[1].items[eng_name])
            return body

        block.sync(section("sync"))
        block.tensor(section("tensor"))
        block.vector(section("vector"))
        block.scalar(section("scalar"))
        block.gpsimd(section("gpsimd"))
    return nc


def _consts_np():
    r = np.arange(128)
    consts = np.stack([np.ones(128), np.full(128, 2.0 ** -30), (r % 32).astype(float),
                       np.where(r % 64 < 32, 1.0, -1.0)], axis=1).astype(np.float32)
    onesrow = np.ones((1, 128), np.float32)
    tri = np.triu(np.ones((128, 128))).astype(ml_dtypes.bfloat16)
    psw = np.zeros((128, 128), ml_dtypes.bfloat16)
    for j in range(128):
        psw[j ^ 32, j] = 1.0
    ident = np.eye(128, dtype=np.float32)
    return consts, onesrow, tri, psw, ident


def kernel(L0_attn_x, L0_attn_Wq, L0_attn_Wk, L0_attn_Wq2, L0_attn_Wk2, L0_attn_Wv,
           L0_attn_Wproj, L0_ffn_w_gate, L0_ffn_w_up, L0_ffn_w_down):
    x = np.ascontiguousarray(np.asarray(L0_attn_x, dtype=np.float32))
    wTs = {
        "wqT": L0_attn_Wq, "wkT": L0_attn_Wk, "wq2T": L0_attn_Wq2, "wk2T": L0_attn_Wk2,
        "wvT": L0_attn_Wv, "wprojT": L0_attn_Wproj, "wgT": L0_ffn_w_gate,
        "wuT": L0_ffn_w_up, "wdT": L0_ffn_w_down,
    }
    wTs = {k: np.ascontiguousarray(np.asarray(v, dtype=np.float32).T) for k, v in wTs.items()}
    consts, onesrow, tri, psw, ident = _consts_np()

    nc = build()
    in_maps = []
    for core in range(8):
        b, half = core // 2, core % 2
        m = {"x": np.ascontiguousarray(x[b]),
             "consts": consts, "onesrow": onesrow, "trimask": tri, "pswap": psw,
             "ident": ident, "flag": np.full((1, 1), half, np.int32)}
        m.update(wTs)
        in_maps.append(m)
    res = run_bass_kernel_spmd(nc, in_maps, core_ids=list(range(8)))
    full = np.empty((4, T, D), np.float32)
    for core in range(8):
        b, half = core // 2, core % 2
        o = res.results[core]["out"]
        if half == 0:
            full[b, 0:1024] = o[0:1024]
            full[b, 3072:4096] = o[1024:2048]
        else:
            full[b, 1024:3072] = o
    return full


# version 08
# speedup vs baseline: 1.0205x; 1.0205x over previous
import numpy as np
import ml_dtypes
from contextlib import ExitStack

import concourse.bass as bass
import concourse.mybir as mybir
from concourse.bass_utils import run_bass_kernel_spmd

F32 = mybir.dt.float32
BF16 = mybir.dt.bfloat16
I32 = mybir.dt.int32
ALU = mybir.AluOpType
AF = mybir.ActivationFunctionType
AX = mybir.AxisListType

D = 1024
T = 4096
INNER = 4096
NPAIR = 8
N_OWN = 2048
NT_OWN = 16
SS = 4
LN_EPS = 1e-5
Q_EPS = 1e-5
RNDC = 1.5 * 2.0 ** 23
RNDK = 1.5 * 2.0 ** 52 - 1.5 * 2.0 ** 23

ENGINES = ("sync", "tensor", "vector", "scalar", "gpsimd")
DEBUG = False


def own_chunks(half):
    return [0, 1, 6, 7] if half == 0 else [2, 3, 4, 5]


def kv_tiles(half):
    return 32 if half == 0 else 24


class Prog:
    def __init__(self):
        self.items = {e: [] for e in ENGINES}
        self.cnt = {}

    def ev(self, sem):
        return (sem, self.cnt.get(sem, 0))

    def _add(self, eng, fn, waits, sem, amt, drain):
        self.cnt[sem] = self.cnt.get(sem, 0) + amt
        evt = (sem, self.cnt[sem])
        self.items[eng].append((fn, tuple(w for w in waits if w is not None), sem, amt, drain))
        return evt

    def dma(self, fn, w=(), sem="ds"):
        return self._add("sync", fn, w, sem, 16, False)

    def dmag(self, fn, w=(), sem="dg"):
        return self._add("gpsimd", fn, w, sem, 16, False)

    def t(self, fn, w=(), drain=False):
        return self._add("tensor", fn, w, "t", 1, drain)

    def v(self, fn, w=(), drain=False):
        return self._add("vector", fn, w, "v", 1, drain)

    def a(self, fn, w=(), drain=False):
        return self._add("scalar", fn, w, "a", 1, drain)

    def g(self, fn, w=(), drain=False):
        return self._add("gpsimd", fn, w, "g", 1, drain)

    def barrier(self, eng):
        if eng == "vector":
            return self.v(lambda e: e.engine_nop())
        if eng == "scalar":
            return self.a(lambda e: e.engine_nop())
        raise ValueError(eng)


class Ctx:
    pass


def alloc_tensors(nc, stack):
    c = Ctx()
    sb = lambda n, s, d: stack.enter_context(nc.sbuf_tensor(n, s, d))
    c.bfA = sb("bfA", [128, 60416], BF16)
    c.f32A = sb("f32A", [128, 19968], F32)
    c.ps = stack.enter_context(nc.psum_tensor("ps", [128, 4096], F32))

    c.o_hnT = 0
    c.o_KmA = 32768
    c.o_KmB = 36864
    c.o_Vtok = 40960
    c.o_QmA = 45056
    c.o_QmB = 47104
    c.o_WTqA = 49152
    c.o_WTqB = 50176
    c.o_WTkA = 51200
    c.o_WTkB = 52224
    c.o_WTv = 53248
    c.o_Xb = 54272
    c.o_Pbuf = 55296
    c.o_zbuf = 56320
    c.o_cnat = 58368
    c.o_zfull = 0
    c.o_WprojT = 16384
    c.o_c1T = 24576
    c.o_icc = 0
    c.o_c2T = 8192
    c.o_cdblk = 49152

    c.o_cosT = 0
    c.o_sinS = 4096
    c.o_xstg = 8192
    c.o_wstg = 10240
    c.o_d2 = 14336
    c.o_Xf = 15360
    c.o_tA = 16384
    c.o_s2sb = 17408
    c.o_gu = 0
    c.o_gates = 16384
    c.o_ups = 18432
    c.o_x1rb = 18944

    c.cols = sb("cols", [128, 64], F32)
    c.r1g = sb("r1g", [128, 16], F32)
    c.r1u = sb("r1u", [128, 16], F32)
    c.rdc = sb("rdc", [128, 16], F32)
    c.Ccol = sb("Ccol", [128, 1], F32)
    c.nCcol = sb("nCcol", [128, 1], F32)
    c.swg = sb("swg", [128, 1], F32)
    c.swu = sb("swu", [128, 1], F32)
    c.swd = sb("swd", [128, 1], F32)
    c.acc = sb("acc", [128, 80], F32)
    c.consts = sb("constsb", [128, 4], F32)
    c.onesrow = sb("onesrowb", [1, 128], F32)
    c.trimask = sb("trimaskb", [128, 128], BF16)
    c.pswap = sb("pswapb", [128, 128], BF16)
    c.ident = sb("identb", [128, 128], F32)
    c.flag = sb("flagsb", [1, 1], I32)
    c.x1stg = sb("x1stg", [128, 2048], F32)
    c.ssc = sb("ssc", [1, 1], F32)
    return c


def bank(c, i):
    return c.ps[:, i * 512:(i + 1) * 512]


def build_half(nc, c, P, half, io):
    OC = own_chunks(half)
    KT = kv_tiles(half)
    bfA, f32A, ps = c.bfA, c.f32A, c.ps
    x, out = io["x"], io["out"]
    wT = io["wT"]
    scr = io["scr"]

    ones = c.consts[:, 0:1]
    epsv = c.consts[:, 1:2]
    jcol = c.consts[:, 2:3]
    sgn = c.consts[:, 3:4]
    cosT = f32A[:, c.o_cosT:c.o_cosT + 4096]
    sinS = f32A[:, c.o_sinS:c.o_sinS + 4096]

    P.dma(lambda e: e.dma_start(out=c.consts[:, 0:4], in_=io["consts"]), sem="dc")
    P.dma(lambda e: e.dma_start(out=c.onesrow[:], in_=io["onesrow"]), sem="dc")
    P.dma(lambda e: e.dma_start(out=c.trimask[:], in_=io["trimask"]), sem="dc")
    P.dma(lambda e: e.dma_start(out=c.pswap[:], in_=io["pswap"]), sem="dc")
    e_consts = e_or = e_tri = e_psw = e_id = P.dma(
        lambda e: e.dma_start(out=c.ident[:], in_=io["ident"]), sem="dc")

    P.v(lambda e: e.tensor_add(c.cols[:, 0:1], ones, epsv), w=[e_consts], drain=True)
    P.v(lambda e: e.tensor_sub(c.cols[:, 0:1], c.cols[:, 0:1], ones), drain=True)
    P.v(lambda e: e.tensor_scalar(c.cols[:, 0:1], c.cols[:, 0:1], 2.0 ** 30, None, ALU.mult), drain=True)
    P.v(lambda e: e.tensor_scalar(c.Ccol[:], c.cols[:, 0:1], RNDK, RNDC, ALU.mult, ALU.add), drain=True)
    e_C = P.v(lambda e: e.tensor_scalar(c.nCcol[:], c.Ccol[:], -1.0, None, ALU.mult), drain=True)

    e_tab = P.dma(lambda e: e.dma_start(out=f32A[:, 0:8192], in_=io["ropetab"]), sem="dc")

    ln_done = []
    for gt in range(KT):
        par = gt % 2
        xst = f32A[:, c.o_xstg + par * 1024: c.o_xstg + (par + 1) * 1024]
        d2 = f32A[:, c.o_d2:c.o_d2 + 1024]
        w_x = [ln_done[gt - 2]] if gt >= 2 else []
        e_x = P.dma(lambda e, gt=gt, xst=xst: e.dma_start(out=xst, in_=x[gt * 128:(gt + 1) * 128, :]),
                    w=w_x, sem="dx%d" % (gt % 2))
        w_cols = [ln_done[gt - 1]] if gt >= 1 else [e_C]
        P.v(lambda e, xst=xst: e.reduce_sum(c.cols[:, 16:17], xst, AX.X), w=[e_x] + w_cols, drain=True)
        e_nm = P.v(lambda e: e.tensor_scalar(c.cols[:, 17:18], c.cols[:, 16:17], -1.0 / D, None, ALU.mult),
                   drain=True)
        e_d2 = P.a(lambda e, xst=xst, d2=d2: e.activation(d2, xst, AF.Square, bias=c.cols[:, 17:18],
                                                          scale=1.0), w=[e_nm], drain=True)
        P.v(lambda e, d2=d2: e.reduce_sum(c.cols[:, 18:19], d2, AX.X), w=[e_d2], drain=True)
        e_vp = P.v(lambda e: e.tensor_scalar(c.cols[:, 18:19], c.cols[:, 18:19], 1.0 / D, LN_EPS,
                                             ALU.mult, ALU.add), drain=True)
        e_sq = P.a(lambda e: e.activation(c.cols[:, 19:20], c.cols[:, 18:19], AF.Sqrt, bias=0.0,
                                          scale=1.0), w=[e_vp], drain=True)
        P.v(lambda e: e.reciprocal(c.cols[:, 20:21], c.cols[:, 19:20]), w=[e_sq], drain=True)
        for _ in range(2):
            P.v(lambda e: e.tensor_mul(c.cols[:, 21:22], c.cols[:, 20:21], c.cols[:, 20:21]), drain=True)
            P.v(lambda e: e.tensor_mul(c.cols[:, 21:22], c.cols[:, 21:22], c.cols[:, 18:19]), drain=True)
            P.v(lambda e: e.tensor_scalar(c.cols[:, 21:22], c.cols[:, 21:22], -0.5, 1.5, ALU.mult,
                                          ALU.add), drain=True)
            P.v(lambda e: e.tensor_mul(c.cols[:, 20:21], c.cols[:, 20:21], c.cols[:, 21:22]), drain=True)
        e_nmr = P.v(lambda e: e.tensor_mul(c.cols[:, 22:23], c.cols[:, 17:18], c.cols[:, 20:21]),
                    drain=True)
        e_hn = P.a(lambda e, xst=xst, d2=d2: e.activation(d2, xst, AF.Identity, bias=c.cols[:, 22:23],
                                                          scale=c.cols[:, 20:21]), w=[e_nmr], drain=True)
        evs = []
        for fc in range(8):
            slot = fc % 2
            w_tr = [e_hn, e_id]
            if fc >= 2:
                w_tr.append(evs[fc - 2])
            elif gt >= 1:
                w_tr.append(ln_done[gt - 1])
            e_tr = P.t(lambda e, fc=fc, slot=slot, d2=d2: e.transpose(
                bank(c, slot)[:, 0:128], d2[:, fc * 128:(fc + 1) * 128], c.ident[:]), w=w_tr)
            e_ev = P.a(lambda e, fc=fc, slot=slot, gt=gt: e.copy(
                bfA[:, c.o_hnT + fc * 4096 + gt * 128: c.o_hnT + fc * 4096 + gt * 128 + 128],
                bank(c, slot)[:, 0:128]), w=[e_tr])
            evs.append(e_ev)
        ln_done.append(evs[-1])
    e_hnT = ln_done[-1]

    if DEBUG:
        P.dma(lambda e: e.dma_start(out=io["dbg_tab"], in_=f32A[:, 0:8192]), w=[e_tab], sem="dbg")
        P.dma(lambda e: e.dma_start(out=io["dbg_hnT"], in_=bfA[:, 0:32768]), w=[e_hnT], sem="dbg")

    slot0 = {"wgT": 0, "wuT": 16, "wdT": 48}
    nchunks = {"wgT": 8, "wuT": 8, "wdT": 32}
    ccols = {"wgT": INNER, "wuT": INNER, "wdT": D}
    prev_red = e_tab
    for wname in ("wgT", "wuT", "wdT"):
        for ck in range(nchunks[wname]):
            wst = f32A[:, c.o_wstg: c.o_wstg + ccols[wname]]
            e_l = P.dma(lambda e, wname=wname, ck=ck, wst=wst: e.dma_start(
                out=wst, in_=wT[wname][ck * 128:(ck + 1) * 128, :]), w=[prev_red], sem="dw")
            prev_red = P.v(lambda e, wst=wst, s=slot0[wname] + ck: e.reduce_sum(
                c.acc[:, s:s + 1], wst, AX.X, apply_absolute_value=True), w=[e_l])
    e_p1 = P.barrier("vector")

    sw_events = {}

    def emit_sw_totals():
        for wname, swcol, nelem in (("wgT", c.swg, INNER * D), ("wuT", c.swu, INNER * D),
                                    ("wdT", c.swd, D * INNER)):
            s0, nck = slot0[wname], nchunks[wname]
            e_r = P.v(lambda e, s0=s0, nck=nck: e.reduce_sum(c.cols[:, 24:25],
                                                             c.acc[:, s0:s0 + nck], AX.X),
                      w=[e_p1], drain=True)
            e_m1 = P.t(lambda e: e.matmul(ps[0:1, 1024:1025], c.cols[:, 24:25], ones,
                                          start=True, stop=True), w=[e_r, e_consts, P.ev("v")])
            e_e1 = P.a(lambda e: e.copy(c.ssc[:], ps[0:1, 1024:1025]), w=[e_m1], drain=True)
            e_m2 = P.t(lambda e: e.matmul(ps[:, 1025:1026], c.onesrow[:], c.ssc[:],
                                          start=True, stop=True), w=[e_e1, e_or])
            e_e2 = P.a(lambda e: e.copy(c.cols[:, 25:26], ps[:, 1025:1026]), w=[e_m2], drain=True)
            P.v(lambda e, nelem=nelem: e.tensor_scalar(c.cols[:, 26:27], c.cols[:, 25:26],
                                                       1.0 / nelem, None, ALU.mult), w=[e_e2], drain=True)
            P.v(lambda e: e.tensor_scalar_max(c.cols[:, 26:27], c.cols[:, 26:27], Q_EPS), drain=True)
            sw_events[wname] = P.v(lambda e, swcol=swcol: e.reciprocal(swcol[:], c.cols[:, 26:27]),
                                   drain=True)

    p2_ev = {"cg": [None] * 8, "cu": [None] * 8, "cd": [None] * 32}
    _p2_state = {"prevq": e_p1, "prevst": []}

    def emit_p2_chunk(wname, ck):
        swcol = {"wgT": c.swg, "wuT": c.swu, "wdT": c.swd}[wname]
        scrkey = {"wgT": "cg", "wuT": "cu", "wdT": "cd"}[wname]
        ncol = ccols[wname]
        wst = f32A[:, c.o_wstg: c.o_wstg + ncol]
        w_l = [_p2_state["prevq"]]
        e_l = P.dma(lambda e, wname=wname, ck=ck, wst=wst: e.dma_start(
            out=wst, in_=wT[wname][ck * 128:(ck + 1) * 128, :]), w=w_l, sem="dw")
        eq1 = P.v(lambda e, wst=wst, swcol=swcol: e.tensor_scalar(wst, wst, swcol[:], None, ALU.mult),
                  w=[e_l, sw_events[wname], e_C])
        eq2 = P.v(lambda e, wst=wst: e.tensor_scalar(wst, wst, c.Ccol[:], None, ALU.add), w=[eq1])
        eq3 = P.v(lambda e, wst=wst: e.tensor_scalar(wst, wst, c.nCcol[:], None, ALU.add), w=[eq2])
        e_last = None
        e_q = None
        for hh in range(0, ncol, 2048):
            hw = min(2048, ncol - hh)
            cn = bfA[:, c.o_cnat:c.o_cnat + hw]
            w_cn = [_p2_state["prevst"][-1]] if _p2_state["prevst"] else []
            e_q = P.v(lambda e, wst=wst, cn=cn, hh=hh, hw=hw: e.tensor_scalar(
                cn, wst[:, hh:hh + hw], -1.0, 1.0, ALU.max, ALU.min), w=w_cn + [eq3])
            e_st = P.dma(lambda e, cn=cn, scrkey=scrkey, ck=ck, hh=hh, hw=hw: e.dma_start(
                out=scr[scrkey][ck][:, hh:hh + hw], in_=cn), w=[e_q], sem="dq")
            _p2_state["prevst"].append(e_st)
            e_last = e_st
        _p2_state["prevq"] = e_q
        p2_ev[scrkey][ck] = e_last

    p2_list = [("wgT", i) for i in range(8)] + [("wuT", i) for i in range(8)] + \
              [("wdT", i) for i in range(32)]

    rst = {}
    w_dbg_guard = []
    zspill = []
    e_prev_pair = None
    e_wt_free = None
    for hp in range(NPAIR):
        wl = []
        w_guard = [e_wt_free] if e_wt_free is not None else []
        for (wa, wb, offA, offB) in (("wqT", "wq2T", c.o_WTqA, c.o_WTqB),
                                     ("wkT", "wk2T", c.o_WTkA, c.o_WTkB)):
            for sub, off in ((0, offA), (1, offB)):
                for wn, halfcol in ((wa, 0), (wb, 64)):
                    def f(e, wn=wn, sub=sub, off=off, halfcol=halfcol, hp=hp):
                        src = wT[wn].rearrange("(fc p) o -> p fc o", p=128)[
                            :, :, hp * 128 + sub * 64: hp * 128 + sub * 64 + 64]
                        dst = bfA[:, off:off + 1024].rearrange("p (fc z) -> p fc z", z=128)[
                            :, :, halfcol:halfcol + 64]
                        return e.dma_start(out=dst, in_=src)
                    wl.append(P.dmag(f, w=w_guard, sem="dwt"))
        def fv(e, hp=hp):
            src = wT["wvT"].rearrange("(fc p) o -> p fc o", p=128)[:, :, hp * 128:(hp + 1) * 128]
            dst = bfA[:, c.o_WTv:c.o_WTv + 1024].rearrange("p (fc z) -> p fc z", z=128)
            return e.dma_start(out=dst, in_=src)
        wl.append(P.dmag(fv, w=w_guard, sem="dwt"))
        P.v(lambda e: e.tensor_scalar(bfA[:, c.o_WTqA:c.o_WTqA + 1024],
                                      bfA[:, c.o_WTqA:c.o_WTqA + 1024], 1.0 / 64, None, ALU.mult),
            w=wl)
        e_wready = P.v(lambda e: e.tensor_scalar(bfA[:, c.o_WTqB:c.o_WTqB + 1024],
                                                 bfA[:, c.o_WTqB:c.o_WTqB + 1024], 1.0 / 64, None,
                                                 ALU.mult), w=wl)

        w_bufs = [e_prev_pair] if e_prev_pair is not None else []
        w_bufs += w_dbg_guard
        if e_prev_pair is None and hp > 0:
            w_bufs.append(zspill[-1])

        vp_last = []
        for ch in range(KT // 4):
            par = ch % 2
            pslot = 5 + par
            e_mm = None
            for fc in range(8):
                w_mm = []
                if fc == 0:
                    w_mm = [e_hnT] + wl + [rst.get("ps%d" % par)]
                e_mm = P.t(lambda e, fc=fc, ch=ch, pslot=pslot: e.matmul(
                    bank(c, pslot), bfA[:, c.o_WTv + fc * 128:c.o_WTv + fc * 128 + 128],
                    bfA[:, c.o_hnT + fc * 4096 + ch * 512: c.o_hnT + fc * 4096 + ch * 512 + 512],
                    start=(fc == 0), stop=(fc == 7)), w=w_mm)
            e_ev = P.a(lambda e, pslot=pslot, ch=ch, par=par: e.copy(
                f32A[:, c.o_Xf + par * 512: c.o_Xf + par * 512 + 512], bank(c, pslot)),
                w=[e_mm, rst.get("xf%d" % par)])
            rst["ps%d" % par] = e_ev
            tr_ev = []
            e_tr = None
            for j in range(4):
                tslot = j % 2
                w_tr = [e_ev, e_id] + w_bufs
                if j >= 2:
                    w_tr.append(tr_ev[j - 2])
                elif ch >= 1:
                    w_tr.append(vp_last[ch - 1])
                e_tr = P.t(lambda e, j=j, ch=ch, tslot=tslot, par=par: e.transpose(
                    bank(c, tslot)[:, 0:128],
                    f32A[:, c.o_Xf + par * 512 + j * 128: c.o_Xf + par * 512 + j * 128 + 128],
                    c.ident[:]), w=w_tr)
                tr_ev.append(P.a(lambda e, j=j, ch=ch, tslot=tslot: e.copy(
                    bfA[:, c.o_Vtok + (ch * 4 + j) * 128: c.o_Vtok + (ch * 4 + j) * 128 + 128],
                    bank(c, tslot)[:, 0:128]), w=[e_tr]))
            rst["xf%d" % par] = e_tr
            vp_last.append(tr_ev[-1])
        e_vtok = vp_last[-1]

        def emit_rope(setoff, dstoff, colbases):
            evs = []
            for i, (dcb, hcb, tcb) in enumerate(colbases):
                par = i % 2
                pslot = 5 + par
                sslot = 7 if par == 0 else 4
                e_mm = None
                for fc in range(8):
                    w_mm = []
                    if fc == 0:
                        w_mm = [e_wready, e_hnT, rst.get("ps%d" % par)]
                    e_mm = P.t(lambda e, fc=fc, pslot=pslot, setoff=setoff, hcb=hcb: e.matmul(
                        bank(c, pslot), bfA[:, setoff + fc * 128: setoff + fc * 128 + 128],
                        bfA[:, c.o_hnT + fc * 4096 + hcb: c.o_hnT + fc * 4096 + hcb + 512],
                        start=(fc == 0), stop=(fc == 7)), w=w_mm)
                e_f = P.a(lambda e, par=par, pslot=pslot: e.copy(
                    f32A[:, c.o_Xf + par * 512: c.o_Xf + par * 512 + 512], bank(c, pslot)),
                    w=[e_mm, rst.get("xf%d" % par)])
                e_b = P.a(lambda e, par=par, pslot=pslot: e.copy(
                    bfA[:, c.o_Xb + par * 512: c.o_Xb + par * 512 + 512], bank(c, pslot)),
                    w=[rst.get("xb%d" % par)])
                rst["ps%d" % par] = e_b
                e_sw = P.t(lambda e, par=par, sslot=sslot: e.matmul(
                    bank(c, sslot), c.pswap[:],
                    bfA[:, c.o_Xb + par * 512: c.o_Xb + par * 512 + 512],
                    start=True, stop=True), w=[e_b, e_psw, rst.get("sw%d" % par)])
                rst["xb%d" % par] = e_sw
                e_tA = P.v(lambda e, par=par, sslot=sslot, tcb=tcb: e.tensor_tensor(
                    f32A[:, c.o_tA + par * 512: c.o_tA + par * 512 + 512], bank(c, sslot),
                    sinS[:, tcb:tcb + 512], ALU.mult),
                    w=[e_sw, e_tab, rst.get("ta%d" % par)])
                rst["sw%d" % par] = e_tA
                e_tB = P.g(lambda e, par=par, tcb=tcb: e.tensor_tensor(
                    f32A[:, c.o_Xf + par * 512: c.o_Xf + par * 512 + 512],
                    f32A[:, c.o_Xf + par * 512: c.o_Xf + par * 512 + 512],
                    cosT[:, tcb:tcb + 512], ALU.mult), w=[e_f, e_tab])
                e_d = P.v(lambda e, par=par, dcb=dcb, dstoff=dstoff: e.tensor_add(
                    bfA[:, dstoff + dcb: dstoff + dcb + 512],
                    f32A[:, c.o_tA + par * 512: c.o_tA + par * 512 + 512],
                    f32A[:, c.o_Xf + par * 512: c.o_Xf + par * 512 + 512]),
                    w=[e_tA, e_tB] + w_bufs)
                rst["xf%d" % par] = e_d
                rst["ta%d" % par] = e_d
                evs.append((e_mm, e_f, e_b, e_sw, e_tA, e_d))
            return evs

        k_cbs = [(ch * 512, ch * 512, ch * 512) for ch in range(KT // 4)]
        q_cbs = [(oc * 512, OC[oc] * 512, OC[oc] * 512) for oc in range(4)]
        evKA = emit_rope(c.o_WTkA, c.o_KmA, k_cbs)
        evKB = emit_rope(c.o_WTkB, c.o_KmB, k_cbs)
        evQA = emit_rope(c.o_WTqA, c.o_QmA, q_cbs)
        evQB = emit_rope(c.o_WTqB, c.o_QmB, q_cbs)
        e_rope = evQB[-1][5]
        e_wt_free = evQB[-1][0]

        zevs = []
        e_zguard = zspill[-1] if zspill else None
        for hb, (koff, qoff, hrow) in enumerate(((c.o_KmA, c.o_QmA, 0), (c.o_KmB, c.o_QmB, 64))):
            for oc in range(4):
                gstart = OC[oc] * 512
                nkt = OC[oc] * 4 + 4
                eP = {}
                ez = None
                for kt in range(nkt):
                    par = kt % 2
                    sb_slot = 0 if par == 0 else 2
                    s2_slot = 1 if par == 0 else 3
                    c0 = max(0, kt * 128 - gstart)
                    w_mm = [e_rope, e_vtok]
                    if kt >= 2:
                        w_mm.append(eP[kt - 2])
                    e_s = P.t(lambda e, koff=koff, qoff=qoff, kt=kt, oc=oc, c0=c0, sb_slot=sb_slot:
                              e.matmul(bank(c, sb_slot)[:, c0:512],
                                       bfA[0:64, koff + kt * 128: koff + kt * 128 + 128],
                                       bfA[0:64, qoff + oc * 512 + c0: qoff + oc * 512 + 512],
                                       start=True, stop=True, tile_position=(0, 0)), w=w_mm)
                    e_s2 = P.t(lambda e, koff=koff, qoff=qoff, kt=kt, oc=oc, c0=c0, s2_slot=s2_slot:
                               e.matmul(bank(c, s2_slot)[:, c0:512],
                                        bfA[64:128, koff + kt * 128: koff + kt * 128 + 128],
                                        bfA[64:128, qoff + oc * 512 + c0: qoff + oc * 512 + 512],
                                        start=True, stop=True, tile_position=(64, 0)))
                    w_e = [e_s2]
                    if kt >= 2:
                        w_e.append(eP[kt - 2])
                    e_ev2 = P.a(lambda e, par=par, c0=c0, s2_slot=s2_slot: e.copy(
                        f32A[:, c.o_s2sb + par * 512 + c0: c.o_s2sb + par * 512 + 512],
                        bank(c, s2_slot)[:, c0:512]), w=w_e)
                    w_p = [e_ev2, e_s]
                    if kt >= 2:
                        w_p.append(eP[kt - 2])
                    if ez is not None:
                        w_p.append(ez)
                    e_p = P.v(lambda e, par=par, c0=c0, sb_slot=sb_slot: e.tensor_tensor(
                        bfA[:, c.o_Pbuf + par * 512 + c0: c.o_Pbuf + par * 512 + 512],
                        bank(c, sb_slot)[:, c0:512],
                        f32A[:, c.o_s2sb + par * 512 + c0: c.o_s2sb + par * 512 + 512],
                        ALU.mult), w=w_p)
                    if kt * 128 >= gstart:
                        e_p = P.v(lambda e, par=par, c0=c0: e.tensor_tensor(
                            bfA[:, c.o_Pbuf + par * 512 + c0: c.o_Pbuf + par * 512 + c0 + 128],
                            bfA[:, c.o_Pbuf + par * 512 + c0: c.o_Pbuf + par * 512 + c0 + 128],
                            c.trimask[:], ALU.mult), w=[e_tri, e_p])
                    eP[kt] = e_p
                    w_z = [e_p]
                    if kt == 0 and zevs:
                        w_z.append(zevs[-1])
                    ez = P.t(lambda e, kt=kt, hrow=hrow, c0=c0, nkt=nkt: e.matmul(
                        bank(c, 4)[hrow:hrow + 64, c0:512],
                        bfA[:, c.o_Vtok + kt * 128 + hrow: c.o_Vtok + kt * 128 + hrow + 64],
                        bfA[:, c.o_Pbuf + (kt % 2) * 512 + c0: c.o_Pbuf + (kt % 2) * 512 + 512],
                        start=(kt == 0), stop=(kt == nkt - 1),
                        tile_position=(0, hrow)), w=w_z)
                w_zv = [ez]
                if e_zguard is not None:
                    w_zv.append(e_zguard)
                zevs.append(P.a(lambda e, hrow=hrow, oc=oc: e.copy(
                    bfA[hrow:hrow + 64, c.o_zbuf + oc * 512: c.o_zbuf + oc * 512 + 512],
                    bank(c, 4)[hrow:hrow + 64, 0:512]), w=w_zv))
        e_prev_pair = zevs[-1]
        zspill.append(P.dma(lambda e, hp=hp: e.dma_start(out=scr["z"][hp],
                                                         in_=bfA[:, c.o_zbuf:c.o_zbuf + 2048]),
                            w=zevs, sem="dz"))
        if DEBUG and hp == 0:
            e_dbg = P.dma(lambda e: e.dma_start(out=io["dbg_p0"],
                                                in_=bfA[:, c.o_KmA:c.o_KmA + 16384]),
                          w=[e_rope, e_vtok], sem="dbg")
            e_prev_pair = None
            w_dbg_guard.append(e_dbg)

        if hp == 1:
            emit_sw_totals()
        if hp >= 2:
            for _ in range(16):
                if p2_list:
                    emit_p2_chunk(*p2_list.pop(0))
    while p2_list:
        emit_p2_chunk(*p2_list.pop(0))

    e_zf = []
    for hp in range(NPAIR):
        e_zf.append(P.dma(lambda e, hp=hp: e.dma_start(
            out=bfA[:, c.o_zfull + hp * 2048: c.o_zfull + hp * 2048 + 2048], in_=scr["z"][hp]),
            w=[zspill[hp], e_wt_free], sem="dzf"))
    e_wp = []
    for hp in range(NPAIR):
        e_wp.append(P.dmag(lambda e, hp=hp: e.dma_start(
            out=bfA[:, c.o_WprojT + hp * 1024: c.o_WprojT + hp * 1024 + 1024],
            in_=wT["wprojT"][hp * 128:(hp + 1) * 128, :]), w=[e_prev_pair], sem="dwp"))

    c1_evs = []
    x1_spills = []
    prev_x1rd = {}
    prev_yadd = {}
    for tt in range(NT_OWN):
        par = tt % 2
        x1t = c.x1stg[:, par * 1024:(par + 1) * 1024]
        grow = OC[tt // 4] * 512 + (tt % 4) * 128
        w_x = list(prev_x1rd.get(par, ()))
        e_x = P.dma(lambda e, grow=grow, par=par: e.dma_start(
            out=f32A[:, c.o_xstg + par * 1024: c.o_xstg + par * 1024 + 1024],
            in_=x[grow:grow + 128, :]), w=w_x, sem="dy%d" % par)
        ev_oc = []
        for oc2 in range(2):
            yslot = 2 + oc2
            e_mm = None
            for hp in range(NPAIR):
                w_mm = []
                if hp == 0:
                    w_mm = list(e_zf) + list(e_wp)
                    if oc2 in prev_yadd:
                        w_mm.append(prev_yadd[oc2])
                e_mm = P.t(lambda e, hp=hp, tt=tt, oc2=oc2, yslot=yslot: e.matmul(
                    bank(c, yslot),
                    bfA[:, c.o_zfull + hp * 2048 + tt * 128: c.o_zfull + hp * 2048 + tt * 128 + 128],
                    bfA[:, c.o_WprojT + hp * 1024 + oc2 * 512: c.o_WprojT + hp * 1024 + oc2 * 512 + 512],
                    start=(hp == 0), stop=(hp == 7)), w=w_mm)
            e_add = P.v(lambda e, oc2=oc2, yslot=yslot, x1t=x1t, par=par: e.tensor_add(
                x1t[:, oc2 * 512:(oc2 + 1) * 512], bank(c, yslot),
                f32A[:, c.o_xstg + par * 1024 + oc2 * 512: c.o_xstg + par * 1024 + oc2 * 512 + 512]),
                w=[e_mm, e_x] + w_x)
            prev_yadd[oc2] = e_add
            ev_oc.append(e_add)
        e_sp = P.dma(lambda e, tt=tt, x1t=x1t: e.dma_start(out=scr["x1"][tt], in_=x1t), w=ev_oc,
                     sem="dsp%d" % (tt % 2))
        x1_spills.append(e_sp)

        d2 = f32A[:, c.o_d2:c.o_d2 + 1024]
        w_cols = [c1_evs[tt - 1]] if tt >= 1 else [e_C]
        P.v(lambda e, x1t=x1t: e.reduce_sum(c.cols[:, 30:31], x1t, AX.X), w=ev_oc + w_cols, drain=True)
        e_nm = P.v(lambda e: e.tensor_scalar(c.cols[:, 31:32], c.cols[:, 30:31], -1.0 / D, None,
                                             ALU.mult), drain=True)
        e_d2 = P.a(lambda e, x1t=x1t, d2=d2: e.activation(d2, x1t, AF.Square, bias=c.cols[:, 31:32],
                                                          scale=1.0), w=[e_nm], drain=True)
        P.v(lambda e, d2=d2: e.reduce_sum(c.cols[:, 32:33], d2, AX.X), w=[e_d2], drain=True)
        e_vp = P.v(lambda e: e.tensor_scalar(c.cols[:, 32:33], c.cols[:, 32:33], 1.0 / D, LN_EPS,
                                             ALU.mult, ALU.add), drain=True)
        e_sq = P.a(lambda e: e.activation(c.cols[:, 33:34], c.cols[:, 32:33], AF.Sqrt, bias=0.0,
                                          scale=1.0), w=[e_vp], drain=True)
        P.v(lambda e: e.reciprocal(c.cols[:, 34:35], c.cols[:, 33:34]), w=[e_sq], drain=True)
        for _ in range(2):
            P.v(lambda e: e.tensor_mul(c.cols[:, 35:36], c.cols[:, 34:35], c.cols[:, 34:35]), drain=True)
            P.v(lambda e: e.tensor_mul(c.cols[:, 35:36], c.cols[:, 35:36], c.cols[:, 32:33]), drain=True)
            P.v(lambda e: e.tensor_scalar(c.cols[:, 35:36], c.cols[:, 35:36], -0.5, 1.5, ALU.mult,
                                          ALU.add), drain=True)
            P.v(lambda e: e.tensor_mul(c.cols[:, 34:35], c.cols[:, 34:35], c.cols[:, 35:36]), drain=True)
        e_rstd = P.barrier("vector")
        e_ab = P.a(lambda e, x1t=x1t, d2=d2: e.activation(d2, x1t, AF.Abs, bias=c.cols[:, 31:32],
                                                          scale=1.0), w=[e_rstd], drain=True)
        P.v(lambda e, d2=d2: e.reduce_max(c.cols[:, 36:37], d2, AX.X), w=[e_ab], drain=True)
        P.v(lambda e: e.tensor_mul(c.cols[:, 36:37], c.cols[:, 36:37], c.cols[:, 34:35]), drain=True)
        P.v(lambda e: e.tensor_scalar_max(c.cols[:, 36:37], c.cols[:, 36:37], Q_EPS), drain=True)
        P.v(lambda e: e.reciprocal(c.cols[:, 37:38], c.cols[:, 36:37]), drain=True)
        P.v(lambda e: e.tensor_scalar(c.cols[:, 37:38], c.cols[:, 37:38], 127.0, None, ALU.mult),
            drain=True)
        P.v(lambda e: e.tensor_mul(c.cols[:, 38:39], c.cols[:, 37:38], c.cols[:, 34:35]), drain=True)
        e_nq = P.v(lambda e: e.tensor_mul(c.cols[:, 39:40], c.cols[:, 31:32], c.cols[:, 38:39]),
                   drain=True)
        P.v(lambda e: e.tensor_mul(c.cols[:, 40:41], c.cols[:, 37:38], c.swg[:]),
            w=[sw_events["wgT"]], drain=True)
        P.v(lambda e, tt=tt: e.reciprocal(c.r1g[:, tt:tt + 1], c.cols[:, 40:41]), drain=True)
        P.v(lambda e: e.tensor_mul(c.cols[:, 40:41], c.cols[:, 37:38], c.swu[:]),
            w=[sw_events["wuT"]], drain=True)
        P.v(lambda e, tt=tt: e.reciprocal(c.r1u[:, tt:tt + 1], c.cols[:, 40:41]), drain=True)
        e_dq = P.a(lambda e, x1t=x1t, d2=d2: e.activation(d2, x1t, AF.Identity, bias=c.cols[:, 39:40],
                                                          scale=c.cols[:, 38:39]), w=[e_nq], drain=True)
        prev_x1rd[par] = (e_dq, e_sp)
        e_r1a = P.v(lambda e, d2=d2: e.tensor_scalar(d2, d2, c.Ccol[:], None, ALU.add), w=[e_dq])
        e_rnd = P.v(lambda e, d2=d2: e.tensor_scalar(d2, d2, c.nCcol[:], None, ALU.add), w=[e_r1a])
        tr = []
        e_tr = None
        for fc in range(8):
            tslot = fc % 2
            w_tr = [e_rnd, e_id]
            if fc >= 2:
                w_tr.append(tr[fc - 2])
            elif tt >= 1:
                w_tr.append(c1_evs[tt - 1])
            e_tr = P.t(lambda e, fc=fc, tslot=tslot, d2=d2: e.transpose(
                bank(c, tslot)[:, 0:128], d2[:, fc * 128:(fc + 1) * 128], c.ident[:]), w=w_tr)
            tr.append(P.a(lambda e, fc=fc, tslot=tslot, tt=tt: e.copy(
                bfA[:, c.o_c1T + fc * 2048 + tt * 128: c.o_c1T + fc * 2048 + tt * 128 + 128],
                bank(c, tslot)[:, 0:128]), w=[e_tr]))
        c1_evs.append(tr[-1])
    e_c1T = c1_evs[-1]
    if DEBUG:
        P.dma(lambda e: e.dma_start(out=io["dbg_c1T"], in_=bfA[:, c.o_c1T:c.o_c1T + 16384]),
              w=[e_c1T], sem="dbg")


    out_dmas = []
    prev_icc = {}
    prev_dstream = {}
    prev_eo = {}
    e_prev_downevac = None
    for ss in range(NT_OWN // SS):
        tts = list(range(ss * SS, (ss + 1) * SS))
        gu_evs = {tt: [] for tt in tts}
        prev_gu_gate = {}
        prev_up = None
        for icc in range(8):
            par = icc % 2
            iccb = bfA[:, c.o_icc + par * 4096: c.o_icc + par * 4096 + 4096]
            w_l0 = [prev_icc.get(par, e_c1T)]
            lg = []
            for fc in range(8):
                lg.append(P.dma(lambda e, fc=fc, icc=icc, iccb=iccb: e.dma_start(
                    out=iccb[:, fc * 512:(fc + 1) * 512],
                    in_=scr["cg"][fc][:, icc * 512:(icc + 1) * 512]),
                    w=(w_l0 if fc == 0 else []) + [p2_ev["cg"][fc]], sem="di%d" % (icc % 2)))
            ev_gate = {}
            for i, tt in enumerate(tts):
                gslot = i % 2
                e_mm = None
                for fc in range(8):
                    w_mm = []
                    if fc == 0:
                        w_mm = lg + [e_c1T]
                        if e_prev_downevac is not None:
                            w_mm.append(e_prev_downevac)
                        if i >= 2:
                            w_mm.append(ev_gate[tts[i - 2]])
                    e_mm = P.t(lambda e, fc=fc, tt=tt, gslot=gslot, iccb=iccb: e.matmul(
                        bank(c, gslot),
                        bfA[:, c.o_c1T + fc * 2048 + tt * 128: c.o_c1T + fc * 2048 + tt * 128 + 128],
                        iccb[:, fc * 512:(fc + 1) * 512], start=(fc == 0), stop=(fc == 7)), w=w_mm)
                w_g = [e_mm]
                if tt in prev_gu_gate:
                    w_g.append(prev_gu_gate[tt])
                ev_gate[tt] = P.a(lambda e, tt=tt, i=i, gslot=gslot: e.activation(
                    f32A[:, c.o_gates + i * 512: c.o_gates + i * 512 + 512], bank(c, gslot),
                    AF.Silu, bias=0.0, scale=c.r1g[:, tt:tt + 1]), w=w_g)
            lu = []
            for fc in range(8):
                lu.append(P.dma(lambda e, fc=fc, icc=icc, iccb=iccb: e.dma_start(
                    out=iccb[:, fc * 512:(fc + 1) * 512],
                    in_=scr["cu"][fc][:, icc * 512:(icc + 1) * 512]),
                    w=(list(ev_gate.values()) if fc == 0 else []) + [p2_ev["cu"][fc]], sem="di%d" % (icc % 2)))
            for i, tt in enumerate(tts):
                uslot = 2 + i % 2
                e_mm = None
                for fc in range(8):
                    w_mm = []
                    if fc == 0:
                        w_mm = lu
                        if i >= 2:
                            w_mm.append(gu_evs[tts[i - 2]][-1])
                    e_mm = P.t(lambda e, fc=fc, tt=tt, uslot=uslot, iccb=iccb: e.matmul(
                        bank(c, uslot),
                        bfA[:, c.o_c1T + fc * 2048 + tt * 128: c.o_c1T + fc * 2048 + tt * 128 + 128],
                        iccb[:, fc * 512:(fc + 1) * 512], start=(fc == 0), stop=(fc == 7)), w=w_mm)
                w_u = [e_mm]
                if prev_up is not None:
                    w_u.append(prev_up)
                e_u = P.a(lambda e, tt=tt, uslot=uslot: e.activation(
                    f32A[:, c.o_ups: c.o_ups + 512], bank(c, uslot),
                    AF.Copy, bias=0.0, scale=c.r1u[:, tt:tt + 1]), w=w_u)
                e_gu = P.v(lambda e, tt=tt, icc=icc, i=i: e.tensor_tensor(
                    f32A[:, c.o_gu + i * 4096 + icc * 512: c.o_gu + i * 4096 + icc * 512 + 512],
                    f32A[:, c.o_gates + i * 512: c.o_gates + i * 512 + 512],
                    f32A[:, c.o_ups: c.o_ups + 512], ALU.mult), w=[e_u, ev_gate[tt]])
                prev_up = e_gu
                prev_gu_gate[tt] = e_gu
                gu_evs[tt].append(e_gu)
            prev_icc[par] = gu_evs[tts[-1]][-1]
        c2_evs = {}
        for i, tt in enumerate(tts):
            gut = f32A[:, c.o_gu + i * 4096: c.o_gu + i * 4096 + 4096]
            w_dbgu = []
            if DEBUG and tt == 0:
                w_dbgu.append(P.dma(lambda e, gut=gut: e.dma_start(out=io["dbg_gu"], in_=gut),
                                    w=gu_evs[tt], sem="dbg"))
            P.v(lambda e, gut=gut: e.reduce_max(c.cols[:, 44:45], gut, AX.X,
                                                apply_absolute_value=True),
                w=gu_evs[tt] + w_dbgu, drain=True)
            P.v(lambda e: e.tensor_scalar_max(c.cols[:, 44:45], c.cols[:, 44:45], Q_EPS), drain=True)
            P.v(lambda e: e.reciprocal(c.cols[:, 45:46], c.cols[:, 44:45]), drain=True)
            e_s2 = P.v(lambda e: e.tensor_scalar(c.cols[:, 45:46], c.cols[:, 45:46], 127.0, None,
                                                 ALU.mult), drain=True)
            P.v(lambda e: e.tensor_mul(c.cols[:, 46:47], c.cols[:, 45:46], c.swd[:]),
                w=[sw_events["wdT"]], drain=True)
            e_rd = P.v(lambda e, tt=tt: e.reciprocal(c.rdc[:, tt:tt + 1], c.cols[:, 46:47]),
                       drain=True)
            e_c2a = P.v(lambda e, gut=gut: e.tensor_scalar(gut, gut, c.cols[:, 45:46], None, ALU.mult),
                        w=[e_s2])
            e_c2b = P.v(lambda e, gut=gut: e.tensor_scalar(gut, gut, c.Ccol[:], None, ALU.add),
                        w=[e_c2a])
            e_rnd = P.v(lambda e, gut=gut: e.tensor_scalar(gut, gut, c.nCcol[:], None, ALU.add),
                        w=[e_c2b])
            tr = []
            for ic in range(32):
                tslot = 4 + ic % 2
                w_tr = [e_rnd, e_id]
                if ic >= 2:
                    w_tr.append(tr[ic - 2])
                elif i >= 1:
                    w_tr.append(c2_evs[tts[i - 1]])
                elif e_prev_downevac is not None:
                    w_tr.append(e_prev_downevac)
                e_tr = P.t(lambda e, ic=ic, tslot=tslot, gut=gut: e.transpose(
                    bank(c, tslot)[:, 0:128], gut[:, ic * 128:(ic + 1) * 128], c.ident[:]), w=w_tr)
                tr.append(P.a(lambda e, ic=ic, i=i, tslot=tslot: e.copy(
                    bfA[:, c.o_c2T + i * 4096 + ic * 128: c.o_c2T + i * 4096 + ic * 128 + 128],
                    bank(c, tslot)[:, 0:128]), w=[e_tr]))
            c2_evs[tt] = tr[-1]
        down_mm = {}
        for ic in range(32):
            par = ic % 2
            cdb = bfA[:, c.o_cdblk + par * 1024: c.o_cdblk + par * 1024 + 1024]
            w_l = [prev_dstream.get(par), p2_ev["cd"][ic]]
            e_cd = P.dma(lambda e, ic=ic, cdb=cdb: e.dma_start(out=cdb, in_=scr["cd"][ic]), w=w_l,
                         sem="dd%d" % (ic % 2))
            for foc in range(2):
                for i, tt in enumerate(tts):
                    dslot = i * 2 + foc
                    w_mm = [e_cd]
                    if ic == 0:
                        w_mm += [c2_evs[tts[-1]]]
                    down_mm[(ic, foc, i)] = P.t(lambda e, ic=ic, foc=foc, i=i, dslot=dslot,
                                                cdb=cdb: e.matmul(
                        bank(c, dslot),
                        bfA[:, c.o_c2T + i * 4096 + ic * 128: c.o_c2T + i * 4096 + ic * 128 + 128],
                        cdb[:, foc * 512:(foc + 1) * 512], start=(ic == 0), stop=(ic == 31)),
                        w=w_mm)
            prev_dstream[par] = down_mm[(ic, 1, SS - 1)]
        e_last_eo = None
        for i, tt in enumerate(tts):
            x1r = f32A[:, c.o_x1rb: c.o_x1rb + 1024]
            w_xr = [x1_spills[-1], x1_spills[-2]]
            if out_dmas:
                w_xr.append(out_dmas[-1])
            e_xr = P.dma(lambda e, tt=tt, x1r=x1r: e.dma_start(out=x1r, in_=scr["x1"][tt]), w=w_xr,
                         sem="dr")
            for foc in range(2):
                dslot = i * 2 + foc
                w_xo = [down_mm[(31, foc, i)], x1_spills[-1], x1_spills[-2]]
                if (foc,) in prev_eo:
                    w_xo.append(prev_eo[(foc,)])
                e_xo = P.a(lambda e, dslot=dslot, tt=tt, foc=foc: e.activation(
                    c.x1stg[:, foc * 512: foc * 512 + 512], bank(c, dslot),
                    AF.Copy, bias=0.0, scale=c.rdc[:, tt:tt + 1]), w=w_xo)
                w_o = [e_xo, e_xr]
                if len(out_dmas) >= 2:
                    w_o.append(out_dmas[-2])
                e_o = P.v(lambda e, foc=foc: e.tensor_add(
                    c.x1stg[:, 1024 + foc * 512: 1024 + foc * 512 + 512],
                    c.x1stg[:, foc * 512: foc * 512 + 512],
                    f32A[:, c.o_x1rb + foc * 512: c.o_x1rb + foc * 512 + 512]), w=w_o)
                prev_eo[(foc,)] = e_o
                out_dmas.append(P.dma(lambda e, tt=tt, foc=foc: e.dma_start(
                    out=out[tt * 128:(tt + 1) * 128, foc * 512:(foc + 1) * 512],
                    in_=c.x1stg[:, 1024 + foc * 512: 1024 + foc * 512 + 512]), w=[e_o],
                    sem="do%d" % foc))
                e_last_eo = e_o
        e_prev_downevac = e_last_eo

    P.items["sync"].append((None, tuple((sname, tot) for sname, tot in P.cnt.items()), None, 0, False))
    return P


def _fix_dma_waits(P):
    for ring_sem, ring_eng in (("ds", "sync"), ("dg", "gpsimd")):
        bounds = []
        tot = 0
        for fn, waits, sem, amt, drain in P.items[ring_eng]:
            if waits:
                bounds.append(tot)
            if sem == ring_sem:
                tot += amt
        bounds.append(tot)
        import bisect
        def remap(v, bounds=bounds):
            i = bisect.bisect_left(bounds, v)
            return bounds[i] if i < len(bounds) else v
        for eng in ENGINES:
            new = []
            for fn, waits, sem, amt, drain in P.items[eng]:
                nw = tuple((s2, remap(v) if s2 == ring_sem else v) for (s2, v) in waits)
                new.append((fn, nw, sem, amt, drain))
            P.items[eng] = new


def build():
    nc = bass.Bass("TRN2", target_bir_lowering=False)
    io = {}
    io["x"] = nc.dram_tensor("x", [T, D], F32, kind="ExternalInput").ap()
    wT = {}
    for wn, shp in (("wqT", [D, D]), ("wkT", [D, D]), ("wq2T", [D, D]), ("wk2T", [D, D]),
                    ("wvT", [D, D]), ("wprojT", [D, D]), ("wgT", [D, INNER]), ("wuT", [D, INNER]),
                    ("wdT", [INNER, D])):
        wT[wn] = nc.dram_tensor(wn, shp, F32, kind="ExternalInput").ap()
    io["wT"] = wT
    io["consts"] = nc.dram_tensor("consts", [128, 4], F32, kind="ExternalInput").ap()
    io["onesrow"] = nc.dram_tensor("onesrow", [1, 128], F32, kind="ExternalInput").ap()
    io["trimask"] = nc.dram_tensor("trimask", [128, 128], BF16, kind="ExternalInput").ap()
    io["pswap"] = nc.dram_tensor("pswap", [128, 128], BF16, kind="ExternalInput").ap()
    io["ident"] = nc.dram_tensor("ident", [128, 128], F32, kind="ExternalInput").ap()
    io["flagin"] = nc.dram_tensor("flag", [1, 1], I32, kind="ExternalInput").ap()
    io["ropetab"] = nc.dram_tensor("ropetab", [128, 8192], F32, kind="ExternalInput").ap()
    io["out"] = nc.dram_tensor("out", [N_OWN, D], F32, kind="ExternalOutput").ap()
    kind = "ExternalOutput"
    scr = {}
    scr["z"] = [nc.dram_tensor(f"scz{i}", [128, 2048], BF16, kind=kind).ap() for i in range(NPAIR)]
    scr["cg"] = [nc.dram_tensor(f"scg{i}", [128, INNER], BF16, kind=kind).ap() for i in range(8)]
    scr["cu"] = [nc.dram_tensor(f"scu{i}", [128, INNER], BF16, kind=kind).ap() for i in range(8)]
    scr["cd"] = [nc.dram_tensor(f"scd{i}", [128, D], BF16, kind=kind).ap() for i in range(32)]
    scr["x1"] = [nc.dram_tensor(f"sx1{i}", [128, D], F32, kind=kind).ap() for i in range(NT_OWN)]
    io["scr"] = scr
    if DEBUG:
        io["dbg_tab"] = nc.dram_tensor("dbg_tab", [128, 8192], F32, kind="ExternalOutput").ap()
        io["dbg_hnT"] = nc.dram_tensor("dbg_hnT", [128, 32768], BF16, kind="ExternalOutput").ap()
        io["dbg_p0"] = nc.dram_tensor("dbg_p0", [128, 16384], BF16, kind="ExternalOutput").ap()
        io["dbg_c1T"] = nc.dram_tensor("dbg_c1T", [128, 16384], BF16, kind="ExternalOutput").ap()
        io["dbg_gu"] = nc.dram_tensor("dbg_gu", [128, 4096], F32, kind="ExternalOutput").ap()
        io["dbg_cols"] = nc.dram_tensor("dbg_cols", [128, 41], F32, kind="ExternalOutput").ap()

    with ExitStack() as stack:
        c = alloc_tensors(nc, stack)
        progs = [build_half(nc, c, Prog(), h, io) for h in (0, 1)]
        sem_names = sorted(set(list(progs[0].cnt) + list(progs[1].cnt))) + ["fl"]
        sems = {s: stack.enter_context(nc.semaphore(s)) for s in sem_names}
        block = stack.enter_context(nc.Block())

        def replay(eng, items):
            waited = {}
            self_cnt = {}
            for fn, waits, sem, amt, drain in items:
                need = {}
                for (s, v) in waits:
                    if v > 0 and v > waited.get(s, 0):
                        need[s] = max(need.get(s, 0), v)
                for s, v in need.items():
                    eng.wait_ge(sems[s], v)
                    waited[s] = v
                if fn is None:
                    continue
                inst = fn(eng)
                if sem is not None:
                    inst.then_inc(sems[sem], amt)
                if drain and sem is not None:
                    eng.wait_ge(sems[sem], self_cnt.get(sem, 0) + amt)
                if sem is not None:
                    self_cnt[sem] = self_cnt.get(sem, 0) + amt
                    waited.setdefault(sem, 0)
                    if drain:
                        waited[sem] = max(waited[sem], self_cnt[sem])

        def section(eng_name):
            def body(eng):
                if eng_name == "sync":
                    eng.dma_start(out=c.flag[:], in_=io["flagin"]).then_inc(sems["fl"], 16)
                eng.wait_ge(sems["fl"], 16)
                with eng.register("rf_" + eng_name) as rf:
                    eng.load(rf, c.flag[0:1, 0:1])
                    with eng.If_eq(rf, 0):
                        replay(eng, progs[0].items[eng_name])
                    with eng.If_eq(rf, 1):
                        replay(eng, progs[1].items[eng_name])
            return body

        block.sync(section("sync"))
        block.tensor(section("tensor"))
        block.vector(section("vector"))
        block.scalar(section("scalar"))
        block.gpsimd(section("gpsimd"))
    return nc


def _rope_tables_np():
    try:
        import jax
        import jax.numpy as jnp
        cpu = jax.devices("cpu")[0]
        with jax.default_device(cpu):
            inv_freq = 1.0 / 10000.0 ** (jnp.arange(0, 64, 2, dtype=jnp.float32) / 64)
            t = jnp.arange(T, dtype=jnp.float32)
            freqs = jnp.outer(t, inv_freq)
            cos = np.asarray(jnp.cos(freqs), dtype=np.float32)
            sin = np.asarray(jnp.sin(freqs), dtype=np.float32)
    except Exception:
        e = np.arange(0, 64, 2, dtype=np.float32) / np.float32(64)
        inv = (1.0 / (10000.0 ** e.astype(np.float64))).astype(np.float32)
        freqs = np.outer(np.arange(T, dtype=np.float32), inv).astype(np.float32)
        cos, sin = np.cos(freqs).astype(np.float32), np.sin(freqs).astype(np.float32)
    r = np.arange(128)
    sgn = np.where(r % 64 < 32, 1.0, -1.0).astype(np.float32)
    cosT = cos.T[r % 32]
    sinS = sin.T[r % 32] * sgn[:, None]
    return np.ascontiguousarray(np.concatenate([cosT, sinS], axis=1))


def _consts_np():
    r = np.arange(128)
    consts = np.stack([np.ones(128), np.full(128, 2.0 ** -30), (r % 32).astype(float),
                       np.where(r % 64 < 32, 1.0, -1.0)], axis=1).astype(np.float32)
    onesrow = np.ones((1, 128), np.float32)
    tri = np.triu(np.ones((128, 128))).astype(ml_dtypes.bfloat16)
    psw = np.zeros((128, 128), ml_dtypes.bfloat16)
    for j in range(128):
        psw[j ^ 32, j] = 1.0
    ident = np.eye(128, dtype=np.float32)
    return consts, onesrow, tri, psw, ident


def kernel(L0_attn_x, L0_attn_Wq, L0_attn_Wk, L0_attn_Wq2, L0_attn_Wk2, L0_attn_Wv,
           L0_attn_Wproj, L0_ffn_w_gate, L0_ffn_w_up, L0_ffn_w_down):
    x = np.ascontiguousarray(np.asarray(L0_attn_x, dtype=np.float32))
    wTs = {
        "wqT": L0_attn_Wq, "wkT": L0_attn_Wk, "wq2T": L0_attn_Wq2, "wk2T": L0_attn_Wk2,
        "wvT": L0_attn_Wv, "wprojT": L0_attn_Wproj, "wgT": L0_ffn_w_gate,
        "wuT": L0_ffn_w_up, "wdT": L0_ffn_w_down,
    }
    wTs = {k: np.ascontiguousarray(np.asarray(v, dtype=np.float32).T) for k, v in wTs.items()}
    consts, onesrow, tri, psw, ident = _consts_np()
    ropetab = _rope_tables_np()

    nc = build()
    in_maps = []
    for core in range(8):
        b, half = core // 2, core % 2
        m = {"x": np.ascontiguousarray(x[b]),
             "consts": consts, "onesrow": onesrow, "trimask": tri, "pswap": psw,
             "ident": ident, "flag": np.full((1, 1), half, np.int32), "ropetab": ropetab}
        m.update(wTs)
        in_maps.append(m)
    res = run_bass_kernel_spmd(nc, in_maps, core_ids=list(range(8)))
    full = np.empty((4, T, D), np.float32)
    for core in range(8):
        b, half = core // 2, core % 2
        o = res.results[core]["out"]
        if half == 0:
            full[b, 0:1024] = o[0:1024]
            full[b, 3072:4096] = o[1024:2048]
        else:
            full[b, 1024:3072] = o
    return full


# version 10
# speedup vs baseline: 1.1295x; 1.1069x over previous
import numpy as np
import ml_dtypes
from contextlib import ExitStack

import concourse.bass as bass
import concourse.mybir as mybir
from concourse.bass_utils import run_bass_kernel_spmd

F32 = mybir.dt.float32
BF16 = mybir.dt.bfloat16
I32 = mybir.dt.int32
ALU = mybir.AluOpType
AF = mybir.ActivationFunctionType
AX = mybir.AxisListType

D = 1024
T = 4096
INNER = 4096
NPAIR = 8
N_OWN = 2048
NT_OWN = 16
SS = 4
LN_EPS = 1e-5
Q_EPS = 1e-5
RNDC = 1.5 * 2.0 ** 23
RNDK = 1.5 * 2.0 ** 52 - 1.5 * 2.0 ** 23

ENGINES = ("sync", "tensor", "vector", "scalar", "gpsimd")
DEBUG = False


def own_chunks(half):
    return [0, 1, 6, 7] if half == 0 else [2, 3, 4, 5]


def kv_tiles(half):
    return 32 if half == 0 else 24


class Prog:
    def __init__(self):
        self.items = {e: [] for e in ENGINES}
        self.cnt = {}

    def ev(self, sem):
        return (sem, self.cnt.get(sem, 0))

    def _add(self, eng, fn, waits, sem, amt, drain):
        self.cnt[sem] = self.cnt.get(sem, 0) + amt
        evt = (sem, self.cnt[sem])
        self.items[eng].append((fn, tuple(w for w in waits if w is not None), sem, amt, drain))
        return evt

    def dma(self, fn, w=(), sem="ds"):
        return self._add("sync", fn, w, sem, 16, False)

    def dmag(self, fn, w=(), sem="dg"):
        return self._add("gpsimd", fn, w, sem, 16, False)

    def t(self, fn, w=(), drain=False):
        return self._add("tensor", fn, w, "t", 1, drain)

    def v(self, fn, w=(), drain=False):
        return self._add("vector", fn, w, "v", 1, drain)

    def a(self, fn, w=(), drain=False):
        return self._add("scalar", fn, w, "a", 1, drain)

    def g(self, fn, w=(), drain=False):
        return self._add("gpsimd", fn, w, "g", 1, drain)

    def barrier(self, eng):
        if eng == "vector":
            return self.v(lambda e: e.engine_nop())
        if eng == "scalar":
            return self.a(lambda e: e.engine_nop())
        raise ValueError(eng)


class Ctx:
    pass


def alloc_tensors(nc, stack):
    c = Ctx()
    sb = lambda n, s, d: stack.enter_context(nc.sbuf_tensor(n, s, d))
    c.bfA = sb("bfA", [128, 60416], BF16)
    c.f32A = sb("f32A", [128, 19968], F32)
    c.ps = stack.enter_context(nc.psum_tensor("ps", [128, 4096], F32))

    c.o_hnT = 0
    c.o_KmA = 32768
    c.o_KmB = 36864
    c.o_Vtok = 40960
    c.o_QmA = 45056
    c.o_QmB = 47104
    c.o_WTqA = 49152
    c.o_WTqB = 50176
    c.o_WTkA = 51200
    c.o_WTkB = 52224
    c.o_WTv = 53248
    c.o_Xb = 54272
    c.o_Pbuf = 55296
    c.o_zbuf = 57344
    c.o_zfull = 0
    c.o_WprojT = 16384
    c.o_c1T = 24576
    c.o_icc = 0
    c.o_iccu = 40960
    c.o_c2T = 8192
    c.o_cdblk = 49152

    c.o_cosT = 0
    c.o_sinS = 4096
    c.o_xstg = 8192
    c.o_wstg = 10240
    c.o_d2 = 14336
    c.o_Xf = 15360
    c.o_tA = 16384
    c.o_s2sb = 17408
    c.o_gu = 0
    c.o_x1rb = 16384
    c.o_gates = 17408
    c.o_ups = 19456

    c.cols = sb("cols", [128, 64], F32)
    c.c2s = sb("c2s", [128, 16], F32)
    c.r1g = sb("r1g", [128, 16], F32)
    c.r1u = sb("r1u", [128, 16], F32)
    c.rdc = sb("rdc", [128, 16], F32)
    c.Ccol = sb("Ccol", [128, 1], F32)
    c.nCcol = sb("nCcol", [128, 1], F32)
    c.swg = sb("swg", [128, 1], F32)
    c.swu = sb("swu", [128, 1], F32)
    c.swd = sb("swd", [128, 1], F32)
    c.acc = sb("acc", [128, 80], F32)
    c.consts = sb("constsb", [128, 4], F32)
    c.onesrow = sb("onesrowb", [1, 128], F32)
    c.trimask = sb("trimaskb", [128, 128], BF16)
    c.pswap = sb("pswapb", [128, 128], BF16)
    c.ident = sb("identb", [128, 128], F32)
    c.flag = sb("flagsb", [1, 1], I32)
    c.x1stg = sb("x1stg", [128, 2048], F32)
    c.ssc = sb("ssc", [1, 1], F32)
    return c


def bank(c, i):
    return c.ps[:, i * 512:(i + 1) * 512]


def build_half(nc, c, P, half, io):
    OC = own_chunks(half)
    KT = kv_tiles(half)
    bfA, f32A, ps = c.bfA, c.f32A, c.ps
    x, out = io["x"], io["out"]
    wT = io["wT"]
    scr = io["scr"]

    ones = c.consts[:, 0:1]
    epsv = c.consts[:, 1:2]
    jcol = c.consts[:, 2:3]
    sgn = c.consts[:, 3:4]
    cosT = f32A[:, c.o_cosT:c.o_cosT + 4096]
    sinS = f32A[:, c.o_sinS:c.o_sinS + 4096]

    P.dma(lambda e: e.dma_start(out=c.consts[:, 0:4], in_=io["consts"]), sem="dc")
    P.dma(lambda e: e.dma_start(out=c.onesrow[:], in_=io["onesrow"]), sem="dc")
    P.dma(lambda e: e.dma_start(out=c.trimask[:], in_=io["trimask"]), sem="dc")
    P.dma(lambda e: e.dma_start(out=c.pswap[:], in_=io["pswap"]), sem="dc")
    P.dma(lambda e: e.dma_start(out=c.ident[:], in_=io["ident"]), sem="dc")
    e_consts = e_or = e_tri = e_psw = e_id = P.dma(
        lambda e: e.dma_start(out=f32A[:, 0:8192], in_=io["ropetab"]), sem="dc")

    P.v(lambda e: e.tensor_add(c.cols[:, 0:1], ones, epsv), w=[e_consts], drain=True)
    P.v(lambda e: e.tensor_sub(c.cols[:, 0:1], c.cols[:, 0:1], ones), drain=True)
    P.v(lambda e: e.tensor_scalar(c.cols[:, 0:1], c.cols[:, 0:1], 2.0 ** 30, None, ALU.mult), drain=True)
    P.v(lambda e: e.tensor_scalar(c.Ccol[:], c.cols[:, 0:1], RNDK, RNDC, ALU.mult, ALU.add), drain=True)
    e_C = P.v(lambda e: e.tensor_scalar(c.nCcol[:], c.Ccol[:], -1.0, None, ALU.mult), drain=True)

    e_tab = e_consts

    slot0 = {"wgT": 0, "wuT": 16, "wdT": 32}
    ccols = {"wgT": INNER, "wuT": INNER, "wdT": D}
    p1_units = ([("wgT", ck, hh) for ck in range(8) for hh in (0, 2048)]
                + [("wuT", ck, hh) for ck in range(8) for hh in (0, 2048)]
                + [("wdT", ck, 0) for ck in range(32)])
    p1_state = {"u": 0, "prev": {0: None, 1: None}, "last": None}

    def emit_p1_unit():
        if p1_state["u"] >= len(p1_units):
            return
        u = p1_state["u"]
        p1_state["u"] += 1
        wname, ck, hh = p1_units[u]
        hw = min(2048, ccols[wname] - hh)
        par = u % 2
        s = slot0[wname] + (ck * 2 + hh // 2048 if wname != "wdT" else ck)
        wst = f32A[:, c.o_wstg + par * 2048: c.o_wstg + par * 2048 + hw]
        w_l = [p1_state["prev"][par]] if p1_state["prev"][par] else []
        e_l = P.dma(lambda e, wname=wname, ck=ck, hh=hh, hw=hw, wst=wst: e.dma_start(
            out=wst, in_=wT[wname][ck * 128:(ck + 1) * 128, hh:hh + hw]), w=w_l,
            sem="dw%d" % par)
        e_r = P.v(lambda e, wst=wst, s=s: e.reduce_sum(
            c.acc[:, s:s + 1], wst, AX.X, apply_absolute_value=True), w=[e_l])
        p1_state["prev"][par] = e_r
        p1_state["last"] = e_r

    ln_done = []
    for gt in range(KT):
        par = gt % 2
        xst = f32A[:, c.o_xstg + par * 1024: c.o_xstg + (par + 1) * 1024]
        d2 = f32A[:, c.o_d2:c.o_d2 + 1024]
        w_x = [ln_done[gt - 2]] if gt >= 2 else []
        e_x = P.dma(lambda e, gt=gt, xst=xst: e.dma_start(out=xst, in_=x[gt * 128:(gt + 1) * 128, :]),
                    w=w_x, sem="dx%d" % (gt % 2))
        emit_p1_unit()
        emit_p1_unit()
        w_cols = [ln_done[gt - 1]] if gt >= 1 else [e_C]
        P.v(lambda e, xst=xst: e.reduce_sum(c.cols[:, 16:17], xst, AX.X), w=[e_x] + w_cols, drain=True)
        e_nm = P.v(lambda e: e.tensor_scalar(c.cols[:, 17:18], c.cols[:, 16:17], -1.0 / D, None, ALU.mult),
                   drain=True)
        e_d2 = P.a(lambda e, xst=xst, d2=d2: e.activation(d2, xst, AF.Square, bias=c.cols[:, 17:18],
                                                          scale=1.0), w=[e_nm], drain=True)
        P.v(lambda e, d2=d2: e.reduce_sum(c.cols[:, 18:19], d2, AX.X), w=[e_d2], drain=True)
        e_vp = P.v(lambda e: e.tensor_scalar(c.cols[:, 18:19], c.cols[:, 18:19], 1.0 / D, LN_EPS,
                                             ALU.mult, ALU.add), drain=True)
        e_sq = P.a(lambda e: e.activation(c.cols[:, 19:20], c.cols[:, 18:19], AF.Sqrt, bias=0.0,
                                          scale=1.0), w=[e_vp], drain=True)
        P.v(lambda e: e.reciprocal(c.cols[:, 20:21], c.cols[:, 19:20]), w=[e_sq], drain=True)
        for _ in range(2):
            P.v(lambda e: e.tensor_mul(c.cols[:, 21:22], c.cols[:, 20:21], c.cols[:, 20:21]), drain=True)
            P.v(lambda e: e.tensor_mul(c.cols[:, 21:22], c.cols[:, 21:22], c.cols[:, 18:19]), drain=True)
            P.v(lambda e: e.tensor_scalar(c.cols[:, 21:22], c.cols[:, 21:22], -0.5, 1.5, ALU.mult,
                                          ALU.add), drain=True)
            P.v(lambda e: e.tensor_mul(c.cols[:, 20:21], c.cols[:, 20:21], c.cols[:, 21:22]), drain=True)
        e_nmr = P.v(lambda e: e.tensor_mul(c.cols[:, 22:23], c.cols[:, 17:18], c.cols[:, 20:21]),
                    drain=True)
        e_hn = P.a(lambda e, xst=xst, d2=d2: e.activation(d2, xst, AF.Identity, bias=c.cols[:, 22:23],
                                                          scale=c.cols[:, 20:21]), w=[e_nmr], drain=True)
        evs = []
        for fc in range(8):
            slot = fc % 2
            w_tr = [e_hn, e_id]
            if fc >= 2:
                w_tr.append(evs[fc - 2])
            elif gt >= 1:
                w_tr.append(ln_done[gt - 1])
            e_tr = P.t(lambda e, fc=fc, slot=slot, d2=d2: e.transpose(
                bank(c, slot)[:, 0:128], d2[:, fc * 128:(fc + 1) * 128], c.ident[:]), w=w_tr)
            e_ev = P.a(lambda e, fc=fc, slot=slot, gt=gt: e.copy(
                bfA[:, c.o_hnT + fc * 4096 + gt * 128: c.o_hnT + fc * 4096 + gt * 128 + 128],
                bank(c, slot)[:, 0:128]), w=[e_tr])
            evs.append(e_ev)
        ln_done.append(evs[-1])
    e_hnT = ln_done[-1]

    if DEBUG:
        P.dma(lambda e: e.dma_start(out=io["dbg_tab"], in_=f32A[:, 0:8192]), w=[e_tab], sem="dbg")
        P.dma(lambda e: e.dma_start(out=io["dbg_hnT"], in_=bfA[:, 0:32768]), w=[e_hnT], sem="dbg")

    while p1_state["u"] < len(p1_units):
        emit_p1_unit()
    e_p1 = p1_state["last"]

    sw_events = {}

    def emit_sw_totals():
        for wname, swcol, nelem in (("wgT", c.swg, INNER * D), ("wuT", c.swu, INNER * D),
                                    ("wdT", c.swd, D * INNER)):
            s0, nck = slot0[wname], {"wgT": 16, "wuT": 16, "wdT": 32}[wname]
            e_r = P.v(lambda e, s0=s0, nck=nck: e.reduce_sum(c.cols[:, 24:25],
                                                             c.acc[:, s0:s0 + nck], AX.X),
                      w=[e_p1], drain=True)
            e_m1 = P.t(lambda e: e.matmul(ps[0:1, 1024:1025], c.cols[:, 24:25], ones,
                                          start=True, stop=True), w=[e_r, e_consts, P.ev("v")])
            e_e1 = P.a(lambda e: e.copy(c.ssc[:], ps[0:1, 1024:1025]), w=[e_m1], drain=True)
            e_m2 = P.t(lambda e: e.matmul(ps[:, 1025:1026], c.onesrow[:], c.ssc[:],
                                          start=True, stop=True), w=[e_e1, e_or])
            e_e2 = P.a(lambda e: e.copy(c.cols[:, 25:26], ps[:, 1025:1026]), w=[e_m2], drain=True)
            P.v(lambda e, nelem=nelem: e.tensor_scalar(c.cols[:, 26:27], c.cols[:, 25:26],
                                                       1.0 / nelem, None, ALU.mult), w=[e_e2], drain=True)
            P.v(lambda e: e.tensor_scalar_max(c.cols[:, 26:27], c.cols[:, 26:27], Q_EPS), drain=True)
            sw_events[wname] = P.v(lambda e, swcol=swcol: e.reciprocal(swcol[:], c.cols[:, 26:27]),
                                   drain=True)

    p2_ev = {"cg": [None] * 8, "cu": [None] * 8, "cd": [None] * 32}
    _p2 = {"u": 0, "prevst": {0: e_p1, 1: e_p1}}
    p2_units = ([("wgT", ck, hh) for ck in range(8) for hh in (0, 2048)]
                + [("wuT", ck, hh) for ck in range(8) for hh in (0, 2048)]
                + [("wdT", ck, 0) for ck in range(32)])

    def emit_p2_unit():
        if _p2["u"] >= len(p2_units):
            return
        u = _p2["u"]
        _p2["u"] += 1
        wname, ck, hh = p2_units[u]
        hw = min(2048, ccols[wname] - hh)
        par = u % 2
        swcol = {"wgT": c.swg, "wuT": c.swu, "wdT": c.swd}[wname]
        scrkey = {"wgT": "cg", "wuT": "cu", "wdT": "cd"}[wname]
        wst = f32A[:, c.o_wstg + par * 2048: c.o_wstg + par * 2048 + hw]
        e_l = P.dma(lambda e, wname=wname, ck=ck, hh=hh, hw=hw, wst=wst: e.dma_start(
            out=wst, in_=wT[wname][ck * 128:(ck + 1) * 128, hh:hh + hw]),
            w=[_p2["prevst"][par]], sem="dw%d" % par)
        e_a1 = P.a(lambda e, wst=wst, swcol=swcol: e.activation(
            wst, wst, AF.Identity, bias=c.Ccol[:], scale=swcol[:]),
            w=[e_l, sw_events[wname], e_C])
        e_a2 = P.a(lambda e, wst=wst: e.activation(
            wst, wst, AF.Identity, bias=c.nCcol[:], scale=1.0), w=[e_a1])
        e_q = P.v(lambda e, wst=wst: e.tensor_scalar(wst, wst, -1.0, 1.0, ALU.max, ALU.min),
                  w=[e_a2])
        e_st = P.dmag(lambda e, wst=wst, scrkey=scrkey, ck=ck, hh=hh, hw=hw: e.dma_start(
            out=scr[scrkey][ck][:, hh:hh + hw], in_=wst), w=[e_q], sem="dq%d" % par)
        _p2["prevst"][par] = e_st
        p2_ev[scrkey][ck] = e_st

    rst = {}
    w_dbg_guard = []
    zspill = []
    zev_prev = {}
    zsp_ev = {}
    e_prev_pair = None
    e_wt_free = None
    for hp in range(NPAIR):
        wl = []
        w_guard = [e_wt_free] if e_wt_free is not None else []
        for (wa, wb, offA, offB) in (("wqT", "wq2T", c.o_WTqA, c.o_WTqB),
                                     ("wkT", "wk2T", c.o_WTkA, c.o_WTkB)):
            for sub, off in ((0, offA), (1, offB)):
                for wn, halfcol in ((wa, 0), (wb, 64)):
                    def f(e, wn=wn, sub=sub, off=off, halfcol=halfcol, hp=hp):
                        src = wT[wn].rearrange("(fc p) o -> p fc o", p=128)[
                            :, :, hp * 128 + sub * 64: hp * 128 + sub * 64 + 64]
                        dst = bfA[:, off:off + 1024].rearrange("p (fc z) -> p fc z", z=128)[
                            :, :, halfcol:halfcol + 64]
                        return e.dma_start(out=dst, in_=src)
                    wl.append(P.dmag(f, w=w_guard, sem="dwt"))
        def fv(e, hp=hp):
            src = wT["wvT"].rearrange("(fc p) o -> p fc o", p=128)[:, :, hp * 128:(hp + 1) * 128]
            dst = bfA[:, c.o_WTv:c.o_WTv + 1024].rearrange("p (fc z) -> p fc z", z=128)
            return e.dma_start(out=dst, in_=src)
        wl.append(P.dmag(fv, w=w_guard, sem="dwt"))
        P.v(lambda e: e.tensor_scalar(bfA[:, c.o_WTqA:c.o_WTqA + 1024],
                                      bfA[:, c.o_WTqA:c.o_WTqA + 1024], 1.0 / 64, None, ALU.mult),
            w=wl)
        e_wready = P.v(lambda e: e.tensor_scalar(bfA[:, c.o_WTqB:c.o_WTqB + 1024],
                                                 bfA[:, c.o_WTqB:c.o_WTqB + 1024], 1.0 / 64, None,
                                                 ALU.mult), w=wl)

        w_bufs = [e_prev_pair] if e_prev_pair is not None else []
        w_bufs += w_dbg_guard
        if e_prev_pair is None and hp > 0:
            w_bufs.extend(zspill[-1])

        vp_last = []
        for ch in range(KT // 4):
            par = ch % 2
            pslot = 5 + par
            e_mm = None
            for fc in range(8):
                w_mm = []
                if fc == 0:
                    w_mm = [e_hnT] + wl + [rst.get("ps%d" % par)]
                e_mm = P.t(lambda e, fc=fc, ch=ch, pslot=pslot: e.matmul(
                    bank(c, pslot), bfA[:, c.o_WTv + fc * 128:c.o_WTv + fc * 128 + 128],
                    bfA[:, c.o_hnT + fc * 4096 + ch * 512: c.o_hnT + fc * 4096 + ch * 512 + 512],
                    start=(fc == 0), stop=(fc == 7)), w=w_mm)
            e_ev = P.a(lambda e, pslot=pslot, ch=ch, par=par: e.copy(
                f32A[:, c.o_Xf + par * 512: c.o_Xf + par * 512 + 512], bank(c, pslot)),
                w=[e_mm, rst.get("xf%d" % par)])
            rst["ps%d" % par] = e_ev
            tr_ev = []
            e_tr = None
            for j in range(4):
                tslot = j % 2
                w_tr = [e_ev, e_id] + w_bufs
                if j >= 2:
                    w_tr.append(tr_ev[j - 2])
                elif ch >= 1:
                    w_tr.append(vp_last[ch - 1])
                e_tr = P.t(lambda e, j=j, ch=ch, tslot=tslot, par=par: e.transpose(
                    bank(c, tslot)[:, 0:128],
                    f32A[:, c.o_Xf + par * 512 + j * 128: c.o_Xf + par * 512 + j * 128 + 128],
                    c.ident[:]), w=w_tr)
                tr_ev.append(P.a(lambda e, j=j, ch=ch, tslot=tslot: e.copy(
                    bfA[:, c.o_Vtok + (ch * 4 + j) * 128: c.o_Vtok + (ch * 4 + j) * 128 + 128],
                    bank(c, tslot)[:, 0:128]), w=[e_tr]))
            rst["xf%d" % par] = e_tr
            vp_last.append(tr_ev[-1])
        e_vtok = vp_last[-1]

        def emit_rope(setoff, dstoff, colbases):
            evs = []
            for i, (dcb, hcb, tcb) in enumerate(colbases):
                par = i % 2
                pslot = 5 + par
                sslot = 7 if par == 0 else 4
                e_mm = None
                for fc in range(8):
                    w_mm = []
                    if fc == 0:
                        w_mm = [e_wready, e_hnT, rst.get("ps%d" % par)]
                    e_mm = P.t(lambda e, fc=fc, pslot=pslot, setoff=setoff, hcb=hcb: e.matmul(
                        bank(c, pslot), bfA[:, setoff + fc * 128: setoff + fc * 128 + 128],
                        bfA[:, c.o_hnT + fc * 4096 + hcb: c.o_hnT + fc * 4096 + hcb + 512],
                        start=(fc == 0), stop=(fc == 7)), w=w_mm)
                e_f = P.a(lambda e, par=par, pslot=pslot: e.copy(
                    f32A[:, c.o_Xf + par * 512: c.o_Xf + par * 512 + 512], bank(c, pslot)),
                    w=[e_mm, rst.get("xf%d" % par)])
                e_b = P.a(lambda e, par=par, pslot=pslot: e.copy(
                    bfA[:, c.o_Xb + par * 512: c.o_Xb + par * 512 + 512], bank(c, pslot)),
                    w=[rst.get("xb%d" % par)])
                rst["ps%d" % par] = e_b
                e_sw = P.t(lambda e, par=par, sslot=sslot: e.matmul(
                    bank(c, sslot), c.pswap[:],
                    bfA[:, c.o_Xb + par * 512: c.o_Xb + par * 512 + 512],
                    start=True, stop=True), w=[e_b, e_psw, rst.get("sw%d" % par)])
                rst["xb%d" % par] = e_sw
                e_tA = P.v(lambda e, par=par, sslot=sslot, tcb=tcb: e.tensor_tensor(
                    f32A[:, c.o_tA + par * 512: c.o_tA + par * 512 + 512], bank(c, sslot),
                    sinS[:, tcb:tcb + 512], ALU.mult),
                    w=[e_sw, e_tab, rst.get("ta%d" % par)])
                rst["sw%d" % par] = e_tA
                e_tB = P.g(lambda e, par=par, tcb=tcb: e.tensor_tensor(
                    f32A[:, c.o_Xf + par * 512: c.o_Xf + par * 512 + 512],
                    f32A[:, c.o_Xf + par * 512: c.o_Xf + par * 512 + 512],
                    cosT[:, tcb:tcb + 512], ALU.mult), w=[e_f, e_tab])
                e_d = P.v(lambda e, par=par, dcb=dcb, dstoff=dstoff: e.tensor_add(
                    bfA[:, dstoff + dcb: dstoff + dcb + 512],
                    f32A[:, c.o_tA + par * 512: c.o_tA + par * 512 + 512],
                    f32A[:, c.o_Xf + par * 512: c.o_Xf + par * 512 + 512]),
                    w=[e_tA, e_tB] + w_bufs)
                rst["xf%d" % par] = e_d
                rst["ta%d" % par] = e_d
                evs.append((e_mm, e_f, e_b, e_sw, e_tA, e_d))
            return evs

        k_cbs = [(ch * 512, ch * 512, ch * 512) for ch in range(KT // 4)]
        q_cbs = [(oc * 512, OC[oc] * 512, OC[oc] * 512) for oc in range(4)]
        evKA = emit_rope(c.o_WTkA, c.o_KmA, k_cbs)
        evKB = emit_rope(c.o_WTkB, c.o_KmB, k_cbs)
        evQA = emit_rope(c.o_WTqA, c.o_QmA, q_cbs)
        evQB = emit_rope(c.o_WTqB, c.o_QmB, q_cbs)
        e_rope = evQB[-1][5]
        e_wt_free = evQB[-1][0]

        SBK = ((0, 1), (2, 3), (5, 6))
        e_pair_sp = []
        for hb, (koff, qoff, hrow) in enumerate(((c.o_KmA, c.o_QmA, 0), (c.o_KmB, c.o_QmB, 64))):
            for oc in range(4):
                gstart = OC[oc] * 512
                nkt = OC[oc] * 4 + 4
                eP = {}
                eZ = {}
                ez = None
                for kt in range(nkt):
                    sb_slot, s2_slot = SBK[kt % 3]
                    p4 = kt % 4
                    c0 = max(0, kt * 128 - gstart)
                    w_mm = [e_rope, e_vtok]
                    if kt >= 3:
                        w_mm.append(eP[kt - 3])
                    e_s = P.t(lambda e, koff=koff, qoff=qoff, kt=kt, oc=oc, c0=c0, sb_slot=sb_slot:
                              e.matmul(bank(c, sb_slot)[:, c0:512],
                                       bfA[0:64, koff + kt * 128: koff + kt * 128 + 128],
                                       bfA[0:64, qoff + oc * 512 + c0: qoff + oc * 512 + 512],
                                       start=True, stop=True, tile_position=(0, 0)), w=w_mm)
                    e_s2 = P.t(lambda e, koff=koff, qoff=qoff, kt=kt, oc=oc, c0=c0, s2_slot=s2_slot:
                               e.matmul(bank(c, s2_slot)[:, c0:512],
                                        bfA[64:128, koff + kt * 128: koff + kt * 128 + 128],
                                        bfA[64:128, qoff + oc * 512 + c0: qoff + oc * 512 + 512],
                                        start=True, stop=True, tile_position=(64, 0)))
                    w_e = [e_s2]
                    if kt >= 4:
                        w_e.append(eP[kt - 4])
                    e_ev2 = P.a(lambda e, p4=p4, c0=c0, s2_slot=s2_slot: e.copy(
                        f32A[:, c.o_s2sb + p4 * 512 + c0: c.o_s2sb + p4 * 512 + 512],
                        bank(c, s2_slot)[:, c0:512]), w=w_e)
                    w_p = [e_ev2, e_s]
                    if kt >= 4:
                        w_p.append(eZ[kt - 4])
                    e_p = P.v(lambda e, p4=p4, c0=c0, sb_slot=sb_slot: e.tensor_tensor(
                        bfA[:, c.o_Pbuf + p4 * 512 + c0: c.o_Pbuf + p4 * 512 + 512],
                        bank(c, sb_slot)[:, c0:512],
                        f32A[:, c.o_s2sb + p4 * 512 + c0: c.o_s2sb + p4 * 512 + 512],
                        ALU.mult), w=w_p)
                    if kt * 128 >= gstart:
                        e_p = P.v(lambda e, p4=p4, c0=c0: e.tensor_tensor(
                            bfA[:, c.o_Pbuf + p4 * 512 + c0: c.o_Pbuf + p4 * 512 + c0 + 128],
                            bfA[:, c.o_Pbuf + p4 * 512 + c0: c.o_Pbuf + p4 * 512 + c0 + 128],
                            c.trimask[:], ALU.mult), w=[e_tri, e_p])
                    eP[kt] = e_p
                    w_z = [e_p]
                    if kt == 0 and zev_prev.get(hrow) is not None:
                        w_z.append(zev_prev[hrow])
                    ez = P.t(lambda e, kt=kt, hrow=hrow, c0=c0, nkt=nkt, p4=p4: e.matmul(
                        bank(c, 4)[hrow:hrow + 64, c0:512],
                        bfA[:, c.o_Vtok + kt * 128 + hrow: c.o_Vtok + kt * 128 + hrow + 64],
                        bfA[:, c.o_Pbuf + p4 * 512 + c0: c.o_Pbuf + p4 * 512 + 512],
                        start=(kt == 0), stop=(kt == nkt - 1),
                        tile_position=(0, hrow)), w=w_z)
                    eZ[kt] = ez
                zp = oc % 2
                w_zv = [ez]
                if zsp_ev.get(zp) is not None:
                    w_zv.append(zsp_ev[zp])
                e_zev = P.a(lambda e, hrow=hrow, zp=zp: e.copy(
                    bfA[hrow:hrow + 64, c.o_zbuf + zp * 512: c.o_zbuf + zp * 512 + 512],
                    bank(c, 4)[hrow:hrow + 64, 0:512]), w=w_zv)
                zev_prev[hrow] = e_zev
                e_sp = P.dma(lambda e, hp=hp, hrow=hrow, oc=oc, zp=zp: e.dma_start(
                    out=scr["z"][hp][hrow:hrow + 64, oc * 512:(oc + 1) * 512],
                    in_=bfA[hrow:hrow + 64, c.o_zbuf + zp * 512: c.o_zbuf + zp * 512 + 512]),
                    w=[e_zev], sem="dz%d" % zp)
                zsp_ev[zp] = e_sp
                e_pair_sp.append(e_sp)
        e_prev_pair = zev_prev[64]
        zspill.append((e_pair_sp[-2], e_pair_sp[-1]))
        if DEBUG and hp == 0:
            e_dbg = P.dma(lambda e: e.dma_start(out=io["dbg_p0"],
                                                in_=bfA[:, c.o_KmA:c.o_KmA + 16384]),
                          w=[e_rope, e_vtok], sem="dbg")
            e_prev_pair = None
            w_dbg_guard.append(e_dbg)

        if hp == 1:
            emit_sw_totals()
        if hp >= 2:
            for _ in range(11):
                emit_p2_unit()
    while _p2["u"] < len(p2_units):
        emit_p2_unit()

    e_zf = []
    for hp in range(NPAIR):
        e_zf.append(P.dma(lambda e, hp=hp: e.dma_start(
            out=bfA[:, c.o_zfull + hp * 2048: c.o_zfull + hp * 2048 + 2048], in_=scr["z"][hp]),
            w=list(zspill[hp]) + [e_wt_free], sem="dzf"))
    e_wp = []
    for hp in range(NPAIR):
        e_wp.append(P.dmag(lambda e, hp=hp: e.dma_start(
            out=bfA[:, c.o_WprojT + hp * 1024: c.o_WprojT + hp * 1024 + 1024],
            in_=wT["wprojT"][hp * 128:(hp + 1) * 128, :]), w=[e_prev_pair], sem="dwp"))

    c1_evs = []
    x1_spills = []
    prev_x1rd = {}
    prev_yadd = {}
    for tt in range(NT_OWN):
        par = tt % 2
        x1t = c.x1stg[:, par * 1024:(par + 1) * 1024]
        grow = OC[tt // 4] * 512 + (tt % 4) * 128
        w_x = list(prev_x1rd.get(par, ()))
        e_x = P.dma(lambda e, grow=grow, par=par: e.dma_start(
            out=f32A[:, c.o_xstg + par * 1024: c.o_xstg + par * 1024 + 1024],
            in_=x[grow:grow + 128, :]), w=w_x, sem="dy%d" % par)
        ev_oc = []
        for oc2 in range(2):
            yslot = 2 + oc2
            e_mm = None
            for hp in range(NPAIR):
                w_mm = []
                if hp == 0:
                    w_mm = list(e_zf) + list(e_wp)
                    if oc2 in prev_yadd:
                        w_mm.append(prev_yadd[oc2])
                e_mm = P.t(lambda e, hp=hp, tt=tt, oc2=oc2, yslot=yslot: e.matmul(
                    bank(c, yslot),
                    bfA[:, c.o_zfull + hp * 2048 + tt * 128: c.o_zfull + hp * 2048 + tt * 128 + 128],
                    bfA[:, c.o_WprojT + hp * 1024 + oc2 * 512: c.o_WprojT + hp * 1024 + oc2 * 512 + 512],
                    start=(hp == 0), stop=(hp == 7)), w=w_mm)
            e_add = P.v(lambda e, oc2=oc2, yslot=yslot, x1t=x1t, par=par: e.tensor_add(
                x1t[:, oc2 * 512:(oc2 + 1) * 512], bank(c, yslot),
                f32A[:, c.o_xstg + par * 1024 + oc2 * 512: c.o_xstg + par * 1024 + oc2 * 512 + 512]),
                w=[e_mm, e_x] + w_x)
            prev_yadd[oc2] = e_add
            ev_oc.append(e_add)
        e_sp = P.dma(lambda e, tt=tt, x1t=x1t: e.dma_start(out=scr["x1"][tt], in_=x1t), w=ev_oc,
                     sem="dsp%d" % (tt % 2))
        x1_spills.append(e_sp)

        d2 = f32A[:, c.o_d2:c.o_d2 + 1024]
        w_cols = [c1_evs[tt - 1]] if tt >= 1 else [e_C]
        P.v(lambda e, x1t=x1t: e.reduce_sum(c.cols[:, 30:31], x1t, AX.X), w=ev_oc + w_cols, drain=True)
        e_nm = P.v(lambda e: e.tensor_scalar(c.cols[:, 31:32], c.cols[:, 30:31], -1.0 / D, None,
                                             ALU.mult), drain=True)
        e_d2 = P.a(lambda e, x1t=x1t, d2=d2: e.activation(d2, x1t, AF.Square, bias=c.cols[:, 31:32],
                                                          scale=1.0), w=[e_nm], drain=True)
        P.v(lambda e, d2=d2: e.reduce_sum(c.cols[:, 32:33], d2, AX.X), w=[e_d2], drain=True)
        e_vp = P.v(lambda e: e.tensor_scalar(c.cols[:, 32:33], c.cols[:, 32:33], 1.0 / D, LN_EPS,
                                             ALU.mult, ALU.add), drain=True)
        e_sq = P.a(lambda e: e.activation(c.cols[:, 33:34], c.cols[:, 32:33], AF.Sqrt, bias=0.0,
                                          scale=1.0), w=[e_vp], drain=True)
        P.v(lambda e: e.reciprocal(c.cols[:, 34:35], c.cols[:, 33:34]), w=[e_sq], drain=True)
        for _ in range(2):
            P.v(lambda e: e.tensor_mul(c.cols[:, 35:36], c.cols[:, 34:35], c.cols[:, 34:35]), drain=True)
            P.v(lambda e: e.tensor_mul(c.cols[:, 35:36], c.cols[:, 35:36], c.cols[:, 32:33]), drain=True)
            P.v(lambda e: e.tensor_scalar(c.cols[:, 35:36], c.cols[:, 35:36], -0.5, 1.5, ALU.mult,
                                          ALU.add), drain=True)
            P.v(lambda e: e.tensor_mul(c.cols[:, 34:35], c.cols[:, 34:35], c.cols[:, 35:36]), drain=True)
        e_rstd = P.barrier("vector")
        e_ab = P.a(lambda e, x1t=x1t, d2=d2: e.activation(d2, x1t, AF.Abs, bias=c.cols[:, 31:32],
                                                          scale=1.0), w=[e_rstd], drain=True)
        P.v(lambda e, d2=d2: e.reduce_max(c.cols[:, 36:37], d2, AX.X), w=[e_ab], drain=True)
        P.v(lambda e: e.tensor_mul(c.cols[:, 36:37], c.cols[:, 36:37], c.cols[:, 34:35]), drain=True)
        P.v(lambda e: e.tensor_scalar_max(c.cols[:, 36:37], c.cols[:, 36:37], Q_EPS), drain=True)
        P.v(lambda e: e.reciprocal(c.cols[:, 37:38], c.cols[:, 36:37]), drain=True)
        P.v(lambda e: e.tensor_scalar(c.cols[:, 37:38], c.cols[:, 37:38], 127.0, None, ALU.mult),
            drain=True)
        P.v(lambda e: e.tensor_mul(c.cols[:, 38:39], c.cols[:, 37:38], c.cols[:, 34:35]), drain=True)
        e_nq = P.v(lambda e: e.tensor_mul(c.cols[:, 39:40], c.cols[:, 31:32], c.cols[:, 38:39]),
                   drain=True)
        P.v(lambda e: e.tensor_mul(c.cols[:, 40:41], c.cols[:, 37:38], c.swg[:]),
            w=[sw_events["wgT"]], drain=True)
        P.v(lambda e, tt=tt: e.reciprocal(c.r1g[:, tt:tt + 1], c.cols[:, 40:41]), drain=True)
        P.v(lambda e: e.tensor_mul(c.cols[:, 40:41], c.cols[:, 37:38], c.swu[:]),
            w=[sw_events["wuT"]], drain=True)
        P.v(lambda e, tt=tt: e.reciprocal(c.r1u[:, tt:tt + 1], c.cols[:, 40:41]), drain=True)
        e_dq = P.a(lambda e, x1t=x1t, d2=d2: e.activation(d2, x1t, AF.Identity, bias=c.cols[:, 39:40],
                                                          scale=c.cols[:, 38:39]), w=[e_nq], drain=True)
        prev_x1rd[par] = (e_dq, e_sp)
        e_r1a = P.v(lambda e, d2=d2: e.tensor_scalar(d2, d2, c.Ccol[:], None, ALU.add), w=[e_dq])
        e_rnd = P.v(lambda e, d2=d2: e.tensor_scalar(d2, d2, c.nCcol[:], None, ALU.add), w=[e_r1a])
        tr = []
        e_tr = None
        for fc in range(8):
            tslot = fc % 2
            w_tr = [e_rnd, e_id]
            if fc >= 2:
                w_tr.append(tr[fc - 2])
            elif tt >= 1:
                w_tr.append(c1_evs[tt - 1])
            e_tr = P.t(lambda e, fc=fc, tslot=tslot, d2=d2: e.transpose(
                bank(c, tslot)[:, 0:128], d2[:, fc * 128:(fc + 1) * 128], c.ident[:]), w=w_tr)
            tr.append(P.a(lambda e, fc=fc, tslot=tslot, tt=tt: e.copy(
                bfA[:, c.o_c1T + fc * 2048 + tt * 128: c.o_c1T + fc * 2048 + tt * 128 + 128],
                bank(c, tslot)[:, 0:128]), w=[e_tr]))
        c1_evs.append(tr[-1])
    e_c1T = c1_evs[-1]
    if DEBUG:
        P.dma(lambda e: e.dma_start(out=io["dbg_c1T"], in_=bfA[:, c.o_c1T:c.o_c1T + 16384]),
              w=[e_c1T], sem="dbg")


    out_dmas = []
    prev_icc = {}
    prev_dstream = {}
    prev_eo = {}
    prev_gate_rd = {}
    prev_up_rd = {}
    e_prev_downevac = None
    for ss in range(NT_OWN // SS):
        tts = list(range(ss * SS, (ss + 1) * SS))
        gu_evs = {tt: [] for tt in tts}
        prev_gu_gate = {}
        prev_up = None
        for icc in range(8):
            par = icc % 2
            iccg = bfA[:, c.o_icc + par * 4096: c.o_icc + par * 4096 + 4096]
            iccu = bfA[:, c.o_iccu + par * 4096: c.o_iccu + par * 4096 + 4096]
            lg = []
            for fc in range(8):
                lg.append(P.dma(lambda e, fc=fc, icc=icc, iccg=iccg: e.dma_start(
                    out=iccg[:, fc * 512:(fc + 1) * 512],
                    in_=scr["cg"][fc][:, icc * 512:(icc + 1) * 512]),
                    w=([prev_icc.get(("g", par), e_c1T)] if fc == 0 else []) + [p2_ev["cg"][fc]],
                    sem="di%d" % par))
            lu = []
            for fc in range(8):
                lu.append(P.dma(lambda e, fc=fc, icc=icc, iccu=iccu: e.dma_start(
                    out=iccu[:, fc * 512:(fc + 1) * 512],
                    in_=scr["cu"][fc][:, icc * 512:(icc + 1) * 512]),
                    w=([prev_icc.get(("u", par), e_c1T)] if fc == 0 else []) + [p2_ev["cu"][fc]],
                    sem="du%d" % par))
            ev_gate = {}
            e_gmm = None
            for i, tt in enumerate(tts):
                gslot = i % 2
                e_mm = None
                for fc in range(8):
                    w_mm = []
                    if fc == 0:
                        w_mm = lg + [e_c1T]
                        if e_prev_downevac is not None:
                            w_mm.append(e_prev_downevac)
                        if prev_gate_rd.get(gslot) is not None:
                            w_mm.append(prev_gate_rd[gslot])
                    e_mm = P.t(lambda e, fc=fc, tt=tt, gslot=gslot, iccg=iccg: e.matmul(
                        bank(c, gslot),
                        bfA[:, c.o_c1T + fc * 2048 + tt * 128: c.o_c1T + fc * 2048 + tt * 128 + 128],
                        iccg[:, fc * 512:(fc + 1) * 512], start=(fc == 0), stop=(fc == 7)), w=w_mm)
                w_g = [e_mm]
                if tt in prev_gu_gate:
                    w_g.append(prev_gu_gate[tt])
                ev_gate[tt] = P.a(lambda e, tt=tt, i=i, gslot=gslot: e.activation(
                    f32A[:, c.o_gates + i * 512: c.o_gates + i * 512 + 512], bank(c, gslot),
                    AF.Silu, bias=0.0, scale=c.r1g[:, tt:tt + 1]), w=w_g)
                prev_gate_rd[gslot] = ev_gate[tt]
                e_gmm = e_mm
            prev_icc[("g", par)] = e_gmm
            e_umm = None
            for i, tt in enumerate(tts):
                uslot = 2 + i % 2
                e_mm = None
                for fc in range(8):
                    w_mm = []
                    if fc == 0:
                        w_mm = lu
                        if e_prev_downevac is not None:
                            w_mm.append(e_prev_downevac)
                        if prev_up_rd.get(uslot) is not None:
                            w_mm.append(prev_up_rd[uslot])
                    e_mm = P.t(lambda e, fc=fc, tt=tt, uslot=uslot, iccu=iccu: e.matmul(
                        bank(c, uslot),
                        bfA[:, c.o_c1T + fc * 2048 + tt * 128: c.o_c1T + fc * 2048 + tt * 128 + 128],
                        iccu[:, fc * 512:(fc + 1) * 512], start=(fc == 0), stop=(fc == 7)), w=w_mm)
                w_u = [e_mm]
                if prev_up is not None:
                    w_u.append(prev_up)
                e_u = P.a(lambda e, tt=tt, uslot=uslot: e.activation(
                    f32A[:, c.o_ups: c.o_ups + 512], bank(c, uslot),
                    AF.Copy, bias=0.0, scale=c.r1u[:, tt:tt + 1]), w=w_u)
                prev_up_rd[uslot] = e_u
                w_gu = [e_u, ev_gate[tt]]
                if ss == 0 and icc == 0 and i == 0:
                    w_gu += [_p2["prevst"][0], _p2["prevst"][1]]
                e_gu = P.v(lambda e, tt=tt, icc=icc, i=i: e.tensor_tensor(
                    f32A[:, c.o_gu + i * 4096 + icc * 512: c.o_gu + i * 4096 + icc * 512 + 512],
                    f32A[:, c.o_gates + i * 512: c.o_gates + i * 512 + 512],
                    f32A[:, c.o_ups: c.o_ups + 512], ALU.mult), w=w_gu)
                prev_up = e_gu
                prev_gu_gate[tt] = e_gu
                gu_evs[tt].append(e_gu)
                e_umm = e_mm
            prev_icc[("u", par)] = e_umm
        c2_evs = {}
        for i, tt in enumerate(tts):
            gut = f32A[:, c.o_gu + i * 4096: c.o_gu + i * 4096 + 4096]
            w_dbgu = []
            if DEBUG and tt == 0:
                w_dbgu.append(P.dma(lambda e, gut=gut: e.dma_start(out=io["dbg_gu"], in_=gut),
                                    w=gu_evs[tt], sem="dbg"))
            P.v(lambda e, gut=gut: e.reduce_max(c.cols[:, 44:45], gut, AX.X,
                                                apply_absolute_value=True),
                w=gu_evs[tt] + w_dbgu, drain=True)
            P.v(lambda e: e.tensor_scalar_max(c.cols[:, 44:45], c.cols[:, 44:45], Q_EPS), drain=True)
            P.v(lambda e: e.reciprocal(c.cols[:, 45:46], c.cols[:, 44:45]), drain=True)
            e_s2 = P.v(lambda e: e.tensor_scalar(c.cols[:, 45:46], c.cols[:, 45:46], 127.0, None,
                                                 ALU.mult), drain=True)
            P.v(lambda e: e.tensor_mul(c.cols[:, 46:47], c.cols[:, 45:46], c.swd[:]),
                w=[sw_events["wdT"]], drain=True)
            e_rd = P.v(lambda e, tt=tt: e.reciprocal(c.rdc[:, tt:tt + 1], c.cols[:, 46:47]),
                       drain=True)
            e_cs = P.v(lambda e, tt=tt: e.tensor_scalar(c.c2s[:, tt:tt + 1], c.cols[:, 45:46],
                                                        1.0, None, ALU.mult), w=[e_rd], drain=True)
            e_a1 = P.a(lambda e, gut=gut, tt=tt: e.activation(
                gut, gut, AF.Identity, bias=c.Ccol[:], scale=c.c2s[:, tt:tt + 1]), w=[e_cs])
            e_rnd = P.a(lambda e, gut=gut: e.activation(
                gut, gut, AF.Identity, bias=c.nCcol[:], scale=1.0), w=[e_a1])
            tr = []
            for ic in range(32):
                tslot = 4 + ic % 2
                w_tr = [e_rnd, e_id]
                if ic >= 2:
                    w_tr.append(tr[ic - 2])
                elif i >= 1:
                    w_tr.append(c2_evs[tts[i - 1]])
                elif e_prev_downevac is not None:
                    w_tr.append(e_prev_downevac)
                e_tr = P.t(lambda e, ic=ic, tslot=tslot, gut=gut: e.transpose(
                    bank(c, tslot)[:, 0:128], gut[:, ic * 128:(ic + 1) * 128], c.ident[:]), w=w_tr)
                tr.append(P.a(lambda e, ic=ic, i=i, tslot=tslot: e.copy(
                    bfA[:, c.o_c2T + i * 4096 + ic * 128: c.o_c2T + i * 4096 + ic * 128 + 128],
                    bank(c, tslot)[:, 0:128]), w=[e_tr]))
            c2_evs[tt] = tr[-1]
        down_mm = {}
        for ic in range(32):
            par = ic % 4
            cdb = bfA[:, c.o_cdblk + par * 1024: c.o_cdblk + par * 1024 + 1024]
            w_l = [prev_dstream.get(par), p2_ev["cd"][ic]]
            e_cd = P.dma(lambda e, ic=ic, cdb=cdb: e.dma_start(out=cdb, in_=scr["cd"][ic]), w=w_l,
                         sem="dd%d" % par)
            for foc in range(2):
                for i, tt in enumerate(tts):
                    dslot = i * 2 + foc
                    w_mm = [e_cd]
                    if ic == 0:
                        w_mm += [c2_evs[tts[-1]]]
                    down_mm[(ic, foc, i)] = P.t(lambda e, ic=ic, foc=foc, i=i, dslot=dslot,
                                                cdb=cdb: e.matmul(
                        bank(c, dslot),
                        bfA[:, c.o_c2T + i * 4096 + ic * 128: c.o_c2T + i * 4096 + ic * 128 + 128],
                        cdb[:, foc * 512:(foc + 1) * 512], start=(ic == 0), stop=(ic == 31)),
                        w=w_mm)
            prev_dstream[par] = down_mm[(ic, 1, SS - 1)]
        e_last_eo = None
        for i, tt in enumerate(tts):
            x1r = f32A[:, c.o_x1rb: c.o_x1rb + 1024]
            w_xr = [x1_spills[-1], x1_spills[-2]]
            if out_dmas:
                w_xr.append(out_dmas[-1])
            e_xr = P.dma(lambda e, tt=tt, x1r=x1r: e.dma_start(out=x1r, in_=scr["x1"][tt]), w=w_xr,
                         sem="dr")
            for foc in range(2):
                dslot = i * 2 + foc
                w_xo = [down_mm[(31, foc, i)], x1_spills[-1], x1_spills[-2]]
                if (foc,) in prev_eo:
                    w_xo.append(prev_eo[(foc,)])
                e_xo = P.a(lambda e, dslot=dslot, tt=tt, foc=foc: e.activation(
                    c.x1stg[:, foc * 512: foc * 512 + 512], bank(c, dslot),
                    AF.Copy, bias=0.0, scale=c.rdc[:, tt:tt + 1]), w=w_xo)
                w_o = [e_xo, e_xr]
                if len(out_dmas) >= 2:
                    w_o.append(out_dmas[-2])
                e_o = P.v(lambda e, foc=foc: e.tensor_add(
                    c.x1stg[:, 1024 + foc * 512: 1024 + foc * 512 + 512],
                    c.x1stg[:, foc * 512: foc * 512 + 512],
                    f32A[:, c.o_x1rb + foc * 512: c.o_x1rb + foc * 512 + 512]), w=w_o)
                prev_eo[(foc,)] = e_o
                out_dmas.append(P.dma(lambda e, tt=tt, foc=foc: e.dma_start(
                    out=out[tt * 128:(tt + 1) * 128, foc * 512:(foc + 1) * 512],
                    in_=c.x1stg[:, 1024 + foc * 512: 1024 + foc * 512 + 512]), w=[e_o],
                    sem="do%d" % foc))
                e_last_eo = e_o
        e_prev_downevac = e_last_eo

    P.items["sync"].append((None, tuple((sname, tot) for sname, tot in P.cnt.items()), None, 0, False))
    return P


def _fix_dma_waits(P):
    for ring_sem, ring_eng in (("ds", "sync"), ("dg", "gpsimd")):
        bounds = []
        tot = 0
        for fn, waits, sem, amt, drain in P.items[ring_eng]:
            if waits:
                bounds.append(tot)
            if sem == ring_sem:
                tot += amt
        bounds.append(tot)
        import bisect
        def remap(v, bounds=bounds):
            i = bisect.bisect_left(bounds, v)
            return bounds[i] if i < len(bounds) else v
        for eng in ENGINES:
            new = []
            for fn, waits, sem, amt, drain in P.items[eng]:
                nw = tuple((s2, remap(v) if s2 == ring_sem else v) for (s2, v) in waits)
                new.append((fn, nw, sem, amt, drain))
            P.items[eng] = new


def build():
    nc = bass.Bass("TRN2", target_bir_lowering=False)
    io = {}
    io["x"] = nc.dram_tensor("x", [T, D], F32, kind="ExternalInput").ap()
    wT = {}
    for wn, shp in (("wqT", [D, D]), ("wkT", [D, D]), ("wq2T", [D, D]), ("wk2T", [D, D]),
                    ("wvT", [D, D]), ("wprojT", [D, D]), ("wgT", [D, INNER]), ("wuT", [D, INNER]),
                    ("wdT", [INNER, D])):
        wT[wn] = nc.dram_tensor(wn, shp, F32, kind="ExternalInput").ap()
    io["wT"] = wT
    io["consts"] = nc.dram_tensor("consts", [128, 4], F32, kind="ExternalInput").ap()
    io["onesrow"] = nc.dram_tensor("onesrow", [1, 128], F32, kind="ExternalInput").ap()
    io["trimask"] = nc.dram_tensor("trimask", [128, 128], BF16, kind="ExternalInput").ap()
    io["pswap"] = nc.dram_tensor("pswap", [128, 128], BF16, kind="ExternalInput").ap()
    io["ident"] = nc.dram_tensor("ident", [128, 128], F32, kind="ExternalInput").ap()
    io["flagin"] = nc.dram_tensor("flag", [1, 1], I32, kind="ExternalInput").ap()
    io["ropetab"] = nc.dram_tensor("ropetab", [128, 8192], F32, kind="ExternalInput").ap()
    io["out"] = nc.dram_tensor("out", [N_OWN, D], F32, kind="ExternalOutput").ap()
    kind = "ExternalOutput"
    scr = {}
    scr["z"] = [nc.dram_tensor(f"scz{i}", [128, 2048], BF16, kind=kind).ap() for i in range(NPAIR)]
    scr["cg"] = [nc.dram_tensor(f"scg{i}", [128, INNER], BF16, kind=kind).ap() for i in range(8)]
    scr["cu"] = [nc.dram_tensor(f"scu{i}", [128, INNER], BF16, kind=kind).ap() for i in range(8)]
    scr["cd"] = [nc.dram_tensor(f"scd{i}", [128, D], BF16, kind=kind).ap() for i in range(32)]
    scr["x1"] = [nc.dram_tensor(f"sx1{i}", [128, D], F32, kind=kind).ap() for i in range(NT_OWN)]
    io["scr"] = scr
    if DEBUG:
        io["dbg_tab"] = nc.dram_tensor("dbg_tab", [128, 8192], F32, kind="ExternalOutput").ap()
        io["dbg_hnT"] = nc.dram_tensor("dbg_hnT", [128, 32768], BF16, kind="ExternalOutput").ap()
        io["dbg_p0"] = nc.dram_tensor("dbg_p0", [128, 16384], BF16, kind="ExternalOutput").ap()
        io["dbg_c1T"] = nc.dram_tensor("dbg_c1T", [128, 16384], BF16, kind="ExternalOutput").ap()
        io["dbg_gu"] = nc.dram_tensor("dbg_gu", [128, 4096], F32, kind="ExternalOutput").ap()
        io["dbg_cols"] = nc.dram_tensor("dbg_cols", [128, 41], F32, kind="ExternalOutput").ap()

    with ExitStack() as stack:
        c = alloc_tensors(nc, stack)
        progs = [build_half(nc, c, Prog(), h, io) for h in (0, 1)]
        sem_names = sorted(set(list(progs[0].cnt) + list(progs[1].cnt))) + ["fl"]
        sems = {s: stack.enter_context(nc.semaphore(s)) for s in sem_names}
        block = stack.enter_context(nc.Block())

        def replay(eng, items):
            waited = {}
            self_cnt = {}
            for fn, waits, sem, amt, drain in items:
                need = {}
                for (s, v) in waits:
                    if v > 0 and v > waited.get(s, 0):
                        need[s] = max(need.get(s, 0), v)
                for s, v in need.items():
                    eng.wait_ge(sems[s], v)
                    waited[s] = v
                if fn is None:
                    continue
                inst = fn(eng)
                if sem is not None:
                    inst.then_inc(sems[sem], amt)
                if drain and sem is not None:
                    eng.wait_ge(sems[sem], self_cnt.get(sem, 0) + amt)
                if sem is not None:
                    self_cnt[sem] = self_cnt.get(sem, 0) + amt
                    waited.setdefault(sem, 0)
                    if drain:
                        waited[sem] = max(waited[sem], self_cnt[sem])

        def section(eng_name):
            def body(eng):
                if eng_name == "sync":
                    eng.dma_start(out=c.flag[:], in_=io["flagin"]).then_inc(sems["fl"], 16)
                eng.wait_ge(sems["fl"], 16)
                with eng.register("rf_" + eng_name) as rf:
                    eng.load(rf, c.flag[0:1, 0:1])
                    with eng.If_eq(rf, 0):
                        replay(eng, progs[0].items[eng_name])
                    with eng.If_eq(rf, 1):
                        replay(eng, progs[1].items[eng_name])
            return body

        block.sync(section("sync"))
        block.tensor(section("tensor"))
        block.vector(section("vector"))
        block.scalar(section("scalar"))
        block.gpsimd(section("gpsimd"))
    return nc


def _rope_tables_np():
    try:
        import jax
        import jax.numpy as jnp
        cpu = jax.devices("cpu")[0]
        with jax.default_device(cpu):
            inv_freq = 1.0 / 10000.0 ** (jnp.arange(0, 64, 2, dtype=jnp.float32) / 64)
            t = jnp.arange(T, dtype=jnp.float32)
            freqs = jnp.outer(t, inv_freq)
            cos = np.asarray(jnp.cos(freqs), dtype=np.float32)
            sin = np.asarray(jnp.sin(freqs), dtype=np.float32)
    except Exception:
        e = np.arange(0, 64, 2, dtype=np.float32) / np.float32(64)
        inv = (1.0 / (10000.0 ** e.astype(np.float64))).astype(np.float32)
        freqs = np.outer(np.arange(T, dtype=np.float32), inv).astype(np.float32)
        cos, sin = np.cos(freqs).astype(np.float32), np.sin(freqs).astype(np.float32)
    r = np.arange(128)
    sgn = np.where(r % 64 < 32, 1.0, -1.0).astype(np.float32)
    cosT = cos.T[r % 32]
    sinS = sin.T[r % 32] * sgn[:, None]
    return np.ascontiguousarray(np.concatenate([cosT, sinS], axis=1))


def _consts_np():
    r = np.arange(128)
    consts = np.stack([np.ones(128), np.full(128, 2.0 ** -30), (r % 32).astype(float),
                       np.where(r % 64 < 32, 1.0, -1.0)], axis=1).astype(np.float32)
    onesrow = np.ones((1, 128), np.float32)
    tri = np.triu(np.ones((128, 128))).astype(ml_dtypes.bfloat16)
    psw = np.zeros((128, 128), ml_dtypes.bfloat16)
    for j in range(128):
        psw[j ^ 32, j] = 1.0
    ident = np.eye(128, dtype=np.float32)
    return consts, onesrow, tri, psw, ident


def kernel(L0_attn_x, L0_attn_Wq, L0_attn_Wk, L0_attn_Wq2, L0_attn_Wk2, L0_attn_Wv,
           L0_attn_Wproj, L0_ffn_w_gate, L0_ffn_w_up, L0_ffn_w_down):
    x = np.ascontiguousarray(np.asarray(L0_attn_x, dtype=np.float32))
    wTs = {
        "wqT": L0_attn_Wq, "wkT": L0_attn_Wk, "wq2T": L0_attn_Wq2, "wk2T": L0_attn_Wk2,
        "wvT": L0_attn_Wv, "wprojT": L0_attn_Wproj, "wgT": L0_ffn_w_gate,
        "wuT": L0_ffn_w_up, "wdT": L0_ffn_w_down,
    }
    wTs = {k: np.ascontiguousarray(np.asarray(v, dtype=np.float32).T) for k, v in wTs.items()}
    consts, onesrow, tri, psw, ident = _consts_np()
    ropetab = _rope_tables_np()

    nc = build()
    in_maps = []
    for core in range(8):
        b, half = core // 2, core % 2
        m = {"x": np.ascontiguousarray(x[b]),
             "consts": consts, "onesrow": onesrow, "trimask": tri, "pswap": psw,
             "ident": ident, "flag": np.full((1, 1), half, np.int32), "ropetab": ropetab}
        m.update(wTs)
        in_maps.append(m)
    res = run_bass_kernel_spmd(nc, in_maps, core_ids=list(range(8)))
    full = np.empty((4, T, D), np.float32)
    for core in range(8):
        b, half = core // 2, core % 2
        o = res.results[core]["out"]
        if half == 0:
            full[b, 0:1024] = o[0:1024]
            full[b, 3072:4096] = o[1024:2048]
        else:
            full[b, 1024:3072] = o
    return full


# version 11
# speedup vs baseline: 1.6509x; 1.4616x over previous
import numpy as np
import ml_dtypes
from contextlib import ExitStack

import concourse.bass as bass
import concourse.mybir as mybir
from concourse.bass_utils import run_bass_kernel_spmd

F32 = mybir.dt.float32
BF16 = mybir.dt.bfloat16
I32 = mybir.dt.int32
ALU = mybir.AluOpType
AF = mybir.ActivationFunctionType
AX = mybir.AxisListType

D = 1024
T = 4096
INNER = 4096
NPAIR = 8
N_OWN = 2048
NT_OWN = 16
SS = 4
LN_EPS = 1e-5
Q_EPS = 1e-5
RNDC = 1.5 * 2.0 ** 23
RNDK = 1.5 * 2.0 ** 52 - 1.5 * 2.0 ** 23

ENGINES = ("sync", "tensor", "vector", "scalar", "gpsimd")
DEBUG = False


def own_chunks(half):
    return [0, 1, 6, 7] if half == 0 else [2, 3, 4, 5]


def kv_tiles(half):
    return 32 if half == 0 else 24


class Prog:
    def __init__(self):
        self.items = {e: [] for e in ENGINES}
        self.cnt = {}

    def ev(self, sem):
        return (sem, self.cnt.get(sem, 0))

    def _add(self, eng, fn, waits, sem, amt, drain):
        self.cnt[sem] = self.cnt.get(sem, 0) + amt
        evt = (sem, self.cnt[sem])
        self.items[eng].append((fn, tuple(w for w in waits if w is not None), sem, amt, drain))
        return evt

    def dma(self, fn, w=(), sem="ds"):
        return self._add("sync", fn, w, sem, 16, False)

    def dmag(self, fn, w=(), sem="dg"):
        return self._add("gpsimd", fn, w, sem, 16, False)

    def t(self, fn, w=(), drain=False):
        return self._add("tensor", fn, w, "t", 1, drain)

    def v(self, fn, w=(), drain=False):
        return self._add("vector", fn, w, "v", 1, drain)

    def a(self, fn, w=(), drain=False):
        return self._add("scalar", fn, w, "a", 1, drain)

    def g(self, fn, w=(), drain=False):
        return self._add("gpsimd", fn, w, "g", 1, drain)

    def barrier(self, eng):
        if eng == "vector":
            return self.v(lambda e: e.engine_nop())
        if eng == "scalar":
            return self.a(lambda e: e.engine_nop())
        raise ValueError(eng)


class Ctx:
    pass


def alloc_tensors(nc, stack):
    c = Ctx()
    sb = lambda n, s, d: stack.enter_context(nc.sbuf_tensor(n, s, d))
    c.bfA = sb("bfA", [128, 60416], BF16)
    c.f32A = sb("f32A", [128, 19968], F32)
    c.ps = stack.enter_context(nc.psum_tensor("ps", [128, 4096], F32))

    c.o_hnT = 0
    c.o_KmA = 32768
    c.o_KmB = 36864
    c.o_Vtok = 40960
    c.o_QmA = 45056
    c.o_QmB = 47104
    c.o_WTqA = 49152
    c.o_WTqB = 50176
    c.o_WTkA = 51200
    c.o_WTkB = 52224
    c.o_WTv = 53248
    c.o_Xb = 54272
    c.o_Pbuf = 55296
    c.o_zbuf = 57344
    c.o_zfull = 0
    c.o_WprojT = 16384
    c.o_c1T = 24576
    c.o_icc = 0
    c.o_iccu = 40960
    c.o_c2T = 8192
    c.o_cdblk = 49152

    c.o_cosT = 0
    c.o_sinS = 4096
    c.o_xstg = 8192
    c.o_wstg = 10240
    c.o_d2 = 14336
    c.o_Xf = 15360
    c.o_tA = 16384
    c.o_s2sb = 17408
    c.o_gu = 0
    c.o_x1rb = 16384
    c.o_gates = 17408
    c.o_ups = 19456

    c.cols = sb("cols", [128, 64], F32)
    c.c2s = sb("c2s", [128, 16], F32)
    c.r1g = sb("r1g", [128, 16], F32)
    c.r1u = sb("r1u", [128, 16], F32)
    c.rdc = sb("rdc", [128, 16], F32)
    c.Ccol = sb("Ccol", [128, 1], F32)
    c.nCcol = sb("nCcol", [128, 1], F32)
    c.swg = sb("swg", [128, 1], F32)
    c.swu = sb("swu", [128, 1], F32)
    c.swd = sb("swd", [128, 1], F32)
    c.acc = sb("acc", [128, 80], F32)
    c.consts = sb("constsb", [128, 4], F32)
    c.onesrow = sb("onesrowb", [1, 128], F32)
    c.trimask = sb("trimaskb", [128, 128], BF16)
    c.pswap = sb("pswapb", [128, 128], BF16)
    c.ident = sb("identb", [128, 128], F32)
    c.flag = sb("flagsb", [1, 1], I32)
    c.x1stg = sb("x1stg", [128, 2048], F32)
    c.ssc = sb("ssc", [1, 1], F32)
    return c


def bank(c, i):
    return c.ps[:, i * 512:(i + 1) * 512]


def build_half(nc, c, P, half, io):
    OC = own_chunks(half)
    KT = kv_tiles(half)
    bfA, f32A, ps = c.bfA, c.f32A, c.ps
    x, out = io["x"], io["out"]
    wT = io["wT"]
    scr = io["scr"]

    ones = c.consts[:, 0:1]
    epsv = c.consts[:, 1:2]
    jcol = c.consts[:, 2:3]
    sgn = c.consts[:, 3:4]
    cosT = f32A[:, c.o_cosT:c.o_cosT + 4096]
    sinS = f32A[:, c.o_sinS:c.o_sinS + 4096]

    P.dma(lambda e: e.dma_start(out=c.consts[:, 0:4], in_=io["consts"]), sem="dc")
    P.dma(lambda e: e.dma_start(out=c.onesrow[:], in_=io["onesrow"]), sem="dc")
    P.dma(lambda e: e.dma_start(out=c.trimask[:], in_=io["trimask"]), sem="dc")
    P.dma(lambda e: e.dma_start(out=c.pswap[:], in_=io["pswap"]), sem="dc")
    P.dma(lambda e: e.dma_start(out=c.ident[:], in_=io["ident"]), sem="dc")
    e_consts = e_or = e_tri = e_psw = e_id = P.dma(
        lambda e: e.dma_start(out=f32A[:, 0:8192], in_=io["ropetab"]), sem="dc")

    P.v(lambda e: e.tensor_add(c.cols[:, 0:1], ones, epsv), w=[e_consts], drain=True)
    P.v(lambda e: e.tensor_sub(c.cols[:, 0:1], c.cols[:, 0:1], ones), drain=True)
    P.v(lambda e: e.tensor_scalar(c.cols[:, 0:1], c.cols[:, 0:1], 2.0 ** 30, None, ALU.mult), drain=True)
    P.v(lambda e: e.tensor_scalar(c.Ccol[:], c.cols[:, 0:1], RNDK, RNDC, ALU.mult, ALU.add), drain=True)
    e_C = P.v(lambda e: e.tensor_scalar(c.nCcol[:], c.Ccol[:], -1.0, None, ALU.mult), drain=True)

    e_tab = e_consts

    slot0 = {"wgT": 0, "wuT": 16, "wdT": 32}
    ccols = {"wgT": INNER, "wuT": INNER, "wdT": D}
    p1_units = ([("wgT", ck, hh) for ck in range(8) for hh in (0, 2048)]
                + [("wuT", ck, hh) for ck in range(8) for hh in (0, 2048)]
                + [("wdT", ck, 0) for ck in range(32)])
    p1_state = {"u": 0, "prev": {0: None, 1: None}, "last": None}

    def emit_p1_unit():
        if p1_state["u"] >= len(p1_units):
            return
        u = p1_state["u"]
        p1_state["u"] += 1
        wname, ck, hh = p1_units[u]
        hw = min(2048, ccols[wname] - hh)
        par = u % 2
        s = slot0[wname] + (ck * 2 + hh // 2048 if wname != "wdT" else ck)
        wst = f32A[:, c.o_wstg + par * 2048: c.o_wstg + par * 2048 + hw]
        w_l = [p1_state["prev"][par]] if p1_state["prev"][par] else []
        e_l = P.dma(lambda e, wname=wname, ck=ck, hh=hh, hw=hw, wst=wst: e.dma_start(
            out=wst, in_=wT[wname][ck * 128:(ck + 1) * 128, hh:hh + hw]), w=w_l,
            sem="dw%d" % par)
        e_r = P.v(lambda e, wst=wst, s=s: e.reduce_sum(
            c.acc[:, s:s + 1], wst, AX.X, apply_absolute_value=True), w=[e_l])
        p1_state["prev"][par] = e_r
        p1_state["last"] = e_r

    ln_done = []
    for gt in range(KT):
        par = gt % 2
        xst = f32A[:, c.o_xstg + par * 1024: c.o_xstg + (par + 1) * 1024]
        d2 = f32A[:, c.o_d2:c.o_d2 + 1024]
        w_x = [ln_done[gt - 2]] if gt >= 2 else []
        e_x = P.dma(lambda e, gt=gt, xst=xst: e.dma_start(out=xst, in_=x[gt * 128:(gt + 1) * 128, :]),
                    w=w_x, sem="dx%d" % (gt % 2))
        emit_p1_unit()
        emit_p1_unit()
        w_cols = [ln_done[gt - 1]] if gt >= 1 else [e_C]
        P.v(lambda e, xst=xst: e.reduce_sum(c.cols[:, 16:17], xst, AX.X), w=[e_x] + w_cols, drain=True)
        e_nm = P.v(lambda e: e.tensor_scalar(c.cols[:, 17:18], c.cols[:, 16:17], -1.0 / D, None, ALU.mult),
                   drain=True)
        e_d2 = P.a(lambda e, xst=xst, d2=d2: e.activation(d2, xst, AF.Square, bias=c.cols[:, 17:18],
                                                          scale=1.0), w=[e_nm], drain=True)
        P.v(lambda e, d2=d2: e.reduce_sum(c.cols[:, 18:19], d2, AX.X), w=[e_d2], drain=True)
        e_vp = P.v(lambda e: e.tensor_scalar(c.cols[:, 18:19], c.cols[:, 18:19], 1.0 / D, LN_EPS,
                                             ALU.mult, ALU.add), drain=True)
        e_sq = P.a(lambda e: e.activation(c.cols[:, 19:20], c.cols[:, 18:19], AF.Sqrt, bias=0.0,
                                          scale=1.0), w=[e_vp], drain=True)
        P.v(lambda e: e.reciprocal(c.cols[:, 20:21], c.cols[:, 19:20]), w=[e_sq], drain=True)
        for _ in range(2):
            P.v(lambda e: e.tensor_mul(c.cols[:, 21:22], c.cols[:, 20:21], c.cols[:, 20:21]), drain=True)
            P.v(lambda e: e.tensor_mul(c.cols[:, 21:22], c.cols[:, 21:22], c.cols[:, 18:19]), drain=True)
            P.v(lambda e: e.tensor_scalar(c.cols[:, 21:22], c.cols[:, 21:22], -0.5, 1.5, ALU.mult,
                                          ALU.add), drain=True)
            P.v(lambda e: e.tensor_mul(c.cols[:, 20:21], c.cols[:, 20:21], c.cols[:, 21:22]), drain=True)
        e_nmr = P.v(lambda e: e.tensor_mul(c.cols[:, 22:23], c.cols[:, 17:18], c.cols[:, 20:21]),
                    drain=True)
        e_hn = P.a(lambda e, xst=xst, d2=d2: e.activation(d2, xst, AF.Identity, bias=c.cols[:, 22:23],
                                                          scale=c.cols[:, 20:21]), w=[e_nmr], drain=True)
        evs = []
        for fc in range(8):
            slot = fc % 2
            w_tr = [e_hn, e_id]
            if fc >= 2:
                w_tr.append(evs[fc - 2])
            elif gt >= 1:
                w_tr.append(ln_done[gt - 1])
            e_tr = P.t(lambda e, fc=fc, slot=slot, d2=d2: e.transpose(
                bank(c, slot)[:, 0:128], d2[:, fc * 128:(fc + 1) * 128], c.ident[:]), w=w_tr)
            e_ev = P.a(lambda e, fc=fc, slot=slot, gt=gt: e.copy(
                bfA[:, c.o_hnT + fc * 4096 + gt * 128: c.o_hnT + fc * 4096 + gt * 128 + 128],
                bank(c, slot)[:, 0:128]), w=[e_tr])
            evs.append(e_ev)
        ln_done.append(evs[-1])
    e_hnT = ln_done[-1]

    if DEBUG:
        P.dma(lambda e: e.dma_start(out=io["dbg_tab"], in_=f32A[:, 0:8192]), w=[e_tab], sem="dbg")
        P.dma(lambda e: e.dma_start(out=io["dbg_hnT"], in_=bfA[:, 0:32768]), w=[e_hnT], sem="dbg")

    while p1_state["u"] < len(p1_units):
        emit_p1_unit()
    e_p1 = p1_state["last"]

    sw_events = {}

    def emit_sw_totals():
        for wname, swcol, nelem in (("wgT", c.swg, INNER * D), ("wuT", c.swu, INNER * D),
                                    ("wdT", c.swd, D * INNER)):
            s0, nck = slot0[wname], {"wgT": 16, "wuT": 16, "wdT": 32}[wname]
            e_r = P.v(lambda e, s0=s0, nck=nck: e.reduce_sum(c.cols[:, 24:25],
                                                             c.acc[:, s0:s0 + nck], AX.X),
                      w=[e_p1], drain=True)
            e_m1 = P.t(lambda e: e.matmul(ps[0:1, 1024:1025], c.cols[:, 24:25], ones,
                                          start=True, stop=True), w=[e_r, e_consts, P.ev("v")])
            e_e1 = P.a(lambda e: e.copy(c.ssc[:], ps[0:1, 1024:1025]), w=[e_m1], drain=True)
            e_m2 = P.t(lambda e: e.matmul(ps[:, 1025:1026], c.onesrow[:], c.ssc[:],
                                          start=True, stop=True), w=[e_e1, e_or])
            e_e2 = P.a(lambda e: e.copy(c.cols[:, 25:26], ps[:, 1025:1026]), w=[e_m2], drain=True)
            P.v(lambda e, nelem=nelem: e.tensor_scalar(c.cols[:, 26:27], c.cols[:, 25:26],
                                                       1.0 / nelem, None, ALU.mult), w=[e_e2], drain=True)
            P.v(lambda e: e.tensor_scalar_max(c.cols[:, 26:27], c.cols[:, 26:27], Q_EPS), drain=True)
            sw_events[wname] = P.v(lambda e, swcol=swcol: e.reciprocal(swcol[:], c.cols[:, 26:27]),
                                   drain=True)

    p2_ev = {"cg": [None] * 8, "cu": [None] * 8, "cd": [None] * 32}
    _p2 = {"u": 0, "prevst": {0: e_p1, 1: e_p1}}
    p2_units = ([("wgT", ck, hh) for ck in range(8) for hh in (0, 2048)]
                + [("wuT", ck, hh) for ck in range(8) for hh in (0, 2048)]
                + [("wdT", ck, 0) for ck in range(32)])

    def emit_p2_unit():
        if _p2["u"] >= len(p2_units):
            return
        u = _p2["u"]
        _p2["u"] += 1
        wname, ck, hh = p2_units[u]
        hw = min(2048, ccols[wname] - hh)
        par = u % 2
        swcol = {"wgT": c.swg, "wuT": c.swu, "wdT": c.swd}[wname]
        scrkey = {"wgT": "cg", "wuT": "cu", "wdT": "cd"}[wname]
        wst = f32A[:, c.o_wstg + par * 2048: c.o_wstg + par * 2048 + hw]
        e_l = P.dma(lambda e, wname=wname, ck=ck, hh=hh, hw=hw, wst=wst: e.dma_start(
            out=wst, in_=wT[wname][ck * 128:(ck + 1) * 128, hh:hh + hw]),
            w=[_p2["prevst"][par]], sem="dw%d" % par)
        e_a1 = P.a(lambda e, wst=wst, swcol=swcol: e.activation(
            wst, wst, AF.Identity, bias=c.Ccol[:], scale=swcol[:]),
            w=[e_l, sw_events[wname], e_C])
        e_a2 = P.a(lambda e, wst=wst: e.activation(
            wst, wst, AF.Identity, bias=c.nCcol[:], scale=1.0), w=[e_a1])
        e_q = P.v(lambda e, wst=wst: e.tensor_scalar(wst, wst, -1.0, 1.0, ALU.max, ALU.min),
                  w=[e_a2])
        e_st = P.dmag(lambda e, wst=wst, scrkey=scrkey, ck=ck, hh=hh, hw=hw: e.dma_start(
            out=scr[scrkey][ck][:, hh:hh + hw], in_=wst), w=[e_q], sem="dq%d" % par)
        _p2["prevst"][par] = e_st
        p2_ev[scrkey][ck] = e_st

    rst = {}
    w_dbg_guard = []
    zspill = []
    zev_prev = {}
    zsp_ev = {}
    e_prev_pair = None
    e_wt_free = None
    for hp in range(NPAIR):
        wl = []
        w_guard = [e_wt_free] if e_wt_free is not None else []
        for (wa, wb, offA, offB) in (("wqT", "wq2T", c.o_WTqA, c.o_WTqB),
                                     ("wkT", "wk2T", c.o_WTkA, c.o_WTkB)):
            for sub, off in ((0, offA), (1, offB)):
                for wn, halfcol in ((wa, 0), (wb, 64)):
                    def f(e, wn=wn, sub=sub, off=off, halfcol=halfcol, hp=hp):
                        src = wT[wn].rearrange("(fc p) o -> p fc o", p=128)[
                            :, :, hp * 128 + sub * 64: hp * 128 + sub * 64 + 64]
                        dst = bfA[:, off:off + 1024].rearrange("p (fc z) -> p fc z", z=128)[
                            :, :, halfcol:halfcol + 64]
                        return e.dma_start(out=dst, in_=src)
                    wl.append(P.dmag(f, w=w_guard, sem="dwt"))
        def fv(e, hp=hp):
            src = wT["wvT"].rearrange("(fc p) o -> p fc o", p=128)[:, :, hp * 128:(hp + 1) * 128]
            dst = bfA[:, c.o_WTv:c.o_WTv + 1024].rearrange("p (fc z) -> p fc z", z=128)
            return e.dma_start(out=dst, in_=src)
        wl.append(P.dmag(fv, w=w_guard, sem="dwt"))
        P.v(lambda e: e.tensor_scalar(bfA[:, c.o_WTqA:c.o_WTqA + 1024],
                                      bfA[:, c.o_WTqA:c.o_WTqA + 1024], 1.0 / 64, None, ALU.mult),
            w=wl)
        e_wready = P.v(lambda e: e.tensor_scalar(bfA[:, c.o_WTqB:c.o_WTqB + 1024],
                                                 bfA[:, c.o_WTqB:c.o_WTqB + 1024], 1.0 / 64, None,
                                                 ALU.mult), w=wl)

        w_bufs = [e_prev_pair] if e_prev_pair is not None else []
        w_bufs += w_dbg_guard
        if e_prev_pair is None and hp > 0:
            w_bufs.extend(zspill[-1])

        vp_last = []
        for ch in range(KT // 4):
            par = ch % 2
            pslot = 5 + par
            e_mm = None
            for fc in range(8):
                w_mm = []
                if fc == 0:
                    w_mm = [e_hnT] + wl + [rst.get("ps%d" % par)]
                e_mm = P.t(lambda e, fc=fc, ch=ch, pslot=pslot: e.matmul(
                    bank(c, pslot), bfA[:, c.o_WTv + fc * 128:c.o_WTv + fc * 128 + 128],
                    bfA[:, c.o_hnT + fc * 4096 + ch * 512: c.o_hnT + fc * 4096 + ch * 512 + 512],
                    start=(fc == 0), stop=(fc == 7)), w=w_mm)
            e_ev = P.a(lambda e, pslot=pslot, ch=ch, par=par: e.copy(
                f32A[:, c.o_Xf + par * 512: c.o_Xf + par * 512 + 512], bank(c, pslot)),
                w=[e_mm, rst.get("xf%d" % par)])
            rst["ps%d" % par] = e_ev
            tr_ev = []
            e_tr = None
            for j in range(4):
                tslot = j % 2
                w_tr = [e_ev, e_id] + w_bufs
                if j >= 2:
                    w_tr.append(tr_ev[j - 2])
                elif ch >= 1:
                    w_tr.append(vp_last[ch - 1])
                e_tr = P.t(lambda e, j=j, ch=ch, tslot=tslot, par=par: e.transpose(
                    bank(c, tslot)[:, 0:128],
                    f32A[:, c.o_Xf + par * 512 + j * 128: c.o_Xf + par * 512 + j * 128 + 128],
                    c.ident[:]), w=w_tr)
                tr_ev.append(P.a(lambda e, j=j, ch=ch, tslot=tslot: e.copy(
                    bfA[:, c.o_Vtok + (ch * 4 + j) * 128: c.o_Vtok + (ch * 4 + j) * 128 + 128],
                    bank(c, tslot)[:, 0:128]), w=[e_tr]))
            rst["xf%d" % par] = e_tr
            vp_last.append(tr_ev[-1])
        e_vtok = vp_last[-1]

        def emit_rope(setoff, dstoff, colbases):
            evs = []
            for i, (dcb, hcb, tcb) in enumerate(colbases):
                par = i % 2
                pslot = 5 + par
                sslot = 7 if par == 0 else 4
                e_mm = None
                for fc in range(8):
                    w_mm = []
                    if fc == 0:
                        w_mm = [e_wready, e_hnT, rst.get("ps%d" % par)]
                    e_mm = P.t(lambda e, fc=fc, pslot=pslot, setoff=setoff, hcb=hcb: e.matmul(
                        bank(c, pslot), bfA[:, setoff + fc * 128: setoff + fc * 128 + 128],
                        bfA[:, c.o_hnT + fc * 4096 + hcb: c.o_hnT + fc * 4096 + hcb + 512],
                        start=(fc == 0), stop=(fc == 7)), w=w_mm)
                e_f = P.a(lambda e, par=par, pslot=pslot: e.copy(
                    f32A[:, c.o_Xf + par * 512: c.o_Xf + par * 512 + 512], bank(c, pslot)),
                    w=[e_mm, rst.get("xf%d" % par)])
                e_b = P.a(lambda e, par=par, pslot=pslot: e.copy(
                    bfA[:, c.o_Xb + par * 512: c.o_Xb + par * 512 + 512], bank(c, pslot)),
                    w=[rst.get("xb%d" % par)])
                rst["ps%d" % par] = e_b
                e_sw = P.t(lambda e, par=par, sslot=sslot: e.matmul(
                    bank(c, sslot), c.pswap[:],
                    bfA[:, c.o_Xb + par * 512: c.o_Xb + par * 512 + 512],
                    start=True, stop=True), w=[e_b, e_psw, rst.get("sw%d" % par)])
                rst["xb%d" % par] = e_sw
                e_tA = P.v(lambda e, par=par, sslot=sslot, tcb=tcb: e.tensor_tensor(
                    f32A[:, c.o_tA + par * 512: c.o_tA + par * 512 + 512], bank(c, sslot),
                    sinS[:, tcb:tcb + 512], ALU.mult),
                    w=[e_sw, e_tab, rst.get("ta%d" % par)])
                rst["sw%d" % par] = e_tA
                e_tB = P.g(lambda e, par=par, tcb=tcb: e.tensor_tensor(
                    f32A[:, c.o_Xf + par * 512: c.o_Xf + par * 512 + 512],
                    f32A[:, c.o_Xf + par * 512: c.o_Xf + par * 512 + 512],
                    cosT[:, tcb:tcb + 512], ALU.mult), w=[e_f, e_tab])
                e_d = P.v(lambda e, par=par, dcb=dcb, dstoff=dstoff: e.tensor_add(
                    bfA[:, dstoff + dcb: dstoff + dcb + 512],
                    f32A[:, c.o_tA + par * 512: c.o_tA + par * 512 + 512],
                    f32A[:, c.o_Xf + par * 512: c.o_Xf + par * 512 + 512]),
                    w=[e_tA, e_tB] + w_bufs)
                rst["xf%d" % par] = e_d
                rst["ta%d" % par] = e_d
                evs.append((e_mm, e_f, e_b, e_sw, e_tA, e_d))
            return evs

        k_cbs = [(ch * 512, ch * 512, ch * 512) for ch in range(KT // 4)]
        q_cbs = [(oc * 512, OC[oc] * 512, OC[oc] * 512) for oc in range(4)]
        evKA = emit_rope(c.o_WTkA, c.o_KmA, k_cbs)
        evKB = emit_rope(c.o_WTkB, c.o_KmB, k_cbs)
        evQA = emit_rope(c.o_WTqA, c.o_QmA, q_cbs)
        evQB = emit_rope(c.o_WTqB, c.o_QmB, q_cbs)
        e_rope = evQB[-1][5]
        e_wt_free = evQB[-1][0]

        SBK = ((0, 1), (2, 3), (5, 6))
        e_pair_sp = []
        for hb, (koff, qoff, hrow) in enumerate(((c.o_KmA, c.o_QmA, 0), (c.o_KmB, c.o_QmB, 64))):
            for oc in range(4):
                gstart = OC[oc] * 512
                nkt = OC[oc] * 4 + 4
                AVD = 2
                eP = {}
                eZ = {}
                ez = None

                def emit_av(kt, nkt=nkt, hrow=hrow, gstart=gstart):
                    p4 = kt % 4
                    c0 = max(0, kt * 128 - gstart)
                    w_z = [eP[kt]]
                    if kt == 0 and zev_prev.get(hrow) is not None:
                        w_z.append(zev_prev[hrow])
                    eZ[kt] = P.t(lambda e, kt=kt, hrow=hrow, c0=c0, nkt=nkt, p4=p4: e.matmul(
                        bank(c, 4)[hrow:hrow + 64, c0:512],
                        bfA[:, c.o_Vtok + kt * 128 + hrow: c.o_Vtok + kt * 128 + hrow + 64],
                        bfA[:, c.o_Pbuf + p4 * 512 + c0: c.o_Pbuf + p4 * 512 + 512],
                        start=(kt == 0), stop=(kt == nkt - 1),
                        tile_position=(0, hrow)), w=w_z)
                    return eZ[kt]

                for kt in range(nkt):
                    sb_slot, s2_slot = SBK[kt % 3]
                    p4 = kt % 4
                    c0 = max(0, kt * 128 - gstart)
                    w_mm = [e_rope, e_vtok]
                    if kt >= 3:
                        w_mm.append(eP[kt - 3])
                    e_s = P.t(lambda e, koff=koff, qoff=qoff, kt=kt, oc=oc, c0=c0, sb_slot=sb_slot:
                              e.matmul(bank(c, sb_slot)[:, c0:512],
                                       bfA[0:64, koff + kt * 128: koff + kt * 128 + 128],
                                       bfA[0:64, qoff + oc * 512 + c0: qoff + oc * 512 + 512],
                                       start=True, stop=True, tile_position=(0, 0)), w=w_mm)
                    e_s2 = P.t(lambda e, koff=koff, qoff=qoff, kt=kt, oc=oc, c0=c0, s2_slot=s2_slot:
                               e.matmul(bank(c, s2_slot)[:, c0:512],
                                        bfA[64:128, koff + kt * 128: koff + kt * 128 + 128],
                                        bfA[64:128, qoff + oc * 512 + c0: qoff + oc * 512 + 512],
                                        start=True, stop=True, tile_position=(64, 0)))
                    w_e = [e_s2]
                    if kt >= 4:
                        w_e.append(eP[kt - 4])
                    e_ev2 = P.a(lambda e, p4=p4, c0=c0, s2_slot=s2_slot: e.copy(
                        f32A[:, c.o_s2sb + p4 * 512 + c0: c.o_s2sb + p4 * 512 + 512],
                        bank(c, s2_slot)[:, c0:512]), w=w_e)
                    w_p = [e_ev2, e_s]
                    if kt >= 4:
                        w_p.append(eZ[kt - 4])
                    e_p = P.v(lambda e, p4=p4, c0=c0, sb_slot=sb_slot: e.tensor_tensor(
                        bfA[:, c.o_Pbuf + p4 * 512 + c0: c.o_Pbuf + p4 * 512 + 512],
                        bank(c, sb_slot)[:, c0:512],
                        f32A[:, c.o_s2sb + p4 * 512 + c0: c.o_s2sb + p4 * 512 + 512],
                        ALU.mult), w=w_p)
                    if kt * 128 >= gstart:
                        e_p = P.v(lambda e, p4=p4, c0=c0: e.tensor_tensor(
                            bfA[:, c.o_Pbuf + p4 * 512 + c0: c.o_Pbuf + p4 * 512 + c0 + 128],
                            bfA[:, c.o_Pbuf + p4 * 512 + c0: c.o_Pbuf + p4 * 512 + c0 + 128],
                            c.trimask[:], ALU.mult), w=[e_tri, e_p])
                    eP[kt] = e_p
                    if kt >= AVD:
                        ez = emit_av(kt - AVD)
                for kt in range(max(0, nkt - AVD), nkt):
                    ez = emit_av(kt)
                zp = oc % 2
                w_zv = [ez]
                if zsp_ev.get(zp) is not None:
                    w_zv.append(zsp_ev[zp])
                e_zev = P.a(lambda e, hrow=hrow, zp=zp: e.copy(
                    bfA[hrow:hrow + 64, c.o_zbuf + zp * 512: c.o_zbuf + zp * 512 + 512],
                    bank(c, 4)[hrow:hrow + 64, 0:512]), w=w_zv)
                zev_prev[hrow] = e_zev
                e_sp = P.dma(lambda e, hp=hp, hrow=hrow, oc=oc, zp=zp: e.dma_start(
                    out=scr["z"][hp][hrow:hrow + 64, oc * 512:(oc + 1) * 512],
                    in_=bfA[hrow:hrow + 64, c.o_zbuf + zp * 512: c.o_zbuf + zp * 512 + 512]),
                    w=[e_zev], sem="dz%d" % zp)
                zsp_ev[zp] = e_sp
                e_pair_sp.append(e_sp)
        e_prev_pair = zev_prev[64]
        zspill.append((e_pair_sp[-2], e_pair_sp[-1]))
        if DEBUG and hp == 0:
            e_dbg = P.dma(lambda e: e.dma_start(out=io["dbg_p0"],
                                                in_=bfA[:, c.o_KmA:c.o_KmA + 16384]),
                          w=[e_rope, e_vtok], sem="dbg")
            e_prev_pair = None
            w_dbg_guard.append(e_dbg)

        if hp == 1:
            emit_sw_totals()
        if hp >= 2:
            for _ in range(11):
                emit_p2_unit()
    while _p2["u"] < len(p2_units):
        emit_p2_unit()

    e_zf = []
    for hp in range(NPAIR):
        e_zf.append(P.dma(lambda e, hp=hp: e.dma_start(
            out=bfA[:, c.o_zfull + hp * 2048: c.o_zfull + hp * 2048 + 2048], in_=scr["z"][hp]),
            w=list(zspill[hp]) + [e_wt_free], sem="dzf"))
    e_wp = []
    for hp in range(NPAIR):
        e_wp.append(P.dmag(lambda e, hp=hp: e.dma_start(
            out=bfA[:, c.o_WprojT + hp * 1024: c.o_WprojT + hp * 1024 + 1024],
            in_=wT["wprojT"][hp * 128:(hp + 1) * 128, :]), w=[e_prev_pair], sem="dwp"))

    c1_evs = []
    x1_spills = []
    prev_x1rd = {}
    prev_yadd = {}
    for tt in range(NT_OWN):
        par = tt % 2
        x1t = c.x1stg[:, par * 1024:(par + 1) * 1024]
        grow = OC[tt // 4] * 512 + (tt % 4) * 128
        w_x = list(prev_x1rd.get(par, ()))
        e_x = P.dma(lambda e, grow=grow, par=par: e.dma_start(
            out=f32A[:, c.o_xstg + par * 1024: c.o_xstg + par * 1024 + 1024],
            in_=x[grow:grow + 128, :]), w=w_x, sem="dy%d" % par)
        ev_oc = []
        for oc2 in range(2):
            yslot = 2 + oc2
            e_mm = None
            for hp in range(NPAIR):
                w_mm = []
                if hp == 0:
                    w_mm = list(e_zf) + list(e_wp)
                    if oc2 in prev_yadd:
                        w_mm.append(prev_yadd[oc2])
                e_mm = P.t(lambda e, hp=hp, tt=tt, oc2=oc2, yslot=yslot: e.matmul(
                    bank(c, yslot),
                    bfA[:, c.o_zfull + hp * 2048 + tt * 128: c.o_zfull + hp * 2048 + tt * 128 + 128],
                    bfA[:, c.o_WprojT + hp * 1024 + oc2 * 512: c.o_WprojT + hp * 1024 + oc2 * 512 + 512],
                    start=(hp == 0), stop=(hp == 7)), w=w_mm)
            e_add = P.v(lambda e, oc2=oc2, yslot=yslot, x1t=x1t, par=par: e.tensor_add(
                x1t[:, oc2 * 512:(oc2 + 1) * 512], bank(c, yslot),
                f32A[:, c.o_xstg + par * 1024 + oc2 * 512: c.o_xstg + par * 1024 + oc2 * 512 + 512]),
                w=[e_mm, e_x] + w_x)
            prev_yadd[oc2] = e_add
            ev_oc.append(e_add)
        e_sp = P.dma(lambda e, tt=tt, x1t=x1t: e.dma_start(out=scr["x1"][tt], in_=x1t), w=ev_oc,
                     sem="dsp%d" % (tt % 2))
        x1_spills.append(e_sp)

        d2 = f32A[:, c.o_d2:c.o_d2 + 1024]
        w_cols = [c1_evs[tt - 1]] if tt >= 1 else [e_C]
        P.v(lambda e, x1t=x1t: e.reduce_sum(c.cols[:, 30:31], x1t, AX.X), w=ev_oc + w_cols, drain=True)
        e_nm = P.v(lambda e: e.tensor_scalar(c.cols[:, 31:32], c.cols[:, 30:31], -1.0 / D, None,
                                             ALU.mult), drain=True)
        e_d2 = P.a(lambda e, x1t=x1t, d2=d2: e.activation(d2, x1t, AF.Square, bias=c.cols[:, 31:32],
                                                          scale=1.0), w=[e_nm], drain=True)
        P.v(lambda e, d2=d2: e.reduce_sum(c.cols[:, 32:33], d2, AX.X), w=[e_d2], drain=True)
        e_vp = P.v(lambda e: e.tensor_scalar(c.cols[:, 32:33], c.cols[:, 32:33], 1.0 / D, LN_EPS,
                                             ALU.mult, ALU.add), drain=True)
        e_sq = P.a(lambda e: e.activation(c.cols[:, 33:34], c.cols[:, 32:33], AF.Sqrt, bias=0.0,
                                          scale=1.0), w=[e_vp], drain=True)
        P.v(lambda e: e.reciprocal(c.cols[:, 34:35], c.cols[:, 33:34]), w=[e_sq], drain=True)
        for _ in range(2):
            P.v(lambda e: e.tensor_mul(c.cols[:, 35:36], c.cols[:, 34:35], c.cols[:, 34:35]), drain=True)
            P.v(lambda e: e.tensor_mul(c.cols[:, 35:36], c.cols[:, 35:36], c.cols[:, 32:33]), drain=True)
            P.v(lambda e: e.tensor_scalar(c.cols[:, 35:36], c.cols[:, 35:36], -0.5, 1.5, ALU.mult,
                                          ALU.add), drain=True)
            P.v(lambda e: e.tensor_mul(c.cols[:, 34:35], c.cols[:, 34:35], c.cols[:, 35:36]), drain=True)
        e_rstd = P.barrier("vector")
        e_ab = P.a(lambda e, x1t=x1t, d2=d2: e.activation(d2, x1t, AF.Abs, bias=c.cols[:, 31:32],
                                                          scale=1.0), w=[e_rstd], drain=True)
        P.v(lambda e, d2=d2: e.reduce_max(c.cols[:, 36:37], d2, AX.X), w=[e_ab], drain=True)
        P.v(lambda e: e.tensor_mul(c.cols[:, 36:37], c.cols[:, 36:37], c.cols[:, 34:35]), drain=True)
        P.v(lambda e: e.tensor_scalar_max(c.cols[:, 36:37], c.cols[:, 36:37], Q_EPS), drain=True)
        P.v(lambda e: e.reciprocal(c.cols[:, 37:38], c.cols[:, 36:37]), drain=True)
        P.v(lambda e: e.tensor_scalar(c.cols[:, 37:38], c.cols[:, 37:38], 127.0, None, ALU.mult),
            drain=True)
        P.v(lambda e: e.tensor_mul(c.cols[:, 38:39], c.cols[:, 37:38], c.cols[:, 34:35]), drain=True)
        e_nq = P.v(lambda e: e.tensor_mul(c.cols[:, 39:40], c.cols[:, 31:32], c.cols[:, 38:39]),
                   drain=True)
        P.v(lambda e: e.tensor_mul(c.cols[:, 40:41], c.cols[:, 37:38], c.swg[:]),
            w=[sw_events["wgT"]], drain=True)
        P.v(lambda e, tt=tt: e.reciprocal(c.r1g[:, tt:tt + 1], c.cols[:, 40:41]), drain=True)
        P.v(lambda e: e.tensor_mul(c.cols[:, 40:41], c.cols[:, 37:38], c.swu[:]),
            w=[sw_events["wuT"]], drain=True)
        P.v(lambda e, tt=tt: e.reciprocal(c.r1u[:, tt:tt + 1], c.cols[:, 40:41]), drain=True)
        e_dq = P.a(lambda e, x1t=x1t, d2=d2: e.activation(d2, x1t, AF.Identity, bias=c.cols[:, 39:40],
                                                          scale=c.cols[:, 38:39]), w=[e_nq], drain=True)
        prev_x1rd[par] = (e_dq, e_sp)
        e_r1a = P.v(lambda e, d2=d2: e.tensor_scalar(d2, d2, c.Ccol[:], None, ALU.add), w=[e_dq])
        e_rnd = P.v(lambda e, d2=d2: e.tensor_scalar(d2, d2, c.nCcol[:], None, ALU.add), w=[e_r1a])
        tr = []
        e_tr = None
        for fc in range(8):
            tslot = fc % 2
            w_tr = [e_rnd, e_id]
            if fc >= 2:
                w_tr.append(tr[fc - 2])
            elif tt >= 1:
                w_tr.append(c1_evs[tt - 1])
            e_tr = P.t(lambda e, fc=fc, tslot=tslot, d2=d2: e.transpose(
                bank(c, tslot)[:, 0:128], d2[:, fc * 128:(fc + 1) * 128], c.ident[:]), w=w_tr)
            tr.append(P.a(lambda e, fc=fc, tslot=tslot, tt=tt: e.copy(
                bfA[:, c.o_c1T + fc * 2048 + tt * 128: c.o_c1T + fc * 2048 + tt * 128 + 128],
                bank(c, tslot)[:, 0:128]), w=[e_tr]))
        c1_evs.append(tr[-1])
    e_c1T = c1_evs[-1]
    if DEBUG:
        P.dma(lambda e: e.dma_start(out=io["dbg_c1T"], in_=bfA[:, c.o_c1T:c.o_c1T + 16384]),
              w=[e_c1T], sem="dbg")


    out_dmas = []
    prev_icc = {}
    prev_dstream = {}
    prev_eo = {}
    prev_gate_rd = {}
    prev_up_rd = {}
    e_prev_downevac = None
    for ss in range(NT_OWN // SS):
        tts = list(range(ss * SS, (ss + 1) * SS))
        gu_evs = {tt: [] for tt in tts}
        prev_gu_gate = {}
        prev_up = None
        for icc in range(8):
            par = icc % 2
            iccg = bfA[:, c.o_icc + par * 4096: c.o_icc + par * 4096 + 4096]
            iccu = bfA[:, c.o_iccu + par * 4096: c.o_iccu + par * 4096 + 4096]
            lg = []
            for fc in range(8):
                lg.append(P.dma(lambda e, fc=fc, icc=icc, iccg=iccg: e.dma_start(
                    out=iccg[:, fc * 512:(fc + 1) * 512],
                    in_=scr["cg"][fc][:, icc * 512:(icc + 1) * 512]),
                    w=([prev_icc.get(("g", par), e_c1T)] if fc == 0 else []) + [p2_ev["cg"][fc]],
                    sem="di%d" % par))
            lu = []
            for fc in range(8):
                lu.append(P.dma(lambda e, fc=fc, icc=icc, iccu=iccu: e.dma_start(
                    out=iccu[:, fc * 512:(fc + 1) * 512],
                    in_=scr["cu"][fc][:, icc * 512:(icc + 1) * 512]),
                    w=([prev_icc.get(("u", par), e_c1T)] if fc == 0 else []) + [p2_ev["cu"][fc]],
                    sem="du%d" % par))
            ev_gate = {}
            e_gmm = None
            for i, tt in enumerate(tts):
                gslot = i % 2
                e_mm = None
                for fc in range(8):
                    w_mm = []
                    if fc == 0:
                        w_mm = lg + [e_c1T]
                        if e_prev_downevac is not None:
                            w_mm.append(e_prev_downevac)
                        if prev_gate_rd.get(gslot) is not None:
                            w_mm.append(prev_gate_rd[gslot])
                    e_mm = P.t(lambda e, fc=fc, tt=tt, gslot=gslot, iccg=iccg: e.matmul(
                        bank(c, gslot),
                        bfA[:, c.o_c1T + fc * 2048 + tt * 128: c.o_c1T + fc * 2048 + tt * 128 + 128],
                        iccg[:, fc * 512:(fc + 1) * 512], start=(fc == 0), stop=(fc == 7)), w=w_mm)
                w_g = [e_mm]
                if tt in prev_gu_gate:
                    w_g.append(prev_gu_gate[tt])
                ev_gate[tt] = P.a(lambda e, tt=tt, i=i, gslot=gslot: e.activation(
                    f32A[:, c.o_gates + i * 512: c.o_gates + i * 512 + 512], bank(c, gslot),
                    AF.Silu, bias=0.0, scale=c.r1g[:, tt:tt + 1]), w=w_g)
                prev_gate_rd[gslot] = ev_gate[tt]
                e_gmm = e_mm
            prev_icc[("g", par)] = e_gmm
            e_umm = None
            for i, tt in enumerate(tts):
                uslot = 2 + i % 2
                e_mm = None
                for fc in range(8):
                    w_mm = []
                    if fc == 0:
                        w_mm = lu
                        if e_prev_downevac is not None:
                            w_mm.append(e_prev_downevac)
                        if prev_up_rd.get(uslot) is not None:
                            w_mm.append(prev_up_rd[uslot])
                    e_mm = P.t(lambda e, fc=fc, tt=tt, uslot=uslot, iccu=iccu: e.matmul(
                        bank(c, uslot),
                        bfA[:, c.o_c1T + fc * 2048 + tt * 128: c.o_c1T + fc * 2048 + tt * 128 + 128],
                        iccu[:, fc * 512:(fc + 1) * 512], start=(fc == 0), stop=(fc == 7)), w=w_mm)
                w_u = [e_mm]
                if prev_up is not None:
                    w_u.append(prev_up)
                e_u = P.a(lambda e, tt=tt, uslot=uslot: e.activation(
                    f32A[:, c.o_ups: c.o_ups + 512], bank(c, uslot),
                    AF.Copy, bias=0.0, scale=c.r1u[:, tt:tt + 1]), w=w_u)
                prev_up_rd[uslot] = e_u
                w_gu = [e_u, ev_gate[tt]]
                if ss == 0 and icc == 0 and i == 0:
                    w_gu += [_p2["prevst"][0], _p2["prevst"][1]]
                e_gu = P.v(lambda e, tt=tt, icc=icc, i=i: e.tensor_tensor(
                    f32A[:, c.o_gu + i * 4096 + icc * 512: c.o_gu + i * 4096 + icc * 512 + 512],
                    f32A[:, c.o_gates + i * 512: c.o_gates + i * 512 + 512],
                    f32A[:, c.o_ups: c.o_ups + 512], ALU.mult), w=w_gu)
                prev_up = e_gu
                prev_gu_gate[tt] = e_gu
                gu_evs[tt].append(e_gu)
                e_umm = e_mm
            prev_icc[("u", par)] = e_umm
        c2_evs = {}
        for i, tt in enumerate(tts):
            gut = f32A[:, c.o_gu + i * 4096: c.o_gu + i * 4096 + 4096]
            w_dbgu = []
            if DEBUG and tt == 0:
                w_dbgu.append(P.dma(lambda e, gut=gut: e.dma_start(out=io["dbg_gu"], in_=gut),
                                    w=gu_evs[tt], sem="dbg"))
            P.v(lambda e, gut=gut: e.reduce_max(c.cols[:, 44:45], gut, AX.X,
                                                apply_absolute_value=True),
                w=gu_evs[tt] + w_dbgu, drain=True)
            P.v(lambda e: e.tensor_scalar_max(c.cols[:, 44:45], c.cols[:, 44:45], Q_EPS), drain=True)
            P.v(lambda e: e.reciprocal(c.cols[:, 45:46], c.cols[:, 44:45]), drain=True)
            e_s2 = P.v(lambda e: e.tensor_scalar(c.cols[:, 45:46], c.cols[:, 45:46], 127.0, None,
                                                 ALU.mult), drain=True)
            P.v(lambda e: e.tensor_mul(c.cols[:, 46:47], c.cols[:, 45:46], c.swd[:]),
                w=[sw_events["wdT"]], drain=True)
            e_rd = P.v(lambda e, tt=tt: e.reciprocal(c.rdc[:, tt:tt + 1], c.cols[:, 46:47]),
                       drain=True)
            e_cs = P.v(lambda e, tt=tt: e.tensor_scalar(c.c2s[:, tt:tt + 1], c.cols[:, 45:46],
                                                        1.0, None, ALU.mult), w=[e_rd], drain=True)
            e_a1 = P.a(lambda e, gut=gut, tt=tt: e.activation(
                gut, gut, AF.Identity, bias=c.Ccol[:], scale=c.c2s[:, tt:tt + 1]), w=[e_cs])
            e_rnd = P.a(lambda e, gut=gut: e.activation(
                gut, gut, AF.Identity, bias=c.nCcol[:], scale=1.0), w=[e_a1])
            tr = []
            for ic in range(32):
                tslot = 4 + ic % 2
                w_tr = [e_rnd, e_id]
                if ic >= 2:
                    w_tr.append(tr[ic - 2])
                elif i >= 1:
                    w_tr.append(c2_evs[tts[i - 1]])
                elif e_prev_downevac is not None:
                    w_tr.append(e_prev_downevac)
                e_tr = P.t(lambda e, ic=ic, tslot=tslot, gut=gut: e.transpose(
                    bank(c, tslot)[:, 0:128], gut[:, ic * 128:(ic + 1) * 128], c.ident[:]), w=w_tr)
                tr.append(P.a(lambda e, ic=ic, i=i, tslot=tslot: e.copy(
                    bfA[:, c.o_c2T + i * 4096 + ic * 128: c.o_c2T + i * 4096 + ic * 128 + 128],
                    bank(c, tslot)[:, 0:128]), w=[e_tr]))
            c2_evs[tt] = tr[-1]
        down_mm = {}
        for ic in range(32):
            par = ic % 4
            cdb = bfA[:, c.o_cdblk + par * 1024: c.o_cdblk + par * 1024 + 1024]
            w_l = [prev_dstream.get(par), p2_ev["cd"][ic]]
            e_cd = P.dma(lambda e, ic=ic, cdb=cdb: e.dma_start(out=cdb, in_=scr["cd"][ic]), w=w_l,
                         sem="dd%d" % par)
            for foc in range(2):
                for i, tt in enumerate(tts):
                    dslot = i * 2 + foc
                    w_mm = [e_cd]
                    if ic == 0:
                        w_mm += [c2_evs[tts[-1]]]
                    down_mm[(ic, foc, i)] = P.t(lambda e, ic=ic, foc=foc, i=i, dslot=dslot,
                                                cdb=cdb: e.matmul(
                        bank(c, dslot),
                        bfA[:, c.o_c2T + i * 4096 + ic * 128: c.o_c2T + i * 4096 + ic * 128 + 128],
                        cdb[:, foc * 512:(foc + 1) * 512], start=(ic == 0), stop=(ic == 31)),
                        w=w_mm)
            prev_dstream[par] = down_mm[(ic, 1, SS - 1)]
        e_last_eo = None
        for i, tt in enumerate(tts):
            x1r = f32A[:, c.o_x1rb: c.o_x1rb + 1024]
            w_xr = [x1_spills[-1], x1_spills[-2]]
            if out_dmas:
                w_xr.append(out_dmas[-1])
            e_xr = P.dma(lambda e, tt=tt, x1r=x1r: e.dma_start(out=x1r, in_=scr["x1"][tt]), w=w_xr,
                         sem="dr")
            for foc in range(2):
                dslot = i * 2 + foc
                w_xo = [down_mm[(31, foc, i)], x1_spills[-1], x1_spills[-2]]
                if (foc,) in prev_eo:
                    w_xo.append(prev_eo[(foc,)])
                e_xo = P.a(lambda e, dslot=dslot, tt=tt, foc=foc: e.activation(
                    c.x1stg[:, foc * 512: foc * 512 + 512], bank(c, dslot),
                    AF.Copy, bias=0.0, scale=c.rdc[:, tt:tt + 1]), w=w_xo)
                w_o = [e_xo, e_xr]
                if len(out_dmas) >= 2:
                    w_o.append(out_dmas[-2])
                e_o = P.v(lambda e, foc=foc: e.tensor_add(
                    c.x1stg[:, 1024 + foc * 512: 1024 + foc * 512 + 512],
                    c.x1stg[:, foc * 512: foc * 512 + 512],
                    f32A[:, c.o_x1rb + foc * 512: c.o_x1rb + foc * 512 + 512]), w=w_o)
                prev_eo[(foc,)] = e_o
                out_dmas.append(P.dma(lambda e, tt=tt, foc=foc: e.dma_start(
                    out=out[tt * 128:(tt + 1) * 128, foc * 512:(foc + 1) * 512],
                    in_=c.x1stg[:, 1024 + foc * 512: 1024 + foc * 512 + 512]), w=[e_o],
                    sem="do%d" % foc))
                e_last_eo = e_o
        e_prev_downevac = e_last_eo

    P.items["sync"].append((None, tuple((sname, tot) for sname, tot in P.cnt.items()), None, 0, False))
    return P


def _fix_dma_waits(P):
    for ring_sem, ring_eng in (("ds", "sync"), ("dg", "gpsimd")):
        bounds = []
        tot = 0
        for fn, waits, sem, amt, drain in P.items[ring_eng]:
            if waits:
                bounds.append(tot)
            if sem == ring_sem:
                tot += amt
        bounds.append(tot)
        import bisect
        def remap(v, bounds=bounds):
            i = bisect.bisect_left(bounds, v)
            return bounds[i] if i < len(bounds) else v
        for eng in ENGINES:
            new = []
            for fn, waits, sem, amt, drain in P.items[eng]:
                nw = tuple((s2, remap(v) if s2 == ring_sem else v) for (s2, v) in waits)
                new.append((fn, nw, sem, amt, drain))
            P.items[eng] = new


def build():
    nc = bass.Bass("TRN2", target_bir_lowering=False)
    io = {}
    io["x"] = nc.dram_tensor("x", [T, D], F32, kind="ExternalInput").ap()
    wT = {}
    for wn, shp in (("wqT", [D, D]), ("wkT", [D, D]), ("wq2T", [D, D]), ("wk2T", [D, D]),
                    ("wvT", [D, D]), ("wprojT", [D, D]), ("wgT", [D, INNER]), ("wuT", [D, INNER]),
                    ("wdT", [INNER, D])):
        wT[wn] = nc.dram_tensor(wn, shp, F32, kind="ExternalInput").ap()
    io["wT"] = wT
    io["consts"] = nc.dram_tensor("consts", [128, 4], F32, kind="ExternalInput").ap()
    io["onesrow"] = nc.dram_tensor("onesrow", [1, 128], F32, kind="ExternalInput").ap()
    io["trimask"] = nc.dram_tensor("trimask", [128, 128], BF16, kind="ExternalInput").ap()
    io["pswap"] = nc.dram_tensor("pswap", [128, 128], BF16, kind="ExternalInput").ap()
    io["ident"] = nc.dram_tensor("ident", [128, 128], F32, kind="ExternalInput").ap()
    io["flagin"] = nc.dram_tensor("flag", [1, 1], I32, kind="ExternalInput").ap()
    io["ropetab"] = nc.dram_tensor("ropetab", [128, 8192], F32, kind="ExternalInput").ap()
    io["out"] = nc.dram_tensor("out", [N_OWN, D], F32, kind="ExternalOutput").ap()
    kind = "ExternalOutput"
    scr = {}
    scr["z"] = [nc.dram_tensor(f"scz{i}", [128, 2048], BF16, kind=kind).ap() for i in range(NPAIR)]
    scr["cg"] = [nc.dram_tensor(f"scg{i}", [128, INNER], BF16, kind=kind).ap() for i in range(8)]
    scr["cu"] = [nc.dram_tensor(f"scu{i}", [128, INNER], BF16, kind=kind).ap() for i in range(8)]
    scr["cd"] = [nc.dram_tensor(f"scd{i}", [128, D], BF16, kind=kind).ap() for i in range(32)]
    scr["x1"] = [nc.dram_tensor(f"sx1{i}", [128, D], F32, kind=kind).ap() for i in range(NT_OWN)]
    io["scr"] = scr
    if DEBUG:
        io["dbg_tab"] = nc.dram_tensor("dbg_tab", [128, 8192], F32, kind="ExternalOutput").ap()
        io["dbg_hnT"] = nc.dram_tensor("dbg_hnT", [128, 32768], BF16, kind="ExternalOutput").ap()
        io["dbg_p0"] = nc.dram_tensor("dbg_p0", [128, 16384], BF16, kind="ExternalOutput").ap()
        io["dbg_c1T"] = nc.dram_tensor("dbg_c1T", [128, 16384], BF16, kind="ExternalOutput").ap()
        io["dbg_gu"] = nc.dram_tensor("dbg_gu", [128, 4096], F32, kind="ExternalOutput").ap()
        io["dbg_cols"] = nc.dram_tensor("dbg_cols", [128, 41], F32, kind="ExternalOutput").ap()

    with ExitStack() as stack:
        c = alloc_tensors(nc, stack)
        progs = [build_half(nc, c, Prog(), h, io) for h in (0, 1)]
        sem_names = sorted(set(list(progs[0].cnt) + list(progs[1].cnt))) + ["fl"]
        sems = {s: stack.enter_context(nc.semaphore(s)) for s in sem_names}
        block = stack.enter_context(nc.Block())

        def replay(eng, items):
            waited = {}
            self_cnt = {}
            for fn, waits, sem, amt, drain in items:
                need = {}
                for (s, v) in waits:
                    if v > 0 and v > waited.get(s, 0):
                        need[s] = max(need.get(s, 0), v)
                for s, v in need.items():
                    eng.wait_ge(sems[s], v)
                    waited[s] = v
                if fn is None:
                    continue
                inst = fn(eng)
                if sem is not None:
                    inst.then_inc(sems[sem], amt)
                if drain and sem is not None:
                    eng.wait_ge(sems[sem], self_cnt.get(sem, 0) + amt)
                if sem is not None:
                    self_cnt[sem] = self_cnt.get(sem, 0) + amt
                    waited.setdefault(sem, 0)
                    if drain:
                        waited[sem] = max(waited[sem], self_cnt[sem])

        def section(eng_name):
            def body(eng):
                if eng_name == "sync":
                    eng.dma_start(out=c.flag[:], in_=io["flagin"]).then_inc(sems["fl"], 16)
                eng.wait_ge(sems["fl"], 16)
                with eng.register("rf_" + eng_name) as rf:
                    eng.load(rf, c.flag[0:1, 0:1])
                    with eng.If_eq(rf, 0):
                        replay(eng, progs[0].items[eng_name])
                    with eng.If_eq(rf, 1):
                        replay(eng, progs[1].items[eng_name])
            return body

        block.sync(section("sync"))
        block.tensor(section("tensor"))
        block.vector(section("vector"))
        block.scalar(section("scalar"))
        block.gpsimd(section("gpsimd"))
    return nc


def _rope_tables_np():
    try:
        import jax
        import jax.numpy as jnp
        cpu = jax.devices("cpu")[0]
        with jax.default_device(cpu):
            inv_freq = 1.0 / 10000.0 ** (jnp.arange(0, 64, 2, dtype=jnp.float32) / 64)
            t = jnp.arange(T, dtype=jnp.float32)
            freqs = jnp.outer(t, inv_freq)
            cos = np.asarray(jnp.cos(freqs), dtype=np.float32)
            sin = np.asarray(jnp.sin(freqs), dtype=np.float32)
    except Exception:
        e = np.arange(0, 64, 2, dtype=np.float32) / np.float32(64)
        inv = (1.0 / (10000.0 ** e.astype(np.float64))).astype(np.float32)
        freqs = np.outer(np.arange(T, dtype=np.float32), inv).astype(np.float32)
        cos, sin = np.cos(freqs).astype(np.float32), np.sin(freqs).astype(np.float32)
    r = np.arange(128)
    sgn = np.where(r % 64 < 32, 1.0, -1.0).astype(np.float32)
    cosT = cos.T[r % 32]
    sinS = sin.T[r % 32] * sgn[:, None]
    return np.ascontiguousarray(np.concatenate([cosT, sinS], axis=1))


def _consts_np():
    r = np.arange(128)
    consts = np.stack([np.ones(128), np.full(128, 2.0 ** -30), (r % 32).astype(float),
                       np.where(r % 64 < 32, 1.0, -1.0)], axis=1).astype(np.float32)
    onesrow = np.ones((1, 128), np.float32)
    tri = np.triu(np.ones((128, 128))).astype(ml_dtypes.bfloat16)
    psw = np.zeros((128, 128), ml_dtypes.bfloat16)
    for j in range(128):
        psw[j ^ 32, j] = 1.0
    ident = np.eye(128, dtype=np.float32)
    return consts, onesrow, tri, psw, ident


def kernel(L0_attn_x, L0_attn_Wq, L0_attn_Wk, L0_attn_Wq2, L0_attn_Wk2, L0_attn_Wv,
           L0_attn_Wproj, L0_ffn_w_gate, L0_ffn_w_up, L0_ffn_w_down):
    x = np.ascontiguousarray(np.asarray(L0_attn_x, dtype=np.float32))
    wTs = {
        "wqT": L0_attn_Wq, "wkT": L0_attn_Wk, "wq2T": L0_attn_Wq2, "wk2T": L0_attn_Wk2,
        "wvT": L0_attn_Wv, "wprojT": L0_attn_Wproj, "wgT": L0_ffn_w_gate,
        "wuT": L0_ffn_w_up, "wdT": L0_ffn_w_down,
    }
    wTs = {k: np.ascontiguousarray(np.asarray(v, dtype=np.float32).T) for k, v in wTs.items()}
    consts, onesrow, tri, psw, ident = _consts_np()
    ropetab = _rope_tables_np()

    nc = build()
    in_maps = []
    for core in range(8):
        b, half = core // 2, core % 2
        m = {"x": np.ascontiguousarray(x[b]),
             "consts": consts, "onesrow": onesrow, "trimask": tri, "pswap": psw,
             "ident": ident, "flag": np.full((1, 1), half, np.int32), "ropetab": ropetab}
        m.update(wTs)
        in_maps.append(m)
    res = run_bass_kernel_spmd(nc, in_maps, core_ids=list(range(8)))
    full = np.empty((4, T, D), np.float32)
    for core in range(8):
        b, half = core // 2, core % 2
        o = res.results[core]["out"]
        if half == 0:
            full[b, 0:1024] = o[0:1024]
            full[b, 3072:4096] = o[1024:2048]
        else:
            full[b, 1024:3072] = o
    return full


# version 12
# speedup vs baseline: 1.6905x; 1.0239x over previous
import numpy as np
import ml_dtypes
from contextlib import ExitStack

import concourse.bass as bass
import concourse.mybir as mybir
from concourse.bass_utils import run_bass_kernel_spmd

F32 = mybir.dt.float32
BF16 = mybir.dt.bfloat16
I32 = mybir.dt.int32
ALU = mybir.AluOpType
AF = mybir.ActivationFunctionType
AX = mybir.AxisListType

D = 1024
T = 4096
INNER = 4096
NPAIR = 8
N_OWN = 2048
NT_OWN = 16
SS = 4
LN_EPS = 1e-5
Q_EPS = 1e-5
RNDC = 1.5 * 2.0 ** 23
RNDK = 1.5 * 2.0 ** 52 - 1.5 * 2.0 ** 23

ENGINES = ("sync", "tensor", "vector", "scalar", "gpsimd")
DEBUG = False


def own_chunks(half):
    return [0, 1, 6, 7] if half == 0 else [2, 3, 4, 5]


def kv_tiles(half):
    return 32 if half == 0 else 24


class Prog:
    def __init__(self):
        self.items = {e: [] for e in ENGINES}
        self.cnt = {}

    def ev(self, sem):
        return (sem, self.cnt.get(sem, 0))

    def _add(self, eng, fn, waits, sem, amt, drain):
        self.cnt[sem] = self.cnt.get(sem, 0) + amt
        evt = (sem, self.cnt[sem])
        self.items[eng].append((fn, tuple(w for w in waits if w is not None), sem, amt, drain))
        return evt

    def dma(self, fn, w=(), sem="ds"):
        return self._add("sync", fn, w, sem, 16, False)

    def dmag(self, fn, w=(), sem="dg"):
        return self._add("gpsimd", fn, w, sem, 16, False)

    def t(self, fn, w=(), drain=False):
        return self._add("tensor", fn, w, "t", 1, drain)

    def v(self, fn, w=(), drain=False):
        return self._add("vector", fn, w, "v", 1, drain)

    def a(self, fn, w=(), drain=False):
        return self._add("scalar", fn, w, "a", 1, drain)

    def g(self, fn, w=(), drain=False):
        return self._add("gpsimd", fn, w, "g", 1, drain)

    def barrier(self, eng):
        if eng == "vector":
            return self.v(lambda e: e.engine_nop())
        if eng == "scalar":
            return self.a(lambda e: e.engine_nop())
        raise ValueError(eng)


class Ctx:
    pass


def alloc_tensors(nc, stack):
    c = Ctx()
    sb = lambda n, s, d: stack.enter_context(nc.sbuf_tensor(n, s, d))
    c.bfA = sb("bfA", [128, 60416], BF16)
    c.f32A = sb("f32A", [128, 19968], F32)
    c.ps = stack.enter_context(nc.psum_tensor("ps", [128, 4096], F32))

    c.o_hnT = 0
    c.o_KmA = 32768
    c.o_KmB = 36864
    c.o_Vtok = 40960
    c.o_QmA = 45056
    c.o_QmB = 47104
    c.o_WTqA = 49152
    c.o_WTqB = 50176
    c.o_WTkA = 51200
    c.o_WTkB = 52224
    c.o_WTv = 53248
    c.o_Xb = 54272
    c.o_Pbuf = 55296
    c.o_zbuf = 57344
    c.o_WTkP = 58368
    c.o_zfull = 0
    c.o_WprojT = 16384
    c.o_c1T = 24576
    c.o_icc = 0
    c.o_iccu = 40960
    c.o_c2T = 8192
    c.o_cdblk = 49152

    c.o_cosT = 0
    c.o_sinS = 4096
    c.o_xstg = 8192
    c.o_wstg = 10240
    c.o_d2 = 14336
    c.o_Xf = 15360
    c.o_tA = 16384
    c.o_s2sb = 17408
    c.o_gu = 0
    c.o_x1rb = 16384
    c.o_gates = 17408
    c.o_ups = 19456

    c.cols = sb("cols", [128, 64], F32)
    c.c2s = sb("c2s", [128, 16], F32)
    c.r1g = sb("r1g", [128, 16], F32)
    c.r1u = sb("r1u", [128, 16], F32)
    c.rdc = sb("rdc", [128, 16], F32)
    c.Ccol = sb("Ccol", [128, 1], F32)
    c.nCcol = sb("nCcol", [128, 1], F32)
    c.swg = sb("swg", [128, 1], F32)
    c.swu = sb("swu", [128, 1], F32)
    c.swd = sb("swd", [128, 1], F32)
    c.acc = sb("acc", [128, 80], F32)
    c.consts = sb("constsb", [128, 4], F32)
    c.onesrow = sb("onesrowb", [1, 128], F32)
    c.trimask = sb("trimaskb", [128, 128], BF16)
    c.pswap = sb("pswapb", [128, 128], BF16)
    c.ident = sb("identb", [128, 128], F32)
    c.flag = sb("flagsb", [1, 1], I32)
    c.x1stg = sb("x1stg", [128, 2048], F32)
    c.ssc = sb("ssc", [1, 1], F32)
    return c


def bank(c, i):
    return c.ps[:, i * 512:(i + 1) * 512]


def build_half(nc, c, P, half, io):
    OC = own_chunks(half)
    KT = kv_tiles(half)
    bfA, f32A, ps = c.bfA, c.f32A, c.ps
    x, out = io["x"], io["out"]
    wT = io["wT"]
    scr = io["scr"]

    ones = c.consts[:, 0:1]
    epsv = c.consts[:, 1:2]
    jcol = c.consts[:, 2:3]
    sgn = c.consts[:, 3:4]
    cosT = f32A[:, c.o_cosT:c.o_cosT + 4096]
    sinS = f32A[:, c.o_sinS:c.o_sinS + 4096]

    P.dma(lambda e: e.dma_start(out=c.consts[:, 0:4], in_=io["consts"]), sem="dc")
    P.dma(lambda e: e.dma_start(out=c.onesrow[:], in_=io["onesrow"]), sem="dc")
    P.dma(lambda e: e.dma_start(out=c.trimask[:], in_=io["trimask"]), sem="dc")
    P.dma(lambda e: e.dma_start(out=c.pswap[:], in_=io["pswap"]), sem="dc")
    P.dma(lambda e: e.dma_start(out=c.ident[:], in_=io["ident"]), sem="dc")
    e_consts = e_or = e_tri = e_psw = e_id = P.dma(
        lambda e: e.dma_start(out=f32A[:, 0:8192], in_=io["ropetab"]), sem="dc")

    P.v(lambda e: e.tensor_add(c.cols[:, 0:1], ones, epsv), w=[e_consts], drain=True)
    P.v(lambda e: e.tensor_sub(c.cols[:, 0:1], c.cols[:, 0:1], ones), drain=True)
    P.v(lambda e: e.tensor_scalar(c.cols[:, 0:1], c.cols[:, 0:1], 2.0 ** 30, None, ALU.mult), drain=True)
    P.v(lambda e: e.tensor_scalar(c.Ccol[:], c.cols[:, 0:1], RNDK, RNDC, ALU.mult, ALU.add), drain=True)
    e_C = P.v(lambda e: e.tensor_scalar(c.nCcol[:], c.Ccol[:], -1.0, None, ALU.mult), drain=True)

    e_tab = e_consts

    slot0 = {"wgT": 0, "wuT": 16, "wdT": 32}
    ccols = {"wgT": INNER, "wuT": INNER, "wdT": D}
    p1_units = ([("wgT", ck, hh) for ck in range(8) for hh in (0, 2048)]
                + [("wuT", ck, hh) for ck in range(8) for hh in (0, 2048)]
                + [("wdT", ck, 0) for ck in range(32)])
    p1_state = {"u": 0, "prev": {0: None, 1: None}, "last": None}

    def emit_p1_unit():
        if p1_state["u"] >= len(p1_units):
            return
        u = p1_state["u"]
        p1_state["u"] += 1
        wname, ck, hh = p1_units[u]
        hw = min(2048, ccols[wname] - hh)
        par = u % 2
        s = slot0[wname] + (ck * 2 + hh // 2048 if wname != "wdT" else ck)
        wst = f32A[:, c.o_wstg + par * 2048: c.o_wstg + par * 2048 + hw]
        w_l = [p1_state["prev"][par]] if p1_state["prev"][par] else []
        e_l = P.dma(lambda e, wname=wname, ck=ck, hh=hh, hw=hw, wst=wst: e.dma_start(
            out=wst, in_=wT[wname][ck * 128:(ck + 1) * 128, hh:hh + hw]), w=w_l,
            sem="dw%d" % par)
        e_r = P.v(lambda e, wst=wst, s=s: e.reduce_sum(
            c.acc[:, s:s + 1], wst, AX.X, apply_absolute_value=True), w=[e_l])
        p1_state["prev"][par] = e_r
        p1_state["last"] = e_r

    ln_done = []
    for gt in range(KT):
        par = gt % 2
        xst = f32A[:, c.o_xstg + par * 1024: c.o_xstg + (par + 1) * 1024]
        d2 = f32A[:, c.o_d2:c.o_d2 + 1024]
        w_x = [ln_done[gt - 2]] if gt >= 2 else []
        e_x = P.dma(lambda e, gt=gt, xst=xst: e.dma_start(out=xst, in_=x[gt * 128:(gt + 1) * 128, :]),
                    w=w_x, sem="dx%d" % (gt % 2))
        emit_p1_unit()
        emit_p1_unit()
        w_cols = [ln_done[gt - 1]] if gt >= 1 else [e_C]
        P.v(lambda e, xst=xst: e.reduce_sum(c.cols[:, 16:17], xst, AX.X), w=[e_x] + w_cols, drain=True)
        e_nm = P.v(lambda e: e.tensor_scalar(c.cols[:, 17:18], c.cols[:, 16:17], -1.0 / D, None, ALU.mult),
                   drain=True)
        e_d2 = P.a(lambda e, xst=xst, d2=d2: e.activation(d2, xst, AF.Square, bias=c.cols[:, 17:18],
                                                          scale=1.0), w=[e_nm], drain=True)
        P.v(lambda e, d2=d2: e.reduce_sum(c.cols[:, 18:19], d2, AX.X), w=[e_d2], drain=True)
        e_vp = P.v(lambda e: e.tensor_scalar(c.cols[:, 18:19], c.cols[:, 18:19], 1.0 / D, LN_EPS,
                                             ALU.mult, ALU.add), drain=True)
        e_sq = P.a(lambda e: e.activation(c.cols[:, 19:20], c.cols[:, 18:19], AF.Sqrt, bias=0.0,
                                          scale=1.0), w=[e_vp], drain=True)
        P.v(lambda e: e.reciprocal(c.cols[:, 20:21], c.cols[:, 19:20]), w=[e_sq], drain=True)
        for _ in range(2):
            P.v(lambda e: e.tensor_mul(c.cols[:, 21:22], c.cols[:, 20:21], c.cols[:, 20:21]), drain=True)
            P.v(lambda e: e.tensor_mul(c.cols[:, 21:22], c.cols[:, 21:22], c.cols[:, 18:19]), drain=True)
            P.v(lambda e: e.tensor_scalar(c.cols[:, 21:22], c.cols[:, 21:22], -0.5, 1.5, ALU.mult,
                                          ALU.add), drain=True)
            P.v(lambda e: e.tensor_mul(c.cols[:, 20:21], c.cols[:, 20:21], c.cols[:, 21:22]), drain=True)
        e_nmr = P.v(lambda e: e.tensor_mul(c.cols[:, 22:23], c.cols[:, 17:18], c.cols[:, 20:21]),
                    drain=True)
        e_hn = P.a(lambda e, xst=xst, d2=d2: e.activation(d2, xst, AF.Identity, bias=c.cols[:, 22:23],
                                                          scale=c.cols[:, 20:21]), w=[e_nmr], drain=True)
        evs = []
        for fc in range(8):
            slot = fc % 2
            w_tr = [e_hn, e_id]
            if fc >= 2:
                w_tr.append(evs[fc - 2])
            elif gt >= 1:
                w_tr.append(ln_done[gt - 1])
            e_tr = P.t(lambda e, fc=fc, slot=slot, d2=d2: e.transpose(
                bank(c, slot)[:, 0:128], d2[:, fc * 128:(fc + 1) * 128], c.ident[:]), w=w_tr)
            e_ev = P.a(lambda e, fc=fc, slot=slot, gt=gt: e.copy(
                bfA[:, c.o_hnT + fc * 4096 + gt * 128: c.o_hnT + fc * 4096 + gt * 128 + 128],
                bank(c, slot)[:, 0:128]), w=[e_tr])
            evs.append(e_ev)
        ln_done.append(evs[-1])
    e_hnT = ln_done[-1]

    if DEBUG:
        P.dma(lambda e: e.dma_start(out=io["dbg_tab"], in_=f32A[:, 0:8192]), w=[e_tab], sem="dbg")
        P.dma(lambda e: e.dma_start(out=io["dbg_hnT"], in_=bfA[:, 0:32768]), w=[e_hnT], sem="dbg")

    while p1_state["u"] < len(p1_units):
        emit_p1_unit()
    e_p1 = p1_state["last"]

    sw_events = {}

    def emit_sw_totals():
        for wname, swcol, nelem in (("wgT", c.swg, INNER * D), ("wuT", c.swu, INNER * D),
                                    ("wdT", c.swd, D * INNER)):
            s0, nck = slot0[wname], {"wgT": 16, "wuT": 16, "wdT": 32}[wname]
            e_r = P.v(lambda e, s0=s0, nck=nck: e.reduce_sum(c.cols[:, 24:25],
                                                             c.acc[:, s0:s0 + nck], AX.X),
                      w=[e_p1], drain=True)
            e_m1 = P.t(lambda e: e.matmul(ps[0:1, 1024:1025], c.cols[:, 24:25], ones,
                                          start=True, stop=True), w=[e_r, e_consts, P.ev("v")])
            e_e1 = P.a(lambda e: e.copy(c.ssc[:], ps[0:1, 1024:1025]), w=[e_m1], drain=True)
            e_m2 = P.t(lambda e: e.matmul(ps[:, 1025:1026], c.onesrow[:], c.ssc[:],
                                          start=True, stop=True), w=[e_e1, e_or])
            e_e2 = P.a(lambda e: e.copy(c.cols[:, 25:26], ps[:, 1025:1026]), w=[e_m2], drain=True)
            P.v(lambda e, nelem=nelem: e.tensor_scalar(c.cols[:, 26:27], c.cols[:, 25:26],
                                                       1.0 / nelem, None, ALU.mult), w=[e_e2], drain=True)
            P.v(lambda e: e.tensor_scalar_max(c.cols[:, 26:27], c.cols[:, 26:27], Q_EPS), drain=True)
            sw_events[wname] = P.v(lambda e, swcol=swcol: e.reciprocal(swcol[:], c.cols[:, 26:27]),
                                   drain=True)

    p2_ev = {"cg": [None] * 8, "cu": [None] * 8, "cd": [None] * 32}
    _p2 = {"u": 0, "prevst": {0: e_p1, 1: e_p1}}
    p2_units = ([("wgT", ck, hh) for ck in range(8) for hh in (0, 2048)]
                + [("wuT", ck, hh) for ck in range(8) for hh in (0, 2048)]
                + [("wdT", ck, 0) for ck in range(32)])

    def emit_p2_unit():
        if _p2["u"] >= len(p2_units):
            return
        u = _p2["u"]
        _p2["u"] += 1
        wname, ck, hh = p2_units[u]
        hw = min(2048, ccols[wname] - hh)
        par = u % 2
        swcol = {"wgT": c.swg, "wuT": c.swu, "wdT": c.swd}[wname]
        scrkey = {"wgT": "cg", "wuT": "cu", "wdT": "cd"}[wname]
        wst = f32A[:, c.o_wstg + par * 2048: c.o_wstg + par * 2048 + hw]
        e_l = P.dma(lambda e, wname=wname, ck=ck, hh=hh, hw=hw, wst=wst: e.dma_start(
            out=wst, in_=wT[wname][ck * 128:(ck + 1) * 128, hh:hh + hw]),
            w=[_p2["prevst"][par]], sem="dw%d" % par)
        e_a1 = P.a(lambda e, wst=wst, swcol=swcol: e.activation(
            wst, wst, AF.Identity, bias=c.Ccol[:], scale=swcol[:]),
            w=[e_l, sw_events[wname], e_C])
        e_a2 = P.a(lambda e, wst=wst: e.activation(
            wst, wst, AF.Identity, bias=c.nCcol[:], scale=1.0), w=[e_a1])
        e_q = P.v(lambda e, wst=wst: e.tensor_scalar(wst, wst, -1.0, 1.0, ALU.max, ALU.min),
                  w=[e_a2])
        e_st = P.dmag(lambda e, wst=wst, scrkey=scrkey, ck=ck, hh=hh, hw=hw: e.dma_start(
            out=scr[scrkey][ck][:, hh:hh + hw], in_=wst), w=[e_q], sem="dq%d" % par)
        _p2["prevst"][par] = e_st
        p2_ev[scrkey][ck] = e_st

    rst = {}
    w_dbg_guard = []
    zspill = []
    zev_prev = {}
    zsp_ev = {}
    e_prev_pair = None
    e_wt_free = None
    def emit_kpref(hp):
        area = c.o_WTkA if hp % 2 == 0 else c.o_WTkP
        w_g = [karea_rd[hp % 2]] if karea_rd.get(hp % 2) is not None else []
        evs = []
        for sub in (0, 1):
            off = area + sub * 1024
            for wn, halfcol in (("wkT", 0), ("wk2T", 64)):
                def f(e, wn=wn, sub=sub, off=off, halfcol=halfcol, hp=hp):
                    src = wT[wn].rearrange("(fc p) o -> p fc o", p=128)[
                        :, :, hp * 128 + sub * 64: hp * 128 + sub * 64 + 64]
                    dst = bfA[:, off:off + 1024].rearrange("p (fc z) -> p fc z", z=128)[
                        :, :, halfcol:halfcol + 64]
                    return e.dma_start(out=dst, in_=src)
                evs.append(P.dmag(f, w=w_g, sem="dwk%d" % (hp % 2)))
        kpref_evs[hp] = (evs, area)

    karea_rd = {}
    kpref_evs = {}
    emit_kpref(0)

    for hp in range(NPAIR):
        w_guard = [e_wt_free] if e_wt_free is not None else []

        def fv(e, hp=hp):
            src = wT["wvT"].rearrange("(fc p) o -> p fc o", p=128)[:, :, hp * 128:(hp + 1) * 128]
            dst = bfA[:, c.o_WTv:c.o_WTv + 1024].rearrange("p (fc z) -> p fc z", z=128)
            return e.dma_start(out=dst, in_=src)
        e_wv = P.dmag(fv, w=w_guard, sem="dwv")
        wl = [e_wv]
        for sub, off in ((0, c.o_WTqA), (1, c.o_WTqB)):
            for wn, halfcol in (("wqT", 0), ("wq2T", 64)):
                def f(e, wn=wn, sub=sub, off=off, halfcol=halfcol, hp=hp):
                    src = wT[wn].rearrange("(fc p) o -> p fc o", p=128)[
                        :, :, hp * 128 + sub * 64: hp * 128 + sub * 64 + 64]
                    dst = bfA[:, off:off + 1024].rearrange("p (fc z) -> p fc z", z=128)[
                        :, :, halfcol:halfcol + 64]
                    return e.dma_start(out=dst, in_=src)
                wl.append(P.dmag(f, w=w_guard, sem="dwt"))
        P.v(lambda e: e.tensor_scalar(bfA[:, c.o_WTqA:c.o_WTqA + 1024],
                                      bfA[:, c.o_WTqA:c.o_WTqA + 1024], 1.0 / 64, None, ALU.mult),
            w=wl)
        e_wready = P.v(lambda e: e.tensor_scalar(bfA[:, c.o_WTqB:c.o_WTqB + 1024],
                                                 bfA[:, c.o_WTqB:c.o_WTqB + 1024], 1.0 / 64, None,
                                                 ALU.mult), w=wl)

        w_bufs = [e_prev_pair] if e_prev_pair is not None else []
        w_bufs += w_dbg_guard
        if e_prev_pair is None and hp > 0:
            w_bufs.extend(zspill[-1])

        vp_last = []
        for ch in range(KT // 4):
            par = ch % 2
            pslot = 5 + par
            e_mm = None
            for fc in range(8):
                w_mm = []
                if fc == 0:
                    w_mm = [e_hnT, e_wv, rst.get("ps%d" % par)]
                e_mm = P.t(lambda e, fc=fc, ch=ch, pslot=pslot: e.matmul(
                    bank(c, pslot), bfA[:, c.o_WTv + fc * 128:c.o_WTv + fc * 128 + 128],
                    bfA[:, c.o_hnT + fc * 4096 + ch * 512: c.o_hnT + fc * 4096 + ch * 512 + 512],
                    start=(fc == 0), stop=(fc == 7)), w=w_mm)
            e_ev = P.a(lambda e, pslot=pslot, ch=ch, par=par: e.copy(
                f32A[:, c.o_Xf + par * 512: c.o_Xf + par * 512 + 512], bank(c, pslot)),
                w=[e_mm, rst.get("xf%d" % par)])
            rst["ps%d" % par] = e_ev
            tr_ev = []
            e_tr = None
            for j in range(4):
                tslot = j % 2
                w_tr = [e_ev, e_id] + w_bufs
                if j >= 2:
                    w_tr.append(tr_ev[j - 2])
                elif ch >= 1:
                    w_tr.append(vp_last[ch - 1])
                e_tr = P.t(lambda e, j=j, ch=ch, tslot=tslot, par=par: e.transpose(
                    bank(c, tslot)[:, 0:128],
                    f32A[:, c.o_Xf + par * 512 + j * 128: c.o_Xf + par * 512 + j * 128 + 128],
                    c.ident[:]), w=w_tr)
                tr_ev.append(P.a(lambda e, j=j, ch=ch, tslot=tslot: e.copy(
                    bfA[:, c.o_Vtok + (ch * 4 + j) * 128: c.o_Vtok + (ch * 4 + j) * 128 + 128],
                    bank(c, tslot)[:, 0:128]), w=[e_tr]))
            rst["xf%d" % par] = e_tr
            vp_last.append(tr_ev[-1])
        e_vtok = vp_last[-1]

        def emit_rope(setoff, dstoff, colbases, w_set):
            evs = []
            for i, (dcb, hcb, tcb) in enumerate(colbases):
                par = i % 2
                pslot = 5 + par
                sslot = 7 if par == 0 else 4
                e_mm = None
                for fc in range(8):
                    w_mm = []
                    if fc == 0:
                        w_mm = list(w_set) + [e_hnT, rst.get("ps%d" % par)]
                    e_mm = P.t(lambda e, fc=fc, pslot=pslot, setoff=setoff, hcb=hcb: e.matmul(
                        bank(c, pslot), bfA[:, setoff + fc * 128: setoff + fc * 128 + 128],
                        bfA[:, c.o_hnT + fc * 4096 + hcb: c.o_hnT + fc * 4096 + hcb + 512],
                        start=(fc == 0), stop=(fc == 7)), w=w_mm)
                e_f = P.a(lambda e, par=par, pslot=pslot: e.copy(
                    f32A[:, c.o_Xf + par * 512: c.o_Xf + par * 512 + 512], bank(c, pslot)),
                    w=[e_mm, rst.get("xf%d" % par)])
                e_b = P.a(lambda e, par=par, pslot=pslot: e.copy(
                    bfA[:, c.o_Xb + par * 512: c.o_Xb + par * 512 + 512], bank(c, pslot)),
                    w=[rst.get("xb%d" % par)])
                rst["ps%d" % par] = e_b
                e_sw = P.t(lambda e, par=par, sslot=sslot: e.matmul(
                    bank(c, sslot), c.pswap[:],
                    bfA[:, c.o_Xb + par * 512: c.o_Xb + par * 512 + 512],
                    start=True, stop=True), w=[e_b, e_psw, rst.get("sw%d" % par)])
                rst["xb%d" % par] = e_sw
                e_tA = P.v(lambda e, par=par, sslot=sslot, tcb=tcb: e.tensor_tensor(
                    f32A[:, c.o_tA + par * 512: c.o_tA + par * 512 + 512], bank(c, sslot),
                    sinS[:, tcb:tcb + 512], ALU.mult),
                    w=[e_sw, e_tab, rst.get("ta%d" % par)])
                rst["sw%d" % par] = e_tA
                e_tB = P.g(lambda e, par=par, tcb=tcb: e.tensor_tensor(
                    f32A[:, c.o_Xf + par * 512: c.o_Xf + par * 512 + 512],
                    f32A[:, c.o_Xf + par * 512: c.o_Xf + par * 512 + 512],
                    cosT[:, tcb:tcb + 512], ALU.mult), w=[e_f, e_tab])
                e_d = P.v(lambda e, par=par, dcb=dcb, dstoff=dstoff: e.tensor_add(
                    bfA[:, dstoff + dcb: dstoff + dcb + 512],
                    f32A[:, c.o_tA + par * 512: c.o_tA + par * 512 + 512],
                    f32A[:, c.o_Xf + par * 512: c.o_Xf + par * 512 + 512]),
                    w=[e_tA, e_tB] + w_bufs)
                rst["xf%d" % par] = e_d
                rst["ta%d" % par] = e_d
                evs.append((e_mm, e_f, e_b, e_sw, e_tA, e_d))
            return evs

        k_cbs = [(ch * 512, ch * 512, ch * 512) for ch in range(KT // 4)]
        q_cbs = [(oc * 512, OC[oc] * 512, OC[oc] * 512) for oc in range(4)]
        kevs, karea = kpref_evs[hp]
        evKA = emit_rope(karea, c.o_KmA, k_cbs, kevs)
        evKB = emit_rope(karea + 1024, c.o_KmB, k_cbs, kevs)
        evQA = emit_rope(c.o_WTqA, c.o_QmA, q_cbs, [e_wready])
        evQB = emit_rope(c.o_WTqB, c.o_QmB, q_cbs, [e_wready])
        karea_rd[hp % 2] = evKB[-1][0]
        if hp + 1 < NPAIR:
            emit_kpref(hp + 1)
        e_rope = evQB[-1][5]
        e_wt_free = evQB[-1][0]

        SBK = ((0, 1), (2, 3), (5, 6))
        e_pair_sp = []
        for hb, (koff, qoff, hrow) in enumerate(((c.o_KmA, c.o_QmA, 0), (c.o_KmB, c.o_QmB, 64))):
            for oc in range(4):
                gstart = OC[oc] * 512
                nkt = OC[oc] * 4 + 4
                AVD = 3
                eP = {}
                eZ = {}
                ez = None

                def emit_av(kt, nkt=nkt, hrow=hrow, gstart=gstart):
                    p4 = kt % 4
                    c0 = max(0, kt * 128 - gstart)
                    w_z = [eP[kt]]
                    if kt == 0 and zev_prev.get(hrow) is not None:
                        w_z.append(zev_prev[hrow])
                    eZ[kt] = P.t(lambda e, kt=kt, hrow=hrow, c0=c0, nkt=nkt, p4=p4: e.matmul(
                        bank(c, 4)[hrow:hrow + 64, c0:512],
                        bfA[:, c.o_Vtok + kt * 128 + hrow: c.o_Vtok + kt * 128 + hrow + 64],
                        bfA[:, c.o_Pbuf + p4 * 512 + c0: c.o_Pbuf + p4 * 512 + 512],
                        start=(kt == 0), stop=(kt == nkt - 1),
                        tile_position=(0, hrow)), w=w_z)
                    return eZ[kt]

                for kt in range(nkt):
                    sb_slot, s2_slot = SBK[kt % 3]
                    p4 = kt % 4
                    c0 = max(0, kt * 128 - gstart)
                    w_mm = [e_rope, e_vtok]
                    if kt >= 3:
                        w_mm.append(eP[kt - 3])
                    e_s = P.t(lambda e, koff=koff, qoff=qoff, kt=kt, oc=oc, c0=c0, sb_slot=sb_slot:
                              e.matmul(bank(c, sb_slot)[:, c0:512],
                                       bfA[0:64, koff + kt * 128: koff + kt * 128 + 128],
                                       bfA[0:64, qoff + oc * 512 + c0: qoff + oc * 512 + 512],
                                       start=True, stop=True, tile_position=(0, 0)), w=w_mm)
                    e_s2 = P.t(lambda e, koff=koff, qoff=qoff, kt=kt, oc=oc, c0=c0, s2_slot=s2_slot:
                               e.matmul(bank(c, s2_slot)[:, c0:512],
                                        bfA[64:128, koff + kt * 128: koff + kt * 128 + 128],
                                        bfA[64:128, qoff + oc * 512 + c0: qoff + oc * 512 + 512],
                                        start=True, stop=True, tile_position=(64, 0)))
                    w_e = [e_s2]
                    if kt >= 4:
                        w_e.append(eP[kt - 4])
                    e_ev2 = P.a(lambda e, p4=p4, c0=c0, s2_slot=s2_slot: e.copy(
                        f32A[:, c.o_s2sb + p4 * 512 + c0: c.o_s2sb + p4 * 512 + 512],
                        bank(c, s2_slot)[:, c0:512]), w=w_e)
                    w_p = [e_ev2, e_s]
                    if kt >= 4:
                        w_p.append(eZ[kt - 4])
                    e_p = P.v(lambda e, p4=p4, c0=c0, sb_slot=sb_slot: e.tensor_tensor(
                        bfA[:, c.o_Pbuf + p4 * 512 + c0: c.o_Pbuf + p4 * 512 + 512],
                        bank(c, sb_slot)[:, c0:512],
                        f32A[:, c.o_s2sb + p4 * 512 + c0: c.o_s2sb + p4 * 512 + 512],
                        ALU.mult), w=w_p)
                    if kt * 128 >= gstart:
                        e_p = P.v(lambda e, p4=p4, c0=c0: e.tensor_tensor(
                            bfA[:, c.o_Pbuf + p4 * 512 + c0: c.o_Pbuf + p4 * 512 + c0 + 128],
                            bfA[:, c.o_Pbuf + p4 * 512 + c0: c.o_Pbuf + p4 * 512 + c0 + 128],
                            c.trimask[:], ALU.mult), w=[e_tri, e_p])
                    eP[kt] = e_p
                    if kt >= AVD:
                        ez = emit_av(kt - AVD)
                for kt in range(max(0, nkt - AVD), nkt):
                    ez = emit_av(kt)
                zp = oc % 2
                w_zv = [ez]
                if zsp_ev.get(zp) is not None:
                    w_zv.append(zsp_ev[zp])
                e_zev = P.a(lambda e, hrow=hrow, zp=zp: e.copy(
                    bfA[hrow:hrow + 64, c.o_zbuf + zp * 512: c.o_zbuf + zp * 512 + 512],
                    bank(c, 4)[hrow:hrow + 64, 0:512]), w=w_zv)
                zev_prev[hrow] = e_zev
                e_sp = P.dma(lambda e, hp=hp, hrow=hrow, oc=oc, zp=zp: e.dma_start(
                    out=scr["z"][hp][hrow:hrow + 64, oc * 512:(oc + 1) * 512],
                    in_=bfA[hrow:hrow + 64, c.o_zbuf + zp * 512: c.o_zbuf + zp * 512 + 512]),
                    w=[e_zev], sem="dz%d" % zp)
                zsp_ev[zp] = e_sp
                e_pair_sp.append(e_sp)
        e_prev_pair = zev_prev[64]
        zspill.append((e_pair_sp[-2], e_pair_sp[-1]))
        if DEBUG and hp == 0:
            e_dbg = P.dma(lambda e: e.dma_start(out=io["dbg_p0"],
                                                in_=bfA[:, c.o_KmA:c.o_KmA + 16384]),
                          w=[e_rope, e_vtok], sem="dbg")
            e_prev_pair = None
            w_dbg_guard.append(e_dbg)

        if hp == 1:
            emit_sw_totals()
        if hp >= 2:
            for _ in range(11):
                emit_p2_unit()
    while _p2["u"] < len(p2_units):
        emit_p2_unit()

    e_zf = []
    for hp in range(NPAIR):
        e_zf.append(P.dma(lambda e, hp=hp: e.dma_start(
            out=bfA[:, c.o_zfull + hp * 2048: c.o_zfull + hp * 2048 + 2048], in_=scr["z"][hp]),
            w=list(zspill[hp]) + [e_wt_free], sem="dzf"))
    e_wp = []
    for hp in range(NPAIR):
        e_wp.append(P.dmag(lambda e, hp=hp: e.dma_start(
            out=bfA[:, c.o_WprojT + hp * 1024: c.o_WprojT + hp * 1024 + 1024],
            in_=wT["wprojT"][hp * 128:(hp + 1) * 128, :]), w=[e_prev_pair], sem="dwp"))

    c1_evs = []
    x1_spills = []
    prev_x1rd = {}
    prev_yadd = {}
    for tt in range(NT_OWN):
        par = tt % 2
        x1t = c.x1stg[:, par * 1024:(par + 1) * 1024]
        grow = OC[tt // 4] * 512 + (tt % 4) * 128
        w_x = list(prev_x1rd.get(par, ()))
        e_x = P.dma(lambda e, grow=grow, par=par: e.dma_start(
            out=f32A[:, c.o_xstg + par * 1024: c.o_xstg + par * 1024 + 1024],
            in_=x[grow:grow + 128, :]), w=w_x, sem="dy%d" % par)
        ev_oc = []
        for oc2 in range(2):
            yslot = 2 + oc2
            e_mm = None
            for hp in range(NPAIR):
                w_mm = []
                if hp == 0:
                    w_mm = list(e_zf) + list(e_wp)
                    if oc2 in prev_yadd:
                        w_mm.append(prev_yadd[oc2])
                e_mm = P.t(lambda e, hp=hp, tt=tt, oc2=oc2, yslot=yslot: e.matmul(
                    bank(c, yslot),
                    bfA[:, c.o_zfull + hp * 2048 + tt * 128: c.o_zfull + hp * 2048 + tt * 128 + 128],
                    bfA[:, c.o_WprojT + hp * 1024 + oc2 * 512: c.o_WprojT + hp * 1024 + oc2 * 512 + 512],
                    start=(hp == 0), stop=(hp == 7)), w=w_mm)
            e_add = P.v(lambda e, oc2=oc2, yslot=yslot, x1t=x1t, par=par: e.tensor_add(
                x1t[:, oc2 * 512:(oc2 + 1) * 512], bank(c, yslot),
                f32A[:, c.o_xstg + par * 1024 + oc2 * 512: c.o_xstg + par * 1024 + oc2 * 512 + 512]),
                w=[e_mm, e_x] + w_x)
            prev_yadd[oc2] = e_add
            ev_oc.append(e_add)
        e_sp = P.dma(lambda e, tt=tt, x1t=x1t: e.dma_start(out=scr["x1"][tt], in_=x1t), w=ev_oc,
                     sem="dsp%d" % (tt % 2))
        x1_spills.append(e_sp)

        d2 = f32A[:, c.o_d2:c.o_d2 + 1024]
        w_cols = [c1_evs[tt - 1]] if tt >= 1 else [e_C]
        P.v(lambda e, x1t=x1t: e.reduce_sum(c.cols[:, 30:31], x1t, AX.X), w=ev_oc + w_cols, drain=True)
        e_nm = P.v(lambda e: e.tensor_scalar(c.cols[:, 31:32], c.cols[:, 30:31], -1.0 / D, None,
                                             ALU.mult), drain=True)
        e_d2 = P.a(lambda e, x1t=x1t, d2=d2: e.activation(d2, x1t, AF.Square, bias=c.cols[:, 31:32],
                                                          scale=1.0), w=[e_nm], drain=True)
        P.v(lambda e, d2=d2: e.reduce_sum(c.cols[:, 32:33], d2, AX.X), w=[e_d2], drain=True)
        e_vp = P.v(lambda e: e.tensor_scalar(c.cols[:, 32:33], c.cols[:, 32:33], 1.0 / D, LN_EPS,
                                             ALU.mult, ALU.add), drain=True)
        e_sq = P.a(lambda e: e.activation(c.cols[:, 33:34], c.cols[:, 32:33], AF.Sqrt, bias=0.0,
                                          scale=1.0), w=[e_vp], drain=True)
        P.v(lambda e: e.reciprocal(c.cols[:, 34:35], c.cols[:, 33:34]), w=[e_sq], drain=True)
        for _ in range(2):
            P.v(lambda e: e.tensor_mul(c.cols[:, 35:36], c.cols[:, 34:35], c.cols[:, 34:35]), drain=True)
            P.v(lambda e: e.tensor_mul(c.cols[:, 35:36], c.cols[:, 35:36], c.cols[:, 32:33]), drain=True)
            P.v(lambda e: e.tensor_scalar(c.cols[:, 35:36], c.cols[:, 35:36], -0.5, 1.5, ALU.mult,
                                          ALU.add), drain=True)
            P.v(lambda e: e.tensor_mul(c.cols[:, 34:35], c.cols[:, 34:35], c.cols[:, 35:36]), drain=True)
        e_rstd = P.barrier("vector")
        e_ab = P.a(lambda e, x1t=x1t, d2=d2: e.activation(d2, x1t, AF.Abs, bias=c.cols[:, 31:32],
                                                          scale=1.0), w=[e_rstd], drain=True)
        P.v(lambda e, d2=d2: e.reduce_max(c.cols[:, 36:37], d2, AX.X), w=[e_ab], drain=True)
        P.v(lambda e: e.tensor_mul(c.cols[:, 36:37], c.cols[:, 36:37], c.cols[:, 34:35]), drain=True)
        P.v(lambda e: e.tensor_scalar_max(c.cols[:, 36:37], c.cols[:, 36:37], Q_EPS), drain=True)
        P.v(lambda e: e.reciprocal(c.cols[:, 37:38], c.cols[:, 36:37]), drain=True)
        P.v(lambda e: e.tensor_scalar(c.cols[:, 37:38], c.cols[:, 37:38], 127.0, None, ALU.mult),
            drain=True)
        P.v(lambda e: e.tensor_mul(c.cols[:, 38:39], c.cols[:, 37:38], c.cols[:, 34:35]), drain=True)
        e_nq = P.v(lambda e: e.tensor_mul(c.cols[:, 39:40], c.cols[:, 31:32], c.cols[:, 38:39]),
                   drain=True)
        P.v(lambda e: e.tensor_mul(c.cols[:, 40:41], c.cols[:, 37:38], c.swg[:]),
            w=[sw_events["wgT"]], drain=True)
        P.v(lambda e, tt=tt: e.reciprocal(c.r1g[:, tt:tt + 1], c.cols[:, 40:41]), drain=True)
        P.v(lambda e: e.tensor_mul(c.cols[:, 40:41], c.cols[:, 37:38], c.swu[:]),
            w=[sw_events["wuT"]], drain=True)
        P.v(lambda e, tt=tt: e.reciprocal(c.r1u[:, tt:tt + 1], c.cols[:, 40:41]), drain=True)
        e_dq = P.a(lambda e, x1t=x1t, d2=d2: e.activation(d2, x1t, AF.Identity, bias=c.cols[:, 39:40],
                                                          scale=c.cols[:, 38:39]), w=[e_nq], drain=True)
        prev_x1rd[par] = (e_dq, e_sp)
        e_r1a = P.v(lambda e, d2=d2: e.tensor_scalar(d2, d2, c.Ccol[:], None, ALU.add), w=[e_dq])
        e_rnd = P.v(lambda e, d2=d2: e.tensor_scalar(d2, d2, c.nCcol[:], None, ALU.add), w=[e_r1a])
        tr = []
        e_tr = None
        for fc in range(8):
            tslot = fc % 2
            w_tr = [e_rnd, e_id]
            if fc >= 2:
                w_tr.append(tr[fc - 2])
            elif tt >= 1:
                w_tr.append(c1_evs[tt - 1])
            e_tr = P.t(lambda e, fc=fc, tslot=tslot, d2=d2: e.transpose(
                bank(c, tslot)[:, 0:128], d2[:, fc * 128:(fc + 1) * 128], c.ident[:]), w=w_tr)
            tr.append(P.a(lambda e, fc=fc, tslot=tslot, tt=tt: e.copy(
                bfA[:, c.o_c1T + fc * 2048 + tt * 128: c.o_c1T + fc * 2048 + tt * 128 + 128],
                bank(c, tslot)[:, 0:128]), w=[e_tr]))
        c1_evs.append(tr[-1])
    e_c1T = c1_evs[-1]
    if DEBUG:
        P.dma(lambda e: e.dma_start(out=io["dbg_c1T"], in_=bfA[:, c.o_c1T:c.o_c1T + 16384]),
              w=[e_c1T], sem="dbg")


    out_dmas = []
    prev_icc = {}
    prev_dstream = {}
    prev_eo = {}
    prev_gate_rd = {}
    prev_up_rd = {}
    e_prev_downevac = None
    for ss in range(NT_OWN // SS):
        tts = list(range(ss * SS, (ss + 1) * SS))
        gu_evs = {tt: [] for tt in tts}
        prev_gu_gate = {}
        prev_up = None
        for icc in range(8):
            par = icc % 2
            iccg = bfA[:, c.o_icc + par * 4096: c.o_icc + par * 4096 + 4096]
            iccu = bfA[:, c.o_iccu + par * 4096: c.o_iccu + par * 4096 + 4096]
            lg = []
            for fc in range(8):
                lg.append(P.dma(lambda e, fc=fc, icc=icc, iccg=iccg: e.dma_start(
                    out=iccg[:, fc * 512:(fc + 1) * 512],
                    in_=scr["cg"][fc][:, icc * 512:(icc + 1) * 512]),
                    w=([prev_icc.get(("g", par), e_c1T)] if fc == 0 else []) + [p2_ev["cg"][fc]],
                    sem="di%d" % par))
            lu = []
            for fc in range(8):
                lu.append(P.dma(lambda e, fc=fc, icc=icc, iccu=iccu: e.dma_start(
                    out=iccu[:, fc * 512:(fc + 1) * 512],
                    in_=scr["cu"][fc][:, icc * 512:(icc + 1) * 512]),
                    w=([prev_icc.get(("u", par), e_c1T)] if fc == 0 else []) + [p2_ev["cu"][fc]],
                    sem="du%d" % par))
            ev_gate = {}
            e_gmm = None
            for i, tt in enumerate(tts):
                gslot = i % 2
                e_mm = None
                for fc in range(8):
                    w_mm = []
                    if fc == 0:
                        w_mm = lg + [e_c1T]
                        if e_prev_downevac is not None:
                            w_mm.append(e_prev_downevac)
                        if prev_gate_rd.get(gslot) is not None:
                            w_mm.append(prev_gate_rd[gslot])
                    e_mm = P.t(lambda e, fc=fc, tt=tt, gslot=gslot, iccg=iccg: e.matmul(
                        bank(c, gslot),
                        bfA[:, c.o_c1T + fc * 2048 + tt * 128: c.o_c1T + fc * 2048 + tt * 128 + 128],
                        iccg[:, fc * 512:(fc + 1) * 512], start=(fc == 0), stop=(fc == 7)), w=w_mm)
                w_g = [e_mm]
                if tt in prev_gu_gate:
                    w_g.append(prev_gu_gate[tt])
                ev_gate[tt] = P.a(lambda e, tt=tt, i=i, gslot=gslot: e.activation(
                    f32A[:, c.o_gates + i * 512: c.o_gates + i * 512 + 512], bank(c, gslot),
                    AF.Silu, bias=0.0, scale=c.r1g[:, tt:tt + 1]), w=w_g)
                prev_gate_rd[gslot] = ev_gate[tt]
                e_gmm = e_mm
            prev_icc[("g", par)] = e_gmm
            e_umm = None
            for i, tt in enumerate(tts):
                uslot = 2 + i % 2
                e_mm = None
                for fc in range(8):
                    w_mm = []
                    if fc == 0:
                        w_mm = lu
                        if e_prev_downevac is not None:
                            w_mm.append(e_prev_downevac)
                        if prev_up_rd.get(uslot) is not None:
                            w_mm.append(prev_up_rd[uslot])
                    e_mm = P.t(lambda e, fc=fc, tt=tt, uslot=uslot, iccu=iccu: e.matmul(
                        bank(c, uslot),
                        bfA[:, c.o_c1T + fc * 2048 + tt * 128: c.o_c1T + fc * 2048 + tt * 128 + 128],
                        iccu[:, fc * 512:(fc + 1) * 512], start=(fc == 0), stop=(fc == 7)), w=w_mm)
                w_u = [e_mm]
                if prev_up is not None:
                    w_u.append(prev_up)
                e_u = P.a(lambda e, tt=tt, uslot=uslot: e.activation(
                    f32A[:, c.o_ups: c.o_ups + 512], bank(c, uslot),
                    AF.Copy, bias=0.0, scale=c.r1u[:, tt:tt + 1]), w=w_u)
                prev_up_rd[uslot] = e_u
                w_gu = [e_u, ev_gate[tt]]
                if ss == 0 and icc == 0 and i == 0:
                    w_gu += [_p2["prevst"][0], _p2["prevst"][1]]
                e_gu = P.v(lambda e, tt=tt, icc=icc, i=i: e.tensor_tensor(
                    f32A[:, c.o_gu + i * 4096 + icc * 512: c.o_gu + i * 4096 + icc * 512 + 512],
                    f32A[:, c.o_gates + i * 512: c.o_gates + i * 512 + 512],
                    f32A[:, c.o_ups: c.o_ups + 512], ALU.mult), w=w_gu)
                prev_up = e_gu
                prev_gu_gate[tt] = e_gu
                gu_evs[tt].append(e_gu)
                e_umm = e_mm
            prev_icc[("u", par)] = e_umm
        c2_evs = {}
        for i, tt in enumerate(tts):
            gut = f32A[:, c.o_gu + i * 4096: c.o_gu + i * 4096 + 4096]
            w_dbgu = []
            if DEBUG and tt == 0:
                w_dbgu.append(P.dma(lambda e, gut=gut: e.dma_start(out=io["dbg_gu"], in_=gut),
                                    w=gu_evs[tt], sem="dbg"))
            P.v(lambda e, gut=gut: e.reduce_max(c.cols[:, 44:45], gut, AX.X,
                                                apply_absolute_value=True),
                w=gu_evs[tt] + w_dbgu, drain=True)
            P.v(lambda e: e.tensor_scalar_max(c.cols[:, 44:45], c.cols[:, 44:45], Q_EPS), drain=True)
            P.v(lambda e: e.reciprocal(c.cols[:, 45:46], c.cols[:, 44:45]), drain=True)
            e_s2 = P.v(lambda e: e.tensor_scalar(c.cols[:, 45:46], c.cols[:, 45:46], 127.0, None,
                                                 ALU.mult), drain=True)
            P.v(lambda e: e.tensor_mul(c.cols[:, 46:47], c.cols[:, 45:46], c.swd[:]),
                w=[sw_events["wdT"]], drain=True)
            e_rd = P.v(lambda e, tt=tt: e.reciprocal(c.rdc[:, tt:tt + 1], c.cols[:, 46:47]),
                       drain=True)
            e_cs = P.v(lambda e, tt=tt: e.tensor_scalar(c.c2s[:, tt:tt + 1], c.cols[:, 45:46],
                                                        1.0, None, ALU.mult), w=[e_rd], drain=True)
            e_a1 = P.a(lambda e, gut=gut, tt=tt: e.activation(
                gut, gut, AF.Identity, bias=c.Ccol[:], scale=c.c2s[:, tt:tt + 1]), w=[e_cs])
            e_rnd = P.a(lambda e, gut=gut: e.activation(
                gut, gut, AF.Identity, bias=c.nCcol[:], scale=1.0), w=[e_a1])
            tr = []
            for ic in range(32):
                tslot = 4 + ic % 2
                w_tr = [e_rnd, e_id]
                if ic >= 2:
                    w_tr.append(tr[ic - 2])
                elif i >= 1:
                    w_tr.append(c2_evs[tts[i - 1]])
                elif e_prev_downevac is not None:
                    w_tr.append(e_prev_downevac)
                e_tr = P.t(lambda e, ic=ic, tslot=tslot, gut=gut: e.transpose(
                    bank(c, tslot)[:, 0:128], gut[:, ic * 128:(ic + 1) * 128], c.ident[:]), w=w_tr)
                tr.append(P.a(lambda e, ic=ic, i=i, tslot=tslot: e.copy(
                    bfA[:, c.o_c2T + i * 4096 + ic * 128: c.o_c2T + i * 4096 + ic * 128 + 128],
                    bank(c, tslot)[:, 0:128]), w=[e_tr]))
            c2_evs[tt] = tr[-1]
        down_mm = {}
        for ic in range(32):
            par = ic % 4
            cdb = bfA[:, c.o_cdblk + par * 1024: c.o_cdblk + par * 1024 + 1024]
            w_l = [prev_dstream.get(par), p2_ev["cd"][ic]]
            e_cd = P.dma(lambda e, ic=ic, cdb=cdb: e.dma_start(out=cdb, in_=scr["cd"][ic]), w=w_l,
                         sem="dd%d" % par)
            for foc in range(2):
                for i, tt in enumerate(tts):
                    dslot = i * 2 + foc
                    w_mm = [e_cd]
                    if ic == 0:
                        w_mm += [c2_evs[tts[-1]]]
                    down_mm[(ic, foc, i)] = P.t(lambda e, ic=ic, foc=foc, i=i, dslot=dslot,
                                                cdb=cdb: e.matmul(
                        bank(c, dslot),
                        bfA[:, c.o_c2T + i * 4096 + ic * 128: c.o_c2T + i * 4096 + ic * 128 + 128],
                        cdb[:, foc * 512:(foc + 1) * 512], start=(ic == 0), stop=(ic == 31)),
                        w=w_mm)
            prev_dstream[par] = down_mm[(ic, 1, SS - 1)]
        e_last_eo = None
        for i, tt in enumerate(tts):
            x1r = f32A[:, c.o_x1rb: c.o_x1rb + 1024]
            w_xr = [x1_spills[-1], x1_spills[-2]]
            if out_dmas:
                w_xr.append(out_dmas[-1])
            e_xr = P.dma(lambda e, tt=tt, x1r=x1r: e.dma_start(out=x1r, in_=scr["x1"][tt]), w=w_xr,
                         sem="dr")
            for foc in range(2):
                dslot = i * 2 + foc
                w_xo = [down_mm[(31, foc, i)], x1_spills[-1], x1_spills[-2]]
                if (foc,) in prev_eo:
                    w_xo.append(prev_eo[(foc,)])
                e_xo = P.a(lambda e, dslot=dslot, tt=tt, foc=foc: e.activation(
                    c.x1stg[:, foc * 512: foc * 512 + 512], bank(c, dslot),
                    AF.Copy, bias=0.0, scale=c.rdc[:, tt:tt + 1]), w=w_xo)
                w_o = [e_xo, e_xr]
                if len(out_dmas) >= 2:
                    w_o.append(out_dmas[-2])
                e_o = P.v(lambda e, foc=foc: e.tensor_add(
                    c.x1stg[:, 1024 + foc * 512: 1024 + foc * 512 + 512],
                    c.x1stg[:, foc * 512: foc * 512 + 512],
                    f32A[:, c.o_x1rb + foc * 512: c.o_x1rb + foc * 512 + 512]), w=w_o)
                prev_eo[(foc,)] = e_o
                out_dmas.append(P.dma(lambda e, tt=tt, foc=foc: e.dma_start(
                    out=out[tt * 128:(tt + 1) * 128, foc * 512:(foc + 1) * 512],
                    in_=c.x1stg[:, 1024 + foc * 512: 1024 + foc * 512 + 512]), w=[e_o],
                    sem="do%d" % foc))
                e_last_eo = e_o
        e_prev_downevac = e_last_eo

    P.items["sync"].append((None, tuple((sname, tot) for sname, tot in P.cnt.items()), None, 0, False))
    return P


def _fix_dma_waits(P):
    for ring_sem, ring_eng in (("ds", "sync"), ("dg", "gpsimd")):
        bounds = []
        tot = 0
        for fn, waits, sem, amt, drain in P.items[ring_eng]:
            if waits:
                bounds.append(tot)
            if sem == ring_sem:
                tot += amt
        bounds.append(tot)
        import bisect
        def remap(v, bounds=bounds):
            i = bisect.bisect_left(bounds, v)
            return bounds[i] if i < len(bounds) else v
        for eng in ENGINES:
            new = []
            for fn, waits, sem, amt, drain in P.items[eng]:
                nw = tuple((s2, remap(v) if s2 == ring_sem else v) for (s2, v) in waits)
                new.append((fn, nw, sem, amt, drain))
            P.items[eng] = new


def build():
    nc = bass.Bass("TRN2", target_bir_lowering=False)
    io = {}
    io["x"] = nc.dram_tensor("x", [T, D], F32, kind="ExternalInput").ap()
    wT = {}
    for wn, shp in (("wqT", [D, D]), ("wkT", [D, D]), ("wq2T", [D, D]), ("wk2T", [D, D]),
                    ("wvT", [D, D]), ("wprojT", [D, D]), ("wgT", [D, INNER]), ("wuT", [D, INNER]),
                    ("wdT", [INNER, D])):
        wT[wn] = nc.dram_tensor(wn, shp, F32, kind="ExternalInput").ap()
    io["wT"] = wT
    io["consts"] = nc.dram_tensor("consts", [128, 4], F32, kind="ExternalInput").ap()
    io["onesrow"] = nc.dram_tensor("onesrow", [1, 128], F32, kind="ExternalInput").ap()
    io["trimask"] = nc.dram_tensor("trimask", [128, 128], BF16, kind="ExternalInput").ap()
    io["pswap"] = nc.dram_tensor("pswap", [128, 128], BF16, kind="ExternalInput").ap()
    io["ident"] = nc.dram_tensor("ident", [128, 128], F32, kind="ExternalInput").ap()
    io["flagin"] = nc.dram_tensor("flag", [1, 1], I32, kind="ExternalInput").ap()
    io["ropetab"] = nc.dram_tensor("ropetab", [128, 8192], F32, kind="ExternalInput").ap()
    io["out"] = nc.dram_tensor("out", [N_OWN, D], F32, kind="ExternalOutput").ap()
    kind = "ExternalOutput"
    scr = {}
    scr["z"] = [nc.dram_tensor(f"scz{i}", [128, 2048], BF16, kind=kind).ap() for i in range(NPAIR)]
    scr["cg"] = [nc.dram_tensor(f"scg{i}", [128, INNER], BF16, kind=kind).ap() for i in range(8)]
    scr["cu"] = [nc.dram_tensor(f"scu{i}", [128, INNER], BF16, kind=kind).ap() for i in range(8)]
    scr["cd"] = [nc.dram_tensor(f"scd{i}", [128, D], BF16, kind=kind).ap() for i in range(32)]
    scr["x1"] = [nc.dram_tensor(f"sx1{i}", [128, D], F32, kind=kind).ap() for i in range(NT_OWN)]
    io["scr"] = scr
    if DEBUG:
        io["dbg_tab"] = nc.dram_tensor("dbg_tab", [128, 8192], F32, kind="ExternalOutput").ap()
        io["dbg_hnT"] = nc.dram_tensor("dbg_hnT", [128, 32768], BF16, kind="ExternalOutput").ap()
        io["dbg_p0"] = nc.dram_tensor("dbg_p0", [128, 16384], BF16, kind="ExternalOutput").ap()
        io["dbg_c1T"] = nc.dram_tensor("dbg_c1T", [128, 16384], BF16, kind="ExternalOutput").ap()
        io["dbg_gu"] = nc.dram_tensor("dbg_gu", [128, 4096], F32, kind="ExternalOutput").ap()
        io["dbg_cols"] = nc.dram_tensor("dbg_cols", [128, 41], F32, kind="ExternalOutput").ap()

    with ExitStack() as stack:
        c = alloc_tensors(nc, stack)
        progs = [build_half(nc, c, Prog(), h, io) for h in (0, 1)]
        sem_names = sorted(set(list(progs[0].cnt) + list(progs[1].cnt))) + ["fl"]
        sems = {s: stack.enter_context(nc.semaphore(s)) for s in sem_names}
        block = stack.enter_context(nc.Block())

        def replay(eng, items):
            waited = {}
            self_cnt = {}
            for fn, waits, sem, amt, drain in items:
                need = {}
                for (s, v) in waits:
                    if v > 0 and v > waited.get(s, 0):
                        need[s] = max(need.get(s, 0), v)
                for s, v in need.items():
                    eng.wait_ge(sems[s], v)
                    waited[s] = v
                if fn is None:
                    continue
                inst = fn(eng)
                if sem is not None:
                    inst.then_inc(sems[sem], amt)
                if drain and sem is not None:
                    eng.wait_ge(sems[sem], self_cnt.get(sem, 0) + amt)
                if sem is not None:
                    self_cnt[sem] = self_cnt.get(sem, 0) + amt
                    waited.setdefault(sem, 0)
                    if drain:
                        waited[sem] = max(waited[sem], self_cnt[sem])

        def section(eng_name):
            def body(eng):
                if eng_name == "sync":
                    eng.dma_start(out=c.flag[:], in_=io["flagin"]).then_inc(sems["fl"], 16)
                eng.wait_ge(sems["fl"], 16)
                with eng.register("rf_" + eng_name) as rf:
                    eng.load(rf, c.flag[0:1, 0:1])
                    with eng.If_eq(rf, 0):
                        replay(eng, progs[0].items[eng_name])
                    with eng.If_eq(rf, 1):
                        replay(eng, progs[1].items[eng_name])
            return body

        block.sync(section("sync"))
        block.tensor(section("tensor"))
        block.vector(section("vector"))
        block.scalar(section("scalar"))
        block.gpsimd(section("gpsimd"))
    return nc


def _rope_tables_np():
    try:
        import jax
        import jax.numpy as jnp
        cpu = jax.devices("cpu")[0]
        with jax.default_device(cpu):
            inv_freq = 1.0 / 10000.0 ** (jnp.arange(0, 64, 2, dtype=jnp.float32) / 64)
            t = jnp.arange(T, dtype=jnp.float32)
            freqs = jnp.outer(t, inv_freq)
            cos = np.asarray(jnp.cos(freqs), dtype=np.float32)
            sin = np.asarray(jnp.sin(freqs), dtype=np.float32)
    except Exception:
        e = np.arange(0, 64, 2, dtype=np.float32) / np.float32(64)
        inv = (1.0 / (10000.0 ** e.astype(np.float64))).astype(np.float32)
        freqs = np.outer(np.arange(T, dtype=np.float32), inv).astype(np.float32)
        cos, sin = np.cos(freqs).astype(np.float32), np.sin(freqs).astype(np.float32)
    r = np.arange(128)
    sgn = np.where(r % 64 < 32, 1.0, -1.0).astype(np.float32)
    cosT = cos.T[r % 32]
    sinS = sin.T[r % 32] * sgn[:, None]
    return np.ascontiguousarray(np.concatenate([cosT, sinS], axis=1))


def _consts_np():
    r = np.arange(128)
    consts = np.stack([np.ones(128), np.full(128, 2.0 ** -30), (r % 32).astype(float),
                       np.where(r % 64 < 32, 1.0, -1.0)], axis=1).astype(np.float32)
    onesrow = np.ones((1, 128), np.float32)
    tri = np.triu(np.ones((128, 128))).astype(ml_dtypes.bfloat16)
    psw = np.zeros((128, 128), ml_dtypes.bfloat16)
    for j in range(128):
        psw[j ^ 32, j] = 1.0
    ident = np.eye(128, dtype=np.float32)
    return consts, onesrow, tri, psw, ident


def kernel(L0_attn_x, L0_attn_Wq, L0_attn_Wk, L0_attn_Wq2, L0_attn_Wk2, L0_attn_Wv,
           L0_attn_Wproj, L0_ffn_w_gate, L0_ffn_w_up, L0_ffn_w_down):
    x = np.ascontiguousarray(np.asarray(L0_attn_x, dtype=np.float32))
    wTs = {
        "wqT": L0_attn_Wq, "wkT": L0_attn_Wk, "wq2T": L0_attn_Wq2, "wk2T": L0_attn_Wk2,
        "wvT": L0_attn_Wv, "wprojT": L0_attn_Wproj, "wgT": L0_ffn_w_gate,
        "wuT": L0_ffn_w_up, "wdT": L0_ffn_w_down,
    }
    wTs = {k: np.ascontiguousarray(np.asarray(v, dtype=np.float32).T) for k, v in wTs.items()}
    consts, onesrow, tri, psw, ident = _consts_np()
    ropetab = _rope_tables_np()

    nc = build()
    in_maps = []
    for core in range(8):
        b, half = core // 2, core % 2
        m = {"x": np.ascontiguousarray(x[b]),
             "consts": consts, "onesrow": onesrow, "trimask": tri, "pswap": psw,
             "ident": ident, "flag": np.full((1, 1), half, np.int32), "ropetab": ropetab}
        m.update(wTs)
        in_maps.append(m)
    res = run_bass_kernel_spmd(nc, in_maps, core_ids=list(range(8)))
    full = np.empty((4, T, D), np.float32)
    for core in range(8):
        b, half = core // 2, core % 2
        o = res.results[core]["out"]
        if half == 0:
            full[b, 0:1024] = o[0:1024]
            full[b, 3072:4096] = o[1024:2048]
        else:
            full[b, 1024:3072] = o
    return full


# version 13
# speedup vs baseline: 1.7824x; 1.0544x over previous
import numpy as np
import ml_dtypes
from contextlib import ExitStack

import concourse.bass as bass
import concourse.mybir as mybir
from concourse.bass_utils import run_bass_kernel_spmd

F32 = mybir.dt.float32
BF16 = mybir.dt.bfloat16
I32 = mybir.dt.int32
ALU = mybir.AluOpType
AF = mybir.ActivationFunctionType
AX = mybir.AxisListType

D = 1024
T = 4096
INNER = 4096
NPAIR = 8
N_OWN = 2048
NT_OWN = 16
SS = 4
LN_EPS = 1e-5
Q_EPS = 1e-5
RNDC = 1.5 * 2.0 ** 23
RNDK = 1.5 * 2.0 ** 52 - 1.5 * 2.0 ** 23

ENGINES = ("sync", "tensor", "vector", "scalar", "gpsimd")
DEBUG = False


def own_chunks(half):
    return [0, 1, 6, 7] if half == 0 else [2, 3, 4, 5]


def kv_tiles(half):
    return 32 if half == 0 else 24


class Prog:
    def __init__(self):
        self.items = {e: [] for e in ENGINES}
        self.cnt = {}

    def ev(self, sem):
        return (sem, self.cnt.get(sem, 0))

    def _add(self, eng, fn, waits, sem, amt, drain):
        self.cnt[sem] = self.cnt.get(sem, 0) + amt
        evt = (sem, self.cnt[sem])
        self.items[eng].append((fn, tuple(w for w in waits if w is not None), sem, amt, drain))
        return evt

    def dma(self, fn, w=(), sem="ds"):
        return self._add("sync", fn, w, sem, 16, False)

    def dmag(self, fn, w=(), sem="dg"):
        return self._add("gpsimd", fn, w, sem, 16, False)

    def t(self, fn, w=(), drain=False):
        return self._add("tensor", fn, w, "t", 1, drain)

    def v(self, fn, w=(), drain=False):
        return self._add("vector", fn, w, "v", 1, drain)

    def a(self, fn, w=(), drain=False):
        return self._add("scalar", fn, w, "a", 1, drain)

    def g(self, fn, w=(), drain=False):
        return self._add("gpsimd", fn, w, "g", 1, drain)

    def barrier(self, eng):
        if eng == "vector":
            return self.v(lambda e: e.engine_nop())
        if eng == "scalar":
            return self.a(lambda e: e.engine_nop())
        raise ValueError(eng)


class Ctx:
    pass


def alloc_tensors(nc, stack):
    c = Ctx()
    sb = lambda n, s, d: stack.enter_context(nc.sbuf_tensor(n, s, d))
    c.bfA = sb("bfA", [128, 60416], BF16)
    c.f32A = sb("f32A", [128, 19968], F32)
    c.ps = stack.enter_context(nc.psum_tensor("ps", [128, 4096], F32))

    c.o_hnT = 0
    c.o_KmA = 32768
    c.o_KmB = 36864
    c.o_Vtok = 40960
    c.o_QmA = 45056
    c.o_QmB = 47104
    c.o_WTqA = 49152
    c.o_WTqB = 50176
    c.o_WTkA = 51200
    c.o_WTkB = 52224
    c.o_WTv = 53248
    c.o_Xb = 54272
    c.o_Pbuf = 55296
    c.o_zbuf = 57344
    c.o_WTkP = 58368
    c.o_zfull = 0
    c.o_WprojT = 16384
    c.o_c1T = 24576
    c.o_icc = 0
    c.o_iccu = 40960
    c.o_c2T = 8192
    c.o_cdblk = 49152

    c.o_cosT = 0
    c.o_sinS = 4096
    c.o_xstg = 8192
    c.o_wstg = 10240
    c.o_d2 = 14336
    c.o_Xf = 15360
    c.o_tA = 16384
    c.o_s2sb = 17408
    c.o_gu = 0
    c.o_x1rb = 16384
    c.o_gates = 17408
    c.o_ups = 19456

    c.cols = sb("cols", [128, 80], F32)
    c.c2s = sb("c2s", [128, 16], F32)
    c.r1g = sb("r1g", [128, 16], F32)
    c.r1u = sb("r1u", [128, 16], F32)
    c.rdc = sb("rdc", [128, 16], F32)
    c.Ccol = sb("Ccol", [128, 1], F32)
    c.nCcol = sb("nCcol", [128, 1], F32)
    c.swg = sb("swg", [128, 1], F32)
    c.swu = sb("swu", [128, 1], F32)
    c.swd = sb("swd", [128, 1], F32)
    c.acc = sb("acc", [128, 80], F32)
    c.consts = sb("constsb", [128, 4], F32)
    c.onesrow = sb("onesrowb", [1, 128], F32)
    c.trimask = sb("trimaskb", [128, 128], BF16)
    c.pswap = sb("pswapb", [128, 128], BF16)
    c.ident = sb("identb", [128, 128], F32)
    c.flag = sb("flagsb", [1, 1], I32)
    c.x1stg = sb("x1stg", [128, 2048], F32)
    c.ssc = sb("ssc", [1, 1], F32)
    return c


def bank(c, i):
    return c.ps[:, i * 512:(i + 1) * 512]


def build_half(nc, c, P, half, io):
    OC = own_chunks(half)
    KT = kv_tiles(half)
    bfA, f32A, ps = c.bfA, c.f32A, c.ps
    x, out = io["x"], io["out"]
    wT = io["wT"]
    scr = io["scr"]

    ones = c.consts[:, 0:1]
    epsv = c.consts[:, 1:2]
    jcol = c.consts[:, 2:3]
    sgn = c.consts[:, 3:4]
    cosT = f32A[:, c.o_cosT:c.o_cosT + 4096]
    sinS = f32A[:, c.o_sinS:c.o_sinS + 4096]

    P.dma(lambda e: e.dma_start(out=c.consts[:, 0:4], in_=io["consts"]), sem="dc")
    P.dma(lambda e: e.dma_start(out=c.onesrow[:], in_=io["onesrow"]), sem="dc")
    P.dma(lambda e: e.dma_start(out=c.trimask[:], in_=io["trimask"]), sem="dc")
    P.dma(lambda e: e.dma_start(out=c.pswap[:], in_=io["pswap"]), sem="dc")
    P.dma(lambda e: e.dma_start(out=c.ident[:], in_=io["ident"]), sem="dc")
    e_consts = e_or = e_tri = e_psw = e_id = P.dma(
        lambda e: e.dma_start(out=f32A[:, 0:8192], in_=io["ropetab"]), sem="dc")

    P.v(lambda e: e.tensor_add(c.cols[:, 0:1], ones, epsv), w=[e_consts], drain=True)
    P.v(lambda e: e.tensor_sub(c.cols[:, 0:1], c.cols[:, 0:1], ones), drain=True)
    P.v(lambda e: e.tensor_scalar(c.cols[:, 0:1], c.cols[:, 0:1], 2.0 ** 30, None, ALU.mult), drain=True)
    P.v(lambda e: e.tensor_scalar(c.Ccol[:], c.cols[:, 0:1], RNDK, RNDC, ALU.mult, ALU.add), drain=True)
    e_C = P.v(lambda e: e.tensor_scalar(c.nCcol[:], c.Ccol[:], -1.0, None, ALU.mult), drain=True)

    e_tab = e_consts

    slot0 = {"wgT": 0, "wuT": 16, "wdT": 32}
    ccols = {"wgT": INNER, "wuT": INNER, "wdT": D}
    p1_units = ([("wgT", ck, hh) for ck in range(8) for hh in (0, 2048)]
                + [("wuT", ck, hh) for ck in range(8) for hh in (0, 2048)]
                + [("wdT", ck, 0) for ck in range(32)])
    p1_state = {"u": 0, "prev": {0: None, 1: None}, "last": None}

    def emit_p1_unit():
        if p1_state["u"] >= len(p1_units):
            return
        u = p1_state["u"]
        p1_state["u"] += 1
        wname, ck, hh = p1_units[u]
        hw = min(2048, ccols[wname] - hh)
        par = u % 2
        s = slot0[wname] + (ck * 2 + hh // 2048 if wname != "wdT" else ck)
        wst = f32A[:, c.o_wstg + par * 2048: c.o_wstg + par * 2048 + hw]
        w_l = [p1_state["prev"][par]] if p1_state["prev"][par] else []
        e_l = P.dma(lambda e, wname=wname, ck=ck, hh=hh, hw=hw, wst=wst: e.dma_start(
            out=wst, in_=wT[wname][ck * 128:(ck + 1) * 128, hh:hh + hw]), w=w_l,
            sem="dw%d" % par)
        e_r = P.v(lambda e, wst=wst, s=s: e.reduce_sum(
            c.acc[:, s:s + 1], wst, AX.X, apply_absolute_value=True), w=[e_l])
        p1_state["prev"][par] = e_r
        p1_state["last"] = e_r

    ln_done = []
    for gt in range(KT):
        par = gt % 2
        cb = 16 if par == 0 else 47
        xst = f32A[:, c.o_xstg + par * 1024: c.o_xstg + (par + 1) * 1024]
        d2 = f32A[:, c.o_d2 + par * 1024: c.o_d2 + par * 1024 + 1024]
        w_x = [ln_done[gt - 2]] if gt >= 2 else []
        e_x = P.dma(lambda e, gt=gt, xst=xst: e.dma_start(out=xst, in_=x[gt * 128:(gt + 1) * 128, :]),
                    w=w_x, sem="dx%d" % (gt % 2))
        emit_p1_unit()
        emit_p1_unit()
        w_cols = [ln_done[gt - 2]] if gt >= 2 else [e_C]
        P.v(lambda e, xst=xst, cb=cb: e.reduce_sum(c.cols[:, cb:cb + 1], xst, AX.X),
            w=[e_x] + w_cols, drain=True)
        e_nm = P.v(lambda e, cb=cb: e.tensor_scalar(c.cols[:, cb + 1:cb + 2], c.cols[:, cb:cb + 1],
                                                    -1.0 / D, None, ALU.mult), drain=True)
        e_d2 = P.a(lambda e, xst=xst, d2=d2, cb=cb: e.activation(
            d2, xst, AF.Square, bias=c.cols[:, cb + 1:cb + 2], scale=1.0), w=[e_nm], drain=True)
        P.v(lambda e, d2=d2, cb=cb: e.reduce_sum(c.cols[:, cb + 2:cb + 3], d2, AX.X), w=[e_d2],
            drain=True)
        e_vp = P.v(lambda e, cb=cb: e.tensor_scalar(c.cols[:, cb + 2:cb + 3], c.cols[:, cb + 2:cb + 3],
                                                    1.0 / D, LN_EPS, ALU.mult, ALU.add), drain=True)
        e_sq = P.a(lambda e, cb=cb: e.activation(c.cols[:, cb + 3:cb + 4], c.cols[:, cb + 2:cb + 3],
                                                 AF.Sqrt, bias=0.0, scale=1.0), w=[e_vp], drain=True)
        P.v(lambda e, cb=cb: e.reciprocal(c.cols[:, cb + 4:cb + 5], c.cols[:, cb + 3:cb + 4]),
            w=[e_sq], drain=True)
        for _ in range(2):
            P.v(lambda e, cb=cb: e.tensor_mul(c.cols[:, cb + 5:cb + 6], c.cols[:, cb + 4:cb + 5],
                                              c.cols[:, cb + 4:cb + 5]), drain=True)
            P.v(lambda e, cb=cb: e.tensor_mul(c.cols[:, cb + 5:cb + 6], c.cols[:, cb + 5:cb + 6],
                                              c.cols[:, cb + 2:cb + 3]), drain=True)
            P.v(lambda e, cb=cb: e.tensor_scalar(c.cols[:, cb + 5:cb + 6], c.cols[:, cb + 5:cb + 6],
                                                 -0.5, 1.5, ALU.mult, ALU.add), drain=True)
            P.v(lambda e, cb=cb: e.tensor_mul(c.cols[:, cb + 4:cb + 5], c.cols[:, cb + 4:cb + 5],
                                              c.cols[:, cb + 5:cb + 6]), drain=True)
        e_nmr = P.v(lambda e, cb=cb: e.tensor_mul(c.cols[:, cb + 6:cb + 7], c.cols[:, cb + 1:cb + 2],
                                                  c.cols[:, cb + 4:cb + 5]), drain=True)
        e_hn = P.a(lambda e, xst=xst, d2=d2, cb=cb: e.activation(
            d2, xst, AF.Identity, bias=c.cols[:, cb + 6:cb + 7], scale=c.cols[:, cb + 4:cb + 5]),
            w=[e_nmr], drain=True)
        evs = []
        for fc in range(8):
            slot = fc % 2
            w_tr = [e_hn, e_id]
            if fc >= 2:
                w_tr.append(evs[fc - 2])
            elif gt >= 1:
                w_tr.append(ln_done[gt - 1])
            e_tr = P.t(lambda e, fc=fc, slot=slot, d2=d2: e.transpose(
                bank(c, slot)[:, 0:128], d2[:, fc * 128:(fc + 1) * 128], c.ident[:]), w=w_tr)
            e_ev = P.a(lambda e, fc=fc, slot=slot, gt=gt: e.copy(
                bfA[:, c.o_hnT + fc * 4096 + gt * 128: c.o_hnT + fc * 4096 + gt * 128 + 128],
                bank(c, slot)[:, 0:128]), w=[e_tr])
            evs.append(e_ev)
        ln_done.append(evs[-1])
    e_hnT = ln_done[-1]

    if DEBUG:
        P.dma(lambda e: e.dma_start(out=io["dbg_tab"], in_=f32A[:, 0:8192]), w=[e_tab], sem="dbg")
        P.dma(lambda e: e.dma_start(out=io["dbg_hnT"], in_=bfA[:, 0:32768]), w=[e_hnT], sem="dbg")

    while p1_state["u"] < len(p1_units):
        emit_p1_unit()
    e_p1 = p1_state["last"]

    sw_events = {}

    def emit_sw_totals():
        for wname, swcol, nelem in (("wgT", c.swg, INNER * D), ("wuT", c.swu, INNER * D),
                                    ("wdT", c.swd, D * INNER)):
            s0, nck = slot0[wname], {"wgT": 16, "wuT": 16, "wdT": 32}[wname]
            e_r = P.v(lambda e, s0=s0, nck=nck: e.reduce_sum(c.cols[:, 24:25],
                                                             c.acc[:, s0:s0 + nck], AX.X),
                      w=[e_p1], drain=True)
            e_m1 = P.t(lambda e: e.matmul(ps[0:1, 1024:1025], c.cols[:, 24:25], ones,
                                          start=True, stop=True), w=[e_r, e_consts, P.ev("v")])
            e_e1 = P.a(lambda e: e.copy(c.ssc[:], ps[0:1, 1024:1025]), w=[e_m1], drain=True)
            e_m2 = P.t(lambda e: e.matmul(ps[:, 1025:1026], c.onesrow[:], c.ssc[:],
                                          start=True, stop=True), w=[e_e1, e_or])
            e_e2 = P.a(lambda e: e.copy(c.cols[:, 25:26], ps[:, 1025:1026]), w=[e_m2], drain=True)
            P.v(lambda e, nelem=nelem: e.tensor_scalar(c.cols[:, 26:27], c.cols[:, 25:26],
                                                       1.0 / nelem, None, ALU.mult), w=[e_e2], drain=True)
            P.v(lambda e: e.tensor_scalar_max(c.cols[:, 26:27], c.cols[:, 26:27], Q_EPS), drain=True)
            sw_events[wname] = P.v(lambda e, swcol=swcol: e.reciprocal(swcol[:], c.cols[:, 26:27]),
                                   drain=True)

    p2_ev = {"cg": [None] * 8, "cu": [None] * 8, "cd": [None] * 32}
    _p2 = {"u": 0, "prevst": {0: e_p1, 1: e_p1}}
    p2_tick = {"n": 0}
    p2_units = ([("wgT", ck, hh) for ck in range(8) for hh in (0, 2048)]
                + [("wuT", ck, hh) for ck in range(8) for hh in (0, 2048)]
                + [("wdT", ck, 0) for ck in range(32)])

    def emit_p2_unit():
        if _p2["u"] >= len(p2_units):
            return
        u = _p2["u"]
        _p2["u"] += 1
        wname, ck, hh = p2_units[u]
        hw = min(2048, ccols[wname] - hh)
        par = u % 2
        swcol = {"wgT": c.swg, "wuT": c.swu, "wdT": c.swd}[wname]
        scrkey = {"wgT": "cg", "wuT": "cu", "wdT": "cd"}[wname]
        wst = f32A[:, c.o_wstg + par * 2048: c.o_wstg + par * 2048 + hw]
        e_l = P.dma(lambda e, wname=wname, ck=ck, hh=hh, hw=hw, wst=wst: e.dma_start(
            out=wst, in_=wT[wname][ck * 128:(ck + 1) * 128, hh:hh + hw]),
            w=[_p2["prevst"][par]], sem="dw%d" % par)
        e_a1 = P.a(lambda e, wst=wst, swcol=swcol: e.activation(
            wst, wst, AF.Identity, bias=c.Ccol[:], scale=swcol[:]),
            w=[e_l, sw_events[wname], e_C])
        e_a2 = P.a(lambda e, wst=wst: e.activation(
            wst, wst, AF.Identity, bias=c.nCcol[:], scale=1.0), w=[e_a1])
        e_q = P.v(lambda e, wst=wst: e.tensor_scalar(wst, wst, -1.0, 1.0, ALU.max, ALU.min),
                  w=[e_a2])
        e_st = P.dmag(lambda e, wst=wst, scrkey=scrkey, ck=ck, hh=hh, hw=hw: e.dma_start(
            out=scr[scrkey][ck][:, hh:hh + hw], in_=wst), w=[e_q], sem="dq%d" % par)
        _p2["prevst"][par] = e_st
        p2_ev[scrkey][ck] = e_st

    rst = {}
    w_dbg_guard = []
    zspill = []
    zev_prev = {}
    zsp_ev = {}
    e_prev_pair = None
    e_wt_free = None
    def emit_kpref(hp):
        area = c.o_WTkA if hp % 2 == 0 else c.o_WTkP
        w_g = [karea_rd[hp % 2]] if karea_rd.get(hp % 2) is not None else []
        evs = []
        for sub in (0, 1):
            off = area + sub * 1024
            for wn, halfcol in (("wkT", 0), ("wk2T", 64)):
                def f(e, wn=wn, sub=sub, off=off, halfcol=halfcol, hp=hp):
                    src = wT[wn].rearrange("(fc p) o -> p fc o", p=128)[
                        :, :, hp * 128 + sub * 64: hp * 128 + sub * 64 + 64]
                    dst = bfA[:, off:off + 1024].rearrange("p (fc z) -> p fc z", z=128)[
                        :, :, halfcol:halfcol + 64]
                    return e.dma_start(out=dst, in_=src)
                evs.append(P.dmag(f, w=w_g, sem="dwk%d" % (hp % 2)))
        kpref_evs[hp] = (evs, area)

    karea_rd = {}
    kpref_evs = {}
    emit_kpref(0)

    for hp in range(NPAIR):
        w_guard = [e_wt_free] if e_wt_free is not None else []

        def fv(e, hp=hp):
            src = wT["wvT"].rearrange("(fc p) o -> p fc o", p=128)[:, :, hp * 128:(hp + 1) * 128]
            dst = bfA[:, c.o_WTv:c.o_WTv + 1024].rearrange("p (fc z) -> p fc z", z=128)
            return e.dma_start(out=dst, in_=src)
        e_wv = P.dmag(fv, w=w_guard, sem="dwv")
        wl = [e_wv]
        for sub, off in ((0, c.o_WTqA), (1, c.o_WTqB)):
            for wn, halfcol in (("wqT", 0), ("wq2T", 64)):
                def f(e, wn=wn, sub=sub, off=off, halfcol=halfcol, hp=hp):
                    src = wT[wn].rearrange("(fc p) o -> p fc o", p=128)[
                        :, :, hp * 128 + sub * 64: hp * 128 + sub * 64 + 64]
                    dst = bfA[:, off:off + 1024].rearrange("p (fc z) -> p fc z", z=128)[
                        :, :, halfcol:halfcol + 64]
                    return e.dma_start(out=dst, in_=src)
                wl.append(P.dmag(f, w=w_guard, sem="dwt"))
        P.v(lambda e: e.tensor_scalar(bfA[:, c.o_WTqA:c.o_WTqA + 1024],
                                      bfA[:, c.o_WTqA:c.o_WTqA + 1024], 1.0 / 64, None, ALU.mult),
            w=wl)
        e_wready = P.v(lambda e: e.tensor_scalar(bfA[:, c.o_WTqB:c.o_WTqB + 1024],
                                                 bfA[:, c.o_WTqB:c.o_WTqB + 1024], 1.0 / 64, None,
                                                 ALU.mult), w=wl)

        w_bufs = [e_prev_pair] if e_prev_pair is not None else []
        w_bufs += w_dbg_guard
        if e_prev_pair is None and hp > 0:
            w_bufs.extend(zspill[-1])

        vp_last = []
        for ch in range(KT // 4):
            par = ch % 2
            pslot = 5 + par
            e_mm = None
            for fc in range(8):
                w_mm = []
                if fc == 0:
                    w_mm = [e_hnT, e_wv, rst.get("ps%d" % par)]
                e_mm = P.t(lambda e, fc=fc, ch=ch, pslot=pslot: e.matmul(
                    bank(c, pslot), bfA[:, c.o_WTv + fc * 128:c.o_WTv + fc * 128 + 128],
                    bfA[:, c.o_hnT + fc * 4096 + ch * 512: c.o_hnT + fc * 4096 + ch * 512 + 512],
                    start=(fc == 0), stop=(fc == 7)), w=w_mm)
            e_ev = P.a(lambda e, pslot=pslot, ch=ch, par=par: e.copy(
                f32A[:, c.o_Xf + par * 512: c.o_Xf + par * 512 + 512], bank(c, pslot)),
                w=[e_mm, rst.get("xf%d" % par)])
            rst["ps%d" % par] = e_ev
            tr_ev = []
            e_tr = None
            for j in range(4):
                tslot = j % 2
                w_tr = [e_ev, e_id] + w_bufs
                if j >= 2:
                    w_tr.append(tr_ev[j - 2])
                elif ch >= 1:
                    w_tr.append(vp_last[ch - 1])
                e_tr = P.t(lambda e, j=j, ch=ch, tslot=tslot, par=par: e.transpose(
                    bank(c, tslot)[:, 0:128],
                    f32A[:, c.o_Xf + par * 512 + j * 128: c.o_Xf + par * 512 + j * 128 + 128],
                    c.ident[:]), w=w_tr)
                tr_ev.append(P.a(lambda e, j=j, ch=ch, tslot=tslot: e.copy(
                    bfA[:, c.o_Vtok + (ch * 4 + j) * 128: c.o_Vtok + (ch * 4 + j) * 128 + 128],
                    bank(c, tslot)[:, 0:128]), w=[e_tr]))
            rst["xf%d" % par] = e_tr
            vp_last.append(tr_ev[-1])
        e_vtok = vp_last[-1]

        def emit_rope(setoff, dstoff, colbases, w_set):
            evs = []
            for i, (dcb, hcb, tcb) in enumerate(colbases):
                par = i % 2
                pslot = 5 + par
                sslot = 7 if par == 0 else 4
                e_mm = None
                for fc in range(8):
                    w_mm = []
                    if fc == 0:
                        w_mm = list(w_set) + [e_hnT, rst.get("ps%d" % par)]
                    e_mm = P.t(lambda e, fc=fc, pslot=pslot, setoff=setoff, hcb=hcb: e.matmul(
                        bank(c, pslot), bfA[:, setoff + fc * 128: setoff + fc * 128 + 128],
                        bfA[:, c.o_hnT + fc * 4096 + hcb: c.o_hnT + fc * 4096 + hcb + 512],
                        start=(fc == 0), stop=(fc == 7)), w=w_mm)
                e_f = P.a(lambda e, par=par, pslot=pslot: e.copy(
                    f32A[:, c.o_Xf + par * 512: c.o_Xf + par * 512 + 512], bank(c, pslot)),
                    w=[e_mm, rst.get("xf%d" % par)])
                e_b = P.a(lambda e, par=par, pslot=pslot: e.copy(
                    bfA[:, c.o_Xb + par * 512: c.o_Xb + par * 512 + 512], bank(c, pslot)),
                    w=[rst.get("xb%d" % par)])
                rst["ps%d" % par] = e_b
                e_sw = P.t(lambda e, par=par, sslot=sslot: e.matmul(
                    bank(c, sslot), c.pswap[:],
                    bfA[:, c.o_Xb + par * 512: c.o_Xb + par * 512 + 512],
                    start=True, stop=True), w=[e_b, e_psw, rst.get("sw%d" % par)])
                rst["xb%d" % par] = e_sw
                e_tA = P.v(lambda e, par=par, sslot=sslot, tcb=tcb: e.tensor_tensor(
                    f32A[:, c.o_tA + par * 512: c.o_tA + par * 512 + 512], bank(c, sslot),
                    sinS[:, tcb:tcb + 512], ALU.mult),
                    w=[e_sw, e_tab, rst.get("ta%d" % par)])
                rst["sw%d" % par] = e_tA
                e_tB = P.g(lambda e, par=par, tcb=tcb: e.tensor_tensor(
                    f32A[:, c.o_Xf + par * 512: c.o_Xf + par * 512 + 512],
                    f32A[:, c.o_Xf + par * 512: c.o_Xf + par * 512 + 512],
                    cosT[:, tcb:tcb + 512], ALU.mult), w=[e_f, e_tab])
                e_d = P.v(lambda e, par=par, dcb=dcb, dstoff=dstoff: e.tensor_add(
                    bfA[:, dstoff + dcb: dstoff + dcb + 512],
                    f32A[:, c.o_tA + par * 512: c.o_tA + par * 512 + 512],
                    f32A[:, c.o_Xf + par * 512: c.o_Xf + par * 512 + 512]),
                    w=[e_tA, e_tB] + w_bufs)
                rst["xf%d" % par] = e_d
                rst["ta%d" % par] = e_d
                evs.append((e_mm, e_f, e_b, e_sw, e_tA, e_d))
            return evs

        k_cbs = [(ch * 512, ch * 512, ch * 512) for ch in range(KT // 4)]
        q_cbs = [(oc * 512, OC[oc] * 512, OC[oc] * 512) for oc in range(4)]
        kevs, karea = kpref_evs[hp]
        evKA = emit_rope(karea, c.o_KmA, k_cbs, kevs)
        evKB = emit_rope(karea + 1024, c.o_KmB, k_cbs, kevs)
        evQA = emit_rope(c.o_WTqA, c.o_QmA, q_cbs, [e_wready])
        evQB = emit_rope(c.o_WTqB, c.o_QmB, q_cbs, [e_wready])
        karea_rd[hp % 2] = evKB[-1][0]
        if hp + 1 < NPAIR:
            emit_kpref(hp + 1)
        e_rope = evQB[-1][5]
        e_wt_free = evQB[-1][0]

        SBK = ((0, 1), (2, 3), (5, 6))
        e_pair_sp = []
        for hb, (koff, qoff, hrow) in enumerate(((c.o_KmA, c.o_QmA, 0), (c.o_KmB, c.o_QmB, 64))):
            for oc in range(4):
                gstart = OC[oc] * 512
                nkt = OC[oc] * 4 + 4
                AVD = 3
                eP = {}
                eZ = {}
                ez = None

                def emit_av(kt, nkt=nkt, hrow=hrow, gstart=gstart):
                    p4 = kt % 4
                    c0 = max(0, kt * 128 - gstart)
                    w_z = [eP[kt]]
                    if kt == 0 and zev_prev.get(hrow) is not None:
                        w_z.append(zev_prev[hrow])
                    eZ[kt] = P.t(lambda e, kt=kt, hrow=hrow, c0=c0, nkt=nkt, p4=p4: e.matmul(
                        bank(c, 4)[hrow:hrow + 64, c0:512],
                        bfA[:, c.o_Vtok + kt * 128 + hrow: c.o_Vtok + kt * 128 + hrow + 64],
                        bfA[:, c.o_Pbuf + p4 * 512 + c0: c.o_Pbuf + p4 * 512 + 512],
                        start=(kt == 0), stop=(kt == nkt - 1),
                        tile_position=(0, hrow)), w=w_z)
                    return eZ[kt]

                for kt in range(nkt):
                    sb_slot, s2_slot = SBK[kt % 3]
                    p4 = kt % 4
                    c0 = max(0, kt * 128 - gstart)
                    w_mm = [e_rope, e_vtok]
                    if kt >= 3:
                        w_mm.append(eP[kt - 3])
                    e_s = P.t(lambda e, koff=koff, qoff=qoff, kt=kt, oc=oc, c0=c0, sb_slot=sb_slot:
                              e.matmul(bank(c, sb_slot)[:, c0:512],
                                       bfA[0:64, koff + kt * 128: koff + kt * 128 + 128],
                                       bfA[0:64, qoff + oc * 512 + c0: qoff + oc * 512 + 512],
                                       start=True, stop=True, tile_position=(0, 0)), w=w_mm)
                    e_s2 = P.t(lambda e, koff=koff, qoff=qoff, kt=kt, oc=oc, c0=c0, s2_slot=s2_slot:
                               e.matmul(bank(c, s2_slot)[:, c0:512],
                                        bfA[64:128, koff + kt * 128: koff + kt * 128 + 128],
                                        bfA[64:128, qoff + oc * 512 + c0: qoff + oc * 512 + 512],
                                        start=True, stop=True, tile_position=(64, 0)))
                    w_e = [e_s2]
                    if kt >= 4:
                        w_e.append(eP[kt - 4])
                    e_ev2 = P.a(lambda e, p4=p4, c0=c0, s2_slot=s2_slot: e.copy(
                        f32A[:, c.o_s2sb + p4 * 512 + c0: c.o_s2sb + p4 * 512 + 512],
                        bank(c, s2_slot)[:, c0:512]), w=w_e)
                    w_p = [e_ev2, e_s]
                    if kt >= 4:
                        w_p.append(eZ[kt - 4])
                    e_p = P.v(lambda e, p4=p4, c0=c0, sb_slot=sb_slot: e.tensor_tensor(
                        bfA[:, c.o_Pbuf + p4 * 512 + c0: c.o_Pbuf + p4 * 512 + 512],
                        bank(c, sb_slot)[:, c0:512],
                        f32A[:, c.o_s2sb + p4 * 512 + c0: c.o_s2sb + p4 * 512 + 512],
                        ALU.mult), w=w_p)
                    if kt * 128 >= gstart:
                        e_p = P.v(lambda e, p4=p4, c0=c0: e.tensor_tensor(
                            bfA[:, c.o_Pbuf + p4 * 512 + c0: c.o_Pbuf + p4 * 512 + c0 + 128],
                            bfA[:, c.o_Pbuf + p4 * 512 + c0: c.o_Pbuf + p4 * 512 + c0 + 128],
                            c.trimask[:], ALU.mult), w=[e_tri, e_p])
                    eP[kt] = e_p
                    if hp >= 2:
                        p2_tick["n"] += 1
                        if p2_tick["n"] % 13 == 0:
                            emit_p2_unit()
                    if kt >= AVD:
                        ez = emit_av(kt - AVD)
                for kt in range(max(0, nkt - AVD), nkt):
                    ez = emit_av(kt)
                zp = oc % 2
                w_zv = [ez]
                if zsp_ev.get(zp) is not None:
                    w_zv.append(zsp_ev[zp])
                e_zev = P.a(lambda e, hrow=hrow, zp=zp: e.copy(
                    bfA[hrow:hrow + 64, c.o_zbuf + zp * 512: c.o_zbuf + zp * 512 + 512],
                    bank(c, 4)[hrow:hrow + 64, 0:512]), w=w_zv)
                zev_prev[hrow] = e_zev
                e_sp = P.dma(lambda e, hp=hp, hrow=hrow, oc=oc, zp=zp: e.dma_start(
                    out=scr["z"][hp][hrow:hrow + 64, oc * 512:(oc + 1) * 512],
                    in_=bfA[hrow:hrow + 64, c.o_zbuf + zp * 512: c.o_zbuf + zp * 512 + 512]),
                    w=[e_zev], sem="dz%d" % zp)
                zsp_ev[zp] = e_sp
                e_pair_sp.append(e_sp)
        e_prev_pair = zev_prev[64]
        zspill.append((e_pair_sp[-2], e_pair_sp[-1]))
        if DEBUG and hp == 0:
            e_dbg = P.dma(lambda e: e.dma_start(out=io["dbg_p0"],
                                                in_=bfA[:, c.o_KmA:c.o_KmA + 16384]),
                          w=[e_rope, e_vtok], sem="dbg")
            e_prev_pair = None
            w_dbg_guard.append(e_dbg)

        if hp == 1:
            emit_sw_totals()
    while _p2["u"] < len(p2_units):
        emit_p2_unit()

    e_zf = []
    for hp in range(NPAIR):
        e_zf.append(P.dma(lambda e, hp=hp: e.dma_start(
            out=bfA[:, c.o_zfull + hp * 2048: c.o_zfull + hp * 2048 + 2048], in_=scr["z"][hp]),
            w=list(zspill[hp]) + [e_wt_free], sem="dzf"))
    e_wp = []
    for hp in range(NPAIR):
        e_wp.append(P.dmag(lambda e, hp=hp: e.dma_start(
            out=bfA[:, c.o_WprojT + hp * 1024: c.o_WprojT + hp * 1024 + 1024],
            in_=wT["wprojT"][hp * 128:(hp + 1) * 128, :]), w=[e_prev_pair], sem="dwp"))

    c1_evs = []
    x1_spills = []
    prev_x1rd = {}
    prev_yadd = {}
    for tt in range(NT_OWN):
        par = tt % 2
        x1t = c.x1stg[:, par * 1024:(par + 1) * 1024]
        grow = OC[tt // 4] * 512 + (tt % 4) * 128
        w_x = list(prev_x1rd.get(par, ()))
        e_x = P.dma(lambda e, grow=grow, par=par: e.dma_start(
            out=f32A[:, c.o_xstg + par * 1024: c.o_xstg + par * 1024 + 1024],
            in_=x[grow:grow + 128, :]), w=w_x, sem="dy%d" % par)
        ev_oc = []
        for oc2 in range(2):
            yslot = 2 + oc2
            e_mm = None
            for hp in range(NPAIR):
                w_mm = []
                if hp == 0:
                    w_mm = list(e_zf) + list(e_wp)
                    if oc2 in prev_yadd:
                        w_mm.append(prev_yadd[oc2])
                e_mm = P.t(lambda e, hp=hp, tt=tt, oc2=oc2, yslot=yslot: e.matmul(
                    bank(c, yslot),
                    bfA[:, c.o_zfull + hp * 2048 + tt * 128: c.o_zfull + hp * 2048 + tt * 128 + 128],
                    bfA[:, c.o_WprojT + hp * 1024 + oc2 * 512: c.o_WprojT + hp * 1024 + oc2 * 512 + 512],
                    start=(hp == 0), stop=(hp == 7)), w=w_mm)
            e_add = P.v(lambda e, oc2=oc2, yslot=yslot, x1t=x1t, par=par: e.tensor_add(
                x1t[:, oc2 * 512:(oc2 + 1) * 512], bank(c, yslot),
                f32A[:, c.o_xstg + par * 1024 + oc2 * 512: c.o_xstg + par * 1024 + oc2 * 512 + 512]),
                w=[e_mm, e_x] + w_x)
            prev_yadd[oc2] = e_add
            ev_oc.append(e_add)
        e_sp = P.dma(lambda e, tt=tt, x1t=x1t: e.dma_start(out=scr["x1"][tt], in_=x1t), w=ev_oc,
                     sem="dsp%d" % (tt % 2))
        x1_spills.append(e_sp)

        cb = 30 if par == 0 else 54
        d2 = f32A[:, c.o_d2 + par * 1024: c.o_d2 + par * 1024 + 1024]
        w_cols = [c1_evs[tt - 2]] if tt >= 2 else [e_C]
        P.v(lambda e, x1t=x1t, cb=cb: e.reduce_sum(c.cols[:, cb:cb + 1], x1t, AX.X),
            w=ev_oc + w_cols, drain=True)
        e_nm = P.v(lambda e, cb=cb: e.tensor_scalar(c.cols[:, cb + 1:cb + 2], c.cols[:, cb:cb + 1],
                                                    -1.0 / D, None, ALU.mult), drain=True)
        e_d2 = P.a(lambda e, x1t=x1t, d2=d2, cb=cb: e.activation(
            d2, x1t, AF.Square, bias=c.cols[:, cb + 1:cb + 2], scale=1.0), w=[e_nm], drain=True)
        P.v(lambda e, d2=d2, cb=cb: e.reduce_sum(c.cols[:, cb + 2:cb + 3], d2, AX.X), w=[e_d2],
            drain=True)
        e_vp = P.v(lambda e, cb=cb: e.tensor_scalar(c.cols[:, cb + 2:cb + 3], c.cols[:, cb + 2:cb + 3],
                                                    1.0 / D, LN_EPS, ALU.mult, ALU.add), drain=True)
        e_sq = P.a(lambda e, cb=cb: e.activation(c.cols[:, cb + 3:cb + 4], c.cols[:, cb + 2:cb + 3],
                                                 AF.Sqrt, bias=0.0, scale=1.0), w=[e_vp], drain=True)
        P.v(lambda e, cb=cb: e.reciprocal(c.cols[:, cb + 4:cb + 5], c.cols[:, cb + 3:cb + 4]),
            w=[e_sq], drain=True)
        for _ in range(2):
            P.v(lambda e, cb=cb: e.tensor_mul(c.cols[:, cb + 5:cb + 6], c.cols[:, cb + 4:cb + 5],
                                              c.cols[:, cb + 4:cb + 5]), drain=True)
            P.v(lambda e, cb=cb: e.tensor_mul(c.cols[:, cb + 5:cb + 6], c.cols[:, cb + 5:cb + 6],
                                              c.cols[:, cb + 2:cb + 3]), drain=True)
            P.v(lambda e, cb=cb: e.tensor_scalar(c.cols[:, cb + 5:cb + 6], c.cols[:, cb + 5:cb + 6],
                                                 -0.5, 1.5, ALU.mult, ALU.add), drain=True)
            P.v(lambda e, cb=cb: e.tensor_mul(c.cols[:, cb + 4:cb + 5], c.cols[:, cb + 4:cb + 5],
                                              c.cols[:, cb + 5:cb + 6]), drain=True)
        e_rstd = P.barrier("vector")
        e_ab = P.a(lambda e, x1t=x1t, d2=d2, cb=cb: e.activation(
            d2, x1t, AF.Abs, bias=c.cols[:, cb + 1:cb + 2], scale=1.0), w=[e_rstd], drain=True)
        P.v(lambda e, d2=d2, cb=cb: e.reduce_max(c.cols[:, cb + 6:cb + 7], d2, AX.X), w=[e_ab],
            drain=True)
        P.v(lambda e, cb=cb: e.tensor_mul(c.cols[:, cb + 6:cb + 7], c.cols[:, cb + 6:cb + 7],
                                          c.cols[:, cb + 4:cb + 5]), drain=True)
        P.v(lambda e, cb=cb: e.tensor_scalar_max(c.cols[:, cb + 6:cb + 7], c.cols[:, cb + 6:cb + 7],
                                                 Q_EPS), drain=True)
        P.v(lambda e, cb=cb: e.reciprocal(c.cols[:, cb + 7:cb + 8], c.cols[:, cb + 6:cb + 7]),
            drain=True)
        P.v(lambda e, cb=cb: e.tensor_scalar(c.cols[:, cb + 7:cb + 8], c.cols[:, cb + 7:cb + 8],
                                             127.0, None, ALU.mult), drain=True)
        P.v(lambda e, cb=cb: e.tensor_mul(c.cols[:, cb + 8:cb + 9], c.cols[:, cb + 7:cb + 8],
                                          c.cols[:, cb + 4:cb + 5]), drain=True)
        e_nq = P.v(lambda e, cb=cb: e.tensor_mul(c.cols[:, cb + 9:cb + 10], c.cols[:, cb + 1:cb + 2],
                                                 c.cols[:, cb + 8:cb + 9]), drain=True)
        P.v(lambda e, cb=cb: e.tensor_mul(c.cols[:, cb + 10:cb + 11], c.cols[:, cb + 7:cb + 8],
                                          c.swg[:]), w=[sw_events["wgT"]], drain=True)
        P.v(lambda e, tt=tt, cb=cb: e.reciprocal(c.r1g[:, tt:tt + 1], c.cols[:, cb + 10:cb + 11]),
            drain=True)
        P.v(lambda e, cb=cb: e.tensor_mul(c.cols[:, cb + 10:cb + 11], c.cols[:, cb + 7:cb + 8],
                                          c.swu[:]), w=[sw_events["wuT"]], drain=True)
        P.v(lambda e, tt=tt, cb=cb: e.reciprocal(c.r1u[:, tt:tt + 1], c.cols[:, cb + 10:cb + 11]),
            drain=True)
        e_dq = P.a(lambda e, x1t=x1t, d2=d2, cb=cb: e.activation(
            d2, x1t, AF.Identity, bias=c.cols[:, cb + 9:cb + 10], scale=c.cols[:, cb + 8:cb + 9]),
            w=[e_nq], drain=True)
        prev_x1rd[par] = (e_dq, e_sp)
        e_r1a = P.v(lambda e, d2=d2: e.tensor_scalar(d2, d2, c.Ccol[:], None, ALU.add), w=[e_dq])
        e_rnd = P.v(lambda e, d2=d2: e.tensor_scalar(d2, d2, c.nCcol[:], None, ALU.add), w=[e_r1a])
        tr = []
        e_tr = None
        for fc in range(8):
            tslot = fc % 2
            w_tr = [e_rnd, e_id]
            if fc >= 2:
                w_tr.append(tr[fc - 2])
            elif tt >= 1:
                w_tr.append(c1_evs[tt - 1])
            e_tr = P.t(lambda e, fc=fc, tslot=tslot, d2=d2: e.transpose(
                bank(c, tslot)[:, 0:128], d2[:, fc * 128:(fc + 1) * 128], c.ident[:]), w=w_tr)
            tr.append(P.a(lambda e, fc=fc, tslot=tslot, tt=tt: e.copy(
                bfA[:, c.o_c1T + fc * 2048 + tt * 128: c.o_c1T + fc * 2048 + tt * 128 + 128],
                bank(c, tslot)[:, 0:128]), w=[e_tr]))
        c1_evs.append(tr[-1])
    e_c1T = c1_evs[-1]
    if DEBUG:
        P.dma(lambda e: e.dma_start(out=io["dbg_c1T"], in_=bfA[:, c.o_c1T:c.o_c1T + 16384]),
              w=[e_c1T], sem="dbg")


    out_dmas = []
    prev_icc = {}
    prev_dstream = {}
    prev_eo = {}
    prev_gate_rd = {}
    prev_up_rd = {}
    e_prev_downevac = None
    for ss in range(NT_OWN // SS):
        tts = list(range(ss * SS, (ss + 1) * SS))
        gu_evs = {tt: [] for tt in tts}
        prev_gu_gate = {}
        prev_up = None
        for icc in range(8):
            par = icc % 2
            iccg = bfA[:, c.o_icc + par * 4096: c.o_icc + par * 4096 + 4096]
            iccu = bfA[:, c.o_iccu + par * 4096: c.o_iccu + par * 4096 + 4096]
            lg = []
            for fc in range(8):
                lg.append(P.dma(lambda e, fc=fc, icc=icc, iccg=iccg: e.dma_start(
                    out=iccg[:, fc * 512:(fc + 1) * 512],
                    in_=scr["cg"][fc][:, icc * 512:(icc + 1) * 512]),
                    w=([prev_icc.get(("g", par), e_c1T)] if fc == 0 else []) + [p2_ev["cg"][fc]],
                    sem="di%d" % par))
            lu = []
            for fc in range(8):
                lu.append(P.dma(lambda e, fc=fc, icc=icc, iccu=iccu: e.dma_start(
                    out=iccu[:, fc * 512:(fc + 1) * 512],
                    in_=scr["cu"][fc][:, icc * 512:(icc + 1) * 512]),
                    w=([prev_icc.get(("u", par), e_c1T)] if fc == 0 else []) + [p2_ev["cu"][fc]],
                    sem="du%d" % par))
            ev_gate = {}
            e_gmm = None
            for i, tt in enumerate(tts):
                gslot = i % 2
                e_mm = None
                for fc in range(8):
                    w_mm = []
                    if fc == 0:
                        w_mm = lg + [e_c1T]
                        if e_prev_downevac is not None:
                            w_mm.append(e_prev_downevac)
                        if prev_gate_rd.get(gslot) is not None:
                            w_mm.append(prev_gate_rd[gslot])
                    e_mm = P.t(lambda e, fc=fc, tt=tt, gslot=gslot, iccg=iccg: e.matmul(
                        bank(c, gslot),
                        bfA[:, c.o_c1T + fc * 2048 + tt * 128: c.o_c1T + fc * 2048 + tt * 128 + 128],
                        iccg[:, fc * 512:(fc + 1) * 512], start=(fc == 0), stop=(fc == 7)), w=w_mm)
                w_g = [e_mm]
                if tt in prev_gu_gate:
                    w_g.append(prev_gu_gate[tt])
                ev_gate[tt] = P.a(lambda e, tt=tt, i=i, gslot=gslot: e.activation(
                    f32A[:, c.o_gates + i * 512: c.o_gates + i * 512 + 512], bank(c, gslot),
                    AF.Silu, bias=0.0, scale=c.r1g[:, tt:tt + 1]), w=w_g)
                prev_gate_rd[gslot] = ev_gate[tt]
                e_gmm = e_mm
            prev_icc[("g", par)] = e_gmm
            e_umm = None
            for i, tt in enumerate(tts):
                uslot = 2 + i % 2
                e_mm = None
                for fc in range(8):
                    w_mm = []
                    if fc == 0:
                        w_mm = lu
                        if e_prev_downevac is not None:
                            w_mm.append(e_prev_downevac)
                        if prev_up_rd.get(uslot) is not None:
                            w_mm.append(prev_up_rd[uslot])
                    e_mm = P.t(lambda e, fc=fc, tt=tt, uslot=uslot, iccu=iccu: e.matmul(
                        bank(c, uslot),
                        bfA[:, c.o_c1T + fc * 2048 + tt * 128: c.o_c1T + fc * 2048 + tt * 128 + 128],
                        iccu[:, fc * 512:(fc + 1) * 512], start=(fc == 0), stop=(fc == 7)), w=w_mm)
                w_u = [e_mm]
                if prev_up is not None:
                    w_u.append(prev_up)
                e_u = P.a(lambda e, tt=tt, uslot=uslot: e.activation(
                    f32A[:, c.o_ups: c.o_ups + 512], bank(c, uslot),
                    AF.Copy, bias=0.0, scale=c.r1u[:, tt:tt + 1]), w=w_u)
                prev_up_rd[uslot] = e_u
                w_gu = [e_u, ev_gate[tt]]
                if ss == 0 and icc == 0 and i == 0:
                    w_gu += [_p2["prevst"][0], _p2["prevst"][1]]
                e_gu = P.v(lambda e, tt=tt, icc=icc, i=i: e.tensor_tensor(
                    f32A[:, c.o_gu + i * 4096 + icc * 512: c.o_gu + i * 4096 + icc * 512 + 512],
                    f32A[:, c.o_gates + i * 512: c.o_gates + i * 512 + 512],
                    f32A[:, c.o_ups: c.o_ups + 512], ALU.mult), w=w_gu)
                prev_up = e_gu
                prev_gu_gate[tt] = e_gu
                gu_evs[tt].append(e_gu)
                e_umm = e_mm
            prev_icc[("u", par)] = e_umm
        c2_evs = {}
        for i, tt in enumerate(tts):
            gut = f32A[:, c.o_gu + i * 4096: c.o_gu + i * 4096 + 4096]
            w_dbgu = []
            if DEBUG and tt == 0:
                w_dbgu.append(P.dma(lambda e, gut=gut: e.dma_start(out=io["dbg_gu"], in_=gut),
                                    w=gu_evs[tt], sem="dbg"))
            P.v(lambda e, gut=gut: e.reduce_max(c.cols[:, 44:45], gut, AX.X,
                                                apply_absolute_value=True),
                w=gu_evs[tt] + w_dbgu, drain=True)
            P.v(lambda e: e.tensor_scalar_max(c.cols[:, 44:45], c.cols[:, 44:45], Q_EPS), drain=True)
            P.v(lambda e: e.reciprocal(c.cols[:, 45:46], c.cols[:, 44:45]), drain=True)
            e_s2 = P.v(lambda e: e.tensor_scalar(c.cols[:, 45:46], c.cols[:, 45:46], 127.0, None,
                                                 ALU.mult), drain=True)
            P.v(lambda e: e.tensor_mul(c.cols[:, 46:47], c.cols[:, 45:46], c.swd[:]),
                w=[sw_events["wdT"]], drain=True)
            e_rd = P.v(lambda e, tt=tt: e.reciprocal(c.rdc[:, tt:tt + 1], c.cols[:, 46:47]),
                       drain=True)
            e_cs = P.v(lambda e, tt=tt: e.tensor_scalar(c.c2s[:, tt:tt + 1], c.cols[:, 45:46],
                                                        1.0, None, ALU.mult), w=[e_rd], drain=True)
            e_a1 = P.a(lambda e, gut=gut, tt=tt: e.activation(
                gut, gut, AF.Identity, bias=c.Ccol[:], scale=c.c2s[:, tt:tt + 1]), w=[e_cs])
            e_rnd = P.a(lambda e, gut=gut: e.activation(
                gut, gut, AF.Identity, bias=c.nCcol[:], scale=1.0), w=[e_a1])
            tr = []
            for ic in range(32):
                tslot = 4 + ic % 2
                w_tr = [e_rnd, e_id]
                if ic >= 2:
                    w_tr.append(tr[ic - 2])
                elif i >= 1:
                    w_tr.append(c2_evs[tts[i - 1]])
                elif e_prev_downevac is not None:
                    w_tr.append(e_prev_downevac)
                e_tr = P.t(lambda e, ic=ic, tslot=tslot, gut=gut: e.transpose(
                    bank(c, tslot)[:, 0:128], gut[:, ic * 128:(ic + 1) * 128], c.ident[:]), w=w_tr)
                tr.append(P.a(lambda e, ic=ic, i=i, tslot=tslot: e.copy(
                    bfA[:, c.o_c2T + i * 4096 + ic * 128: c.o_c2T + i * 4096 + ic * 128 + 128],
                    bank(c, tslot)[:, 0:128]), w=[e_tr]))
            c2_evs[tt] = tr[-1]
        down_mm = {}
        for ic in range(32):
            par = ic % 4
            cdb = bfA[:, c.o_cdblk + par * 1024: c.o_cdblk + par * 1024 + 1024]
            w_l = [prev_dstream.get(par), p2_ev["cd"][ic]]
            e_cd = P.dma(lambda e, ic=ic, cdb=cdb: e.dma_start(out=cdb, in_=scr["cd"][ic]), w=w_l,
                         sem="dd%d" % par)
            for foc in range(2):
                for i, tt in enumerate(tts):
                    dslot = i * 2 + foc
                    w_mm = [e_cd]
                    if ic == 0:
                        w_mm += [c2_evs[tts[-1]]]
                    down_mm[(ic, foc, i)] = P.t(lambda e, ic=ic, foc=foc, i=i, dslot=dslot,
                                                cdb=cdb: e.matmul(
                        bank(c, dslot),
                        bfA[:, c.o_c2T + i * 4096 + ic * 128: c.o_c2T + i * 4096 + ic * 128 + 128],
                        cdb[:, foc * 512:(foc + 1) * 512], start=(ic == 0), stop=(ic == 31)),
                        w=w_mm)
            prev_dstream[par] = down_mm[(ic, 1, SS - 1)]
        e_last_eo = None
        for i, tt in enumerate(tts):
            x1r = f32A[:, c.o_x1rb: c.o_x1rb + 1024]
            w_xr = [x1_spills[-1], x1_spills[-2]]
            if out_dmas:
                w_xr.append(out_dmas[-1])
            e_xr = P.dma(lambda e, tt=tt, x1r=x1r: e.dma_start(out=x1r, in_=scr["x1"][tt]), w=w_xr,
                         sem="dr")
            for foc in range(2):
                dslot = i * 2 + foc
                w_xo = [down_mm[(31, foc, i)], x1_spills[-1], x1_spills[-2]]
                if (foc,) in prev_eo:
                    w_xo.append(prev_eo[(foc,)])
                e_xo = P.a(lambda e, dslot=dslot, tt=tt, foc=foc: e.activation(
                    c.x1stg[:, foc * 512: foc * 512 + 512], bank(c, dslot),
                    AF.Copy, bias=0.0, scale=c.rdc[:, tt:tt + 1]), w=w_xo)
                w_o = [e_xo, e_xr]
                if len(out_dmas) >= 2:
                    w_o.append(out_dmas[-2])
                e_o = P.v(lambda e, foc=foc: e.tensor_add(
                    c.x1stg[:, 1024 + foc * 512: 1024 + foc * 512 + 512],
                    c.x1stg[:, foc * 512: foc * 512 + 512],
                    f32A[:, c.o_x1rb + foc * 512: c.o_x1rb + foc * 512 + 512]), w=w_o)
                prev_eo[(foc,)] = e_o
                out_dmas.append(P.dma(lambda e, tt=tt, foc=foc: e.dma_start(
                    out=out[tt * 128:(tt + 1) * 128, foc * 512:(foc + 1) * 512],
                    in_=c.x1stg[:, 1024 + foc * 512: 1024 + foc * 512 + 512]), w=[e_o],
                    sem="do%d" % foc))
                e_last_eo = e_o
        e_prev_downevac = e_last_eo

    P.items["sync"].append((None, tuple((sname, tot) for sname, tot in P.cnt.items()), None, 0, False))
    return P


def _fix_dma_waits(P):
    for ring_sem, ring_eng in (("ds", "sync"), ("dg", "gpsimd")):
        bounds = []
        tot = 0
        for fn, waits, sem, amt, drain in P.items[ring_eng]:
            if waits:
                bounds.append(tot)
            if sem == ring_sem:
                tot += amt
        bounds.append(tot)
        import bisect
        def remap(v, bounds=bounds):
            i = bisect.bisect_left(bounds, v)
            return bounds[i] if i < len(bounds) else v
        for eng in ENGINES:
            new = []
            for fn, waits, sem, amt, drain in P.items[eng]:
                nw = tuple((s2, remap(v) if s2 == ring_sem else v) for (s2, v) in waits)
                new.append((fn, nw, sem, amt, drain))
            P.items[eng] = new


def build():
    nc = bass.Bass("TRN2", target_bir_lowering=False)
    io = {}
    io["x"] = nc.dram_tensor("x", [T, D], F32, kind="ExternalInput").ap()
    wT = {}
    for wn, shp in (("wqT", [D, D]), ("wkT", [D, D]), ("wq2T", [D, D]), ("wk2T", [D, D]),
                    ("wvT", [D, D]), ("wprojT", [D, D]), ("wgT", [D, INNER]), ("wuT", [D, INNER]),
                    ("wdT", [INNER, D])):
        wT[wn] = nc.dram_tensor(wn, shp, F32, kind="ExternalInput").ap()
    io["wT"] = wT
    io["consts"] = nc.dram_tensor("consts", [128, 4], F32, kind="ExternalInput").ap()
    io["onesrow"] = nc.dram_tensor("onesrow", [1, 128], F32, kind="ExternalInput").ap()
    io["trimask"] = nc.dram_tensor("trimask", [128, 128], BF16, kind="ExternalInput").ap()
    io["pswap"] = nc.dram_tensor("pswap", [128, 128], BF16, kind="ExternalInput").ap()
    io["ident"] = nc.dram_tensor("ident", [128, 128], F32, kind="ExternalInput").ap()
    io["flagin"] = nc.dram_tensor("flag", [1, 1], I32, kind="ExternalInput").ap()
    io["ropetab"] = nc.dram_tensor("ropetab", [128, 8192], F32, kind="ExternalInput").ap()
    io["out"] = nc.dram_tensor("out", [N_OWN, D], F32, kind="ExternalOutput").ap()
    kind = "ExternalOutput"
    scr = {}
    scr["z"] = [nc.dram_tensor(f"scz{i}", [128, 2048], BF16, kind=kind).ap() for i in range(NPAIR)]
    scr["cg"] = [nc.dram_tensor(f"scg{i}", [128, INNER], BF16, kind=kind).ap() for i in range(8)]
    scr["cu"] = [nc.dram_tensor(f"scu{i}", [128, INNER], BF16, kind=kind).ap() for i in range(8)]
    scr["cd"] = [nc.dram_tensor(f"scd{i}", [128, D], BF16, kind=kind).ap() for i in range(32)]
    scr["x1"] = [nc.dram_tensor(f"sx1{i}", [128, D], F32, kind=kind).ap() for i in range(NT_OWN)]
    io["scr"] = scr
    if DEBUG:
        io["dbg_tab"] = nc.dram_tensor("dbg_tab", [128, 8192], F32, kind="ExternalOutput").ap()
        io["dbg_hnT"] = nc.dram_tensor("dbg_hnT", [128, 32768], BF16, kind="ExternalOutput").ap()
        io["dbg_p0"] = nc.dram_tensor("dbg_p0", [128, 16384], BF16, kind="ExternalOutput").ap()
        io["dbg_c1T"] = nc.dram_tensor("dbg_c1T", [128, 16384], BF16, kind="ExternalOutput").ap()
        io["dbg_gu"] = nc.dram_tensor("dbg_gu", [128, 4096], F32, kind="ExternalOutput").ap()
        io["dbg_cols"] = nc.dram_tensor("dbg_cols", [128, 41], F32, kind="ExternalOutput").ap()

    with ExitStack() as stack:
        c = alloc_tensors(nc, stack)
        progs = [build_half(nc, c, Prog(), h, io) for h in (0, 1)]
        sem_names = sorted(set(list(progs[0].cnt) + list(progs[1].cnt))) + ["fl"]
        sems = {s: stack.enter_context(nc.semaphore(s)) for s in sem_names}
        block = stack.enter_context(nc.Block())

        def replay(eng, items):
            waited = {}
            self_cnt = {}
            for fn, waits, sem, amt, drain in items:
                need = {}
                for (s, v) in waits:
                    if v > 0 and v > waited.get(s, 0):
                        need[s] = max(need.get(s, 0), v)
                for s, v in need.items():
                    eng.wait_ge(sems[s], v)
                    waited[s] = v
                if fn is None:
                    continue
                inst = fn(eng)
                if sem is not None:
                    inst.then_inc(sems[sem], amt)
                if drain and sem is not None:
                    eng.wait_ge(sems[sem], self_cnt.get(sem, 0) + amt)
                if sem is not None:
                    self_cnt[sem] = self_cnt.get(sem, 0) + amt
                    waited.setdefault(sem, 0)
                    if drain:
                        waited[sem] = max(waited[sem], self_cnt[sem])

        def section(eng_name):
            def body(eng):
                if eng_name == "sync":
                    eng.dma_start(out=c.flag[:], in_=io["flagin"]).then_inc(sems["fl"], 16)
                eng.wait_ge(sems["fl"], 16)
                with eng.register("rf_" + eng_name) as rf:
                    eng.load(rf, c.flag[0:1, 0:1])
                    with eng.If_eq(rf, 0):
                        replay(eng, progs[0].items[eng_name])
                    with eng.If_eq(rf, 1):
                        replay(eng, progs[1].items[eng_name])
            return body

        block.sync(section("sync"))
        block.tensor(section("tensor"))
        block.vector(section("vector"))
        block.scalar(section("scalar"))
        block.gpsimd(section("gpsimd"))
    return nc


def _rope_tables_np():
    try:
        import jax
        import jax.numpy as jnp
        cpu = jax.devices("cpu")[0]
        with jax.default_device(cpu):
            inv_freq = 1.0 / 10000.0 ** (jnp.arange(0, 64, 2, dtype=jnp.float32) / 64)
            t = jnp.arange(T, dtype=jnp.float32)
            freqs = jnp.outer(t, inv_freq)
            cos = np.asarray(jnp.cos(freqs), dtype=np.float32)
            sin = np.asarray(jnp.sin(freqs), dtype=np.float32)
    except Exception:
        e = np.arange(0, 64, 2, dtype=np.float32) / np.float32(64)
        inv = (1.0 / (10000.0 ** e.astype(np.float64))).astype(np.float32)
        freqs = np.outer(np.arange(T, dtype=np.float32), inv).astype(np.float32)
        cos, sin = np.cos(freqs).astype(np.float32), np.sin(freqs).astype(np.float32)
    r = np.arange(128)
    sgn = np.where(r % 64 < 32, 1.0, -1.0).astype(np.float32)
    cosT = cos.T[r % 32]
    sinS = sin.T[r % 32] * sgn[:, None]
    return np.ascontiguousarray(np.concatenate([cosT, sinS], axis=1))


def _consts_np():
    r = np.arange(128)
    consts = np.stack([np.ones(128), np.full(128, 2.0 ** -30), (r % 32).astype(float),
                       np.where(r % 64 < 32, 1.0, -1.0)], axis=1).astype(np.float32)
    onesrow = np.ones((1, 128), np.float32)
    tri = np.triu(np.ones((128, 128))).astype(ml_dtypes.bfloat16)
    psw = np.zeros((128, 128), ml_dtypes.bfloat16)
    for j in range(128):
        psw[j ^ 32, j] = 1.0
    ident = np.eye(128, dtype=np.float32)
    return consts, onesrow, tri, psw, ident


def kernel(L0_attn_x, L0_attn_Wq, L0_attn_Wk, L0_attn_Wq2, L0_attn_Wk2, L0_attn_Wv,
           L0_attn_Wproj, L0_ffn_w_gate, L0_ffn_w_up, L0_ffn_w_down):
    x = np.ascontiguousarray(np.asarray(L0_attn_x, dtype=np.float32))
    wTs = {
        "wqT": L0_attn_Wq, "wkT": L0_attn_Wk, "wq2T": L0_attn_Wq2, "wk2T": L0_attn_Wk2,
        "wvT": L0_attn_Wv, "wprojT": L0_attn_Wproj, "wgT": L0_ffn_w_gate,
        "wuT": L0_ffn_w_up, "wdT": L0_ffn_w_down,
    }
    wTs = {k: np.ascontiguousarray(np.asarray(v, dtype=np.float32).T) for k, v in wTs.items()}
    consts, onesrow, tri, psw, ident = _consts_np()
    ropetab = _rope_tables_np()

    nc = build()
    in_maps = []
    for core in range(8):
        b, half = core // 2, core % 2
        m = {"x": np.ascontiguousarray(x[b]),
             "consts": consts, "onesrow": onesrow, "trimask": tri, "pswap": psw,
             "ident": ident, "flag": np.full((1, 1), half, np.int32), "ropetab": ropetab}
        m.update(wTs)
        in_maps.append(m)
    res = run_bass_kernel_spmd(nc, in_maps, core_ids=list(range(8)))
    full = np.empty((4, T, D), np.float32)
    for core in range(8):
        b, half = core // 2, core % 2
        o = res.results[core]["out"]
        if half == 0:
            full[b, 0:1024] = o[0:1024]
            full[b, 3072:4096] = o[1024:2048]
        else:
            full[b, 1024:3072] = o
    return full
